# version 02
# baseline (speedup 1.0000x reference)
import contextlib

import numpy as np
import concourse.bass as bass
import concourse.mybir as mybir
from concourse.bass_utils import run_bass_kernel_spmd

F32 = mybir.dt.float32
BF16 = mybir.dt.bfloat16
I32 = mybir.dt.int32
AF = mybir.ActivationFunctionType
ALU = mybir.AluOpType
AX = mybir.AxisListType
AP = bass.AP
RV = bass.RuntimeValue

D = 2048
TL = 2048
TG = 4096
NCH = 16
NTC = 4
NHEAD = 16
SCALE0 = float(1.0 / np.sqrt(128.0))

ENGS = ("t", "v", "s", "g", "y")

SUBBLOCKS = ["l0a", "l0f", "l1a", "l1f"]
LAUNCHES = None
MOE_EXPERTS = 9
DBG_DRAM_OUT = []
MOE_SKIP_DOWN = False
MOE_SKIP_GU = False


class Plan:
    def __init__(self):
        self.q = {e: [] for e in ENGS}

    def t(self, fn): self.q["t"].append(fn)
    def v(self, fn): self.q["v"].append(fn)
    def s(self, fn): self.q["s"].append(fn)
    def g(self, fn): self.q["g"].append(fn)
    def y(self, fn): self.q["y"].append(fn)

    def run(self, nc):
        with nc.Block() as blk:
            if self.q["g"]:
                @blk.gpsimd
                def _(eng):
                    for fn in self.q["g"]:
                        fn(eng)
            if self.q["t"]:
                @blk.tensor
                def _(eng):
                    for fn in self.q["t"]:
                        fn(eng)
            if self.q["v"]:
                @blk.vector
                def _(eng):
                    for fn in self.q["v"]:
                        fn(eng)
            if self.q["s"]:
                @blk.scalar
                def _(eng):
                    for fn in self.q["s"]:
                        fn(eng)
            if self.q["y"]:
                @blk.sync
                def _(eng):
                    for fn in self.q["y"]:
                        fn(eng)


class Sem:
    def __init__(self, h):
        self.h = h
        self.n = 0

    def inc(self, k=1):
        self.n += k
        return self.n


def dma(pl, eng, out_ap, in_ap, sem, **kw):
    tgt = sem.inc(16)
    h = sem.h
    getattr(pl, eng)(lambda e, o=out_ap, i=in_ap, k=dict(kw):
                     e.dma_start(out=o, in_=i, **k).then_inc(h, 16))
    return tgt


def wait(pl, eng, sem, tgt):
    h = sem.h
    getattr(pl, eng)(lambda e, ht=h, tg=tgt: e.wait_ge(ht, tg))


def tinc(pl, eng, sem, k=1):
    tgt = sem.inc(k)
    h = sem.h
    getattr(pl, eng)(lambda e, ht=h, kk=k: e.nop().then_inc(ht, kk))
    return tgt


def minc(pl, eng, sem, fn):
    tgt = sem.inc(1)
    h = sem.h
    getattr(pl, eng)(lambda e, f=fn, hh=h: f(e).then_inc(hh, 1))
    return tgt



def _rope_tables_L0():
    half = 64
    inv = 10000.0 ** (-np.arange(half, dtype=np.float64) / half)
    ang = np.arange(TG, dtype=np.float64)[None, :] * inv[:, None]
    return (np.repeat(np.cos(ang), 2, axis=0).astype(np.float32),
            np.repeat(np.sin(ang), 2, axis=0).astype(np.float32))


def _rope_tables_L1():
    half = 64
    inv = 10000.0 ** (-np.arange(half, dtype=np.float64) * 2.0 / 128.0)
    ang = np.arange(TG, dtype=np.float64)[None, :] * inv[:, None]
    cos = np.cos(ang).astype(np.float32)
    sin = np.sin(ang).astype(np.float32)
    return np.concatenate([cos, cos], 0), np.concatenate([sin, sin], 0)


def _perm_L0():
    R = np.zeros((128, 128), np.float32)
    for i in range(64):
        R[2 * i, 2 * i + 1] = -1.0
        R[2 * i + 1, 2 * i] = 1.0
    return np.ascontiguousarray(R.T)


def _perm_L1():
    R = np.zeros((128, 128), np.float32)
    for i in range(64):
        R[i, i + 64] = -1.0
        R[i + 64, i] = 1.0
    return np.ascontiguousarray(R.T)


def _masks(parity):
    k = np.arange(128)[:, None]
    q = np.arange(512)[None, :]
    return np.concatenate(
        [((512 * parity + q) >= (128 * w + k)).astype(np.float32) for w in range(8)],
        axis=1)


def _bias_cols(b):
    b = np.asarray(b, np.float32).reshape(-1, 128)
    return np.ascontiguousarray(b.T)


def prepare_inputs(inputs):
    x = np.asarray(inputs["L0_attn_x"], dtype=np.float32)
    weights = {k: np.ascontiguousarray(np.asarray(v, dtype=np.float32))
               for k, v in inputs.items()
               if k != "L0_attn_x" and not k.endswith(("b_latent", "b_d", "b_proj", "b1", "b2"))}
    cos0, sin0 = _rope_tables_L0()
    cos1, sin1 = _rope_tables_L1()
    consts = {
        "cos0": cos0, "sin0": sin0, "cos1": cos1, "sin1": sin1,
        "ident": np.eye(128, dtype=np.float32),
        "perm0": _perm_L0(), "perm1": _perm_L1(),
        "b_latent_c": _bias_cols(inputs["L0_attn_b_latent"]),
        "b_d_c": _bias_cols(inputs["L0_attn_b_d"]),
        "b_proj_c": _bias_cols(inputs["L0_attn_b_proj"]),
        "b1_c": _bias_cols(inputs["L1_ffn_b1"]),
        "b2_c": _bias_cols(inputs["L1_ffn_b2"]),
        "L1_attn_WqT": np.ascontiguousarray(np.asarray(inputs["L1_attn_Wq"], np.float32).T),
        "L1_attn_WkT": np.ascontiguousarray(np.asarray(inputs["L1_attn_Wk"], np.float32).T),
        "L1_attn_WvT": np.ascontiguousarray(np.asarray(inputs["L1_attn_Wv"], np.float32).T),
        "L1_attn_WoT": np.ascontiguousarray(np.asarray(inputs["L1_attn_Wo"], np.float32).T),
    }
    nonce = np.random.randint(1, 2**31 - 2, size=(1, 32)).astype(np.int32)
    in_maps = []
    for c in range(8):
        pair, parity = divmod(c, 2)
        blocks = [2 * j + parity for j in range(4)]
        tok = np.concatenate([np.arange(512 * g, 512 * g + 512) for g in blocks])
        meta = np.zeros((1, 32), np.int32)
        for j in range(4):
            meta[0, j] = blocks[j] * 512
            meta[0, 4 + j] = blocks[j] * 512 * D
        meta[0, 8] = parity * 32
        meta[0, 9] = (1 - parity) * 32
        m = {
            "xT": np.ascontiguousarray(x[pair][tok].T),
            "cos0o": np.ascontiguousarray(cos0[:, tok]),
            "sin0o": np.ascontiguousarray(sin0[:, tok]),
            "cos1o": np.ascontiguousarray(cos1[:, tok]),
            "sin1o": np.ascontiguousarray(sin1[:, tok]),
            "masks": _masks(parity),
            "meta": meta,
            "nonce": nonce,
        }
        m.update(consts)
        m.update(weights)
        in_maps.append(m)
    return in_maps


def assemble_output(results):
    out = np.empty((4, 4096, 2048), np.float32)
    for c in range(8):
        pair, parity = divmod(c, 2)
        o = results[c]["outT"].T
        for j in range(4):
            g = 2 * j + parity
            out[pair, 512 * g:512 * g + 512, :] = o[512 * j:512 * j + 512, :]
    return out



class Ctx:
    pass


class LazyDT(dict):
    def __init__(self, nc, shapes):
        super().__init__()
        self.nc = nc
        self.shapes = dict(shapes)

    def __missing__(self, k):
        shp = self.shapes[k]
        h = self.nc.dram_tensor(k, shp, F32, kind="ExternalInput")
        self[k] = h
        return h


def declare_dram(ctx):
    nc = ctx.nc
    shapes = [
        ("L0_attn_W_latent", [D, 128]),
        ("L0_attn_W_d", [D, D]),
        ("L0_attn_W_proj", [D, D]),
        ("L0_ffn_gate_w", [D, 8]),
        ("L0_ffn_w_gate", [8, D, 1408]), ("L0_ffn_w_up", [8, D, 1408]),
        ("L0_ffn_w_down", [8, 1408, D]),
        ("L0_ffn_sw_gate", [D, 2816]), ("L0_ffn_sw_up", [D, 2816]),
        ("L0_ffn_sw_down", [2816, D]),
        ("L1_attn_Wq", [D, D]), ("L1_attn_Wk", [D, D]),
        ("L1_attn_Wv", [D, D]), ("L1_attn_Wo", [D, D]),
        ("L1_attn_WqT", [D, D]), ("L1_attn_WkT", [D, D]),
        ("L1_attn_WvT", [D, D]), ("L1_attn_WoT", [D, D]),
        ("L1_ffn_w1", [8192, D]),
        ("L1_ffn_w2", [D, 8192]),
        ("cos0", [128, TG]), ("sin0", [128, TG]),
        ("cos1", [128, TG]), ("sin1", [128, TG]),
        ("cos0o", [128, TL]), ("sin0o", [128, TL]),
        ("cos1o", [128, TL]), ("sin1o", [128, TL]),
        ("ident", [128, 128]), ("perm0", [128, 128]), ("perm1", [128, 128]),
        ("masks", [128, 8 * 512]),
        ("b_latent_c", [128, 1]), ("b_d_c", [128, 16]), ("b_proj_c", [128, 16]),
        ("b1_c", [128, 64]), ("b2_c", [128, 16]),
    ]
    t = LazyDT(nc, shapes)
    t["xT"] = nc.dram_tensor("xT", [D, TL], F32, kind="ExternalInput")
    t["outT"] = nc.dram_tensor("outT", [D, TL], F32, kind="ExternalOutput")
    t["meta"] = nc.dram_tensor("meta", [1, 32], I32, kind="ExternalInput")
    t["nonce"] = nc.dram_tensor("nonce", [1, 32], I32, kind="ExternalInput")
    def internal(name, shape, dtype):
        kind = "ExternalOutput" if name in DBG_DRAM_OUT else "Internal"
        return nc.dram_tensor(name, shape, dtype, kind=kind)
    t["master"] = internal("master", [D, TL], F32)
    t["sub_spill"] = internal("sub_spill", [D, TL], F32)
    t["w1T_scratch"] = internal("w1T_scratch", [D, 8192], BF16)
    t["w2T_scratch"] = internal("w2T_scratch", [8192, D], BF16)
    t["wT_scratch"] = internal("wT_scratch", [4 * D, D], BF16)
    t["comb_dram"] = internal("comb_dram", [8, TL], F32)
    t["h_dram"] = internal("h_dram", [8192, TL], BF16)
    t["kv_sh"] = nc.dram_tensor("kv_sh", [128, TG], BF16, addr_space="Shared")
    t["k_sh"] = nc.dram_tensor("k_sh", [D, TG], BF16, addr_space="Shared")
    t["v_sh"] = nc.dram_tensor("v_sh", [TG, D], BF16, addr_space="Shared")
    t["flags"] = nc.dram_tensor("flags", [2, 32], I32, addr_space="Shared")
    ctx.dt = t


def alloc_persistent(ctx, stack):
    nc = ctx.nc
    sb = {}

    def S(name, shape, dtype):
        sb[name] = stack.enter_context(nc.sbuf_tensor(name, shape, dtype))

    S("xsb", [128, NCH * TL], BF16)
    S("ones_b", [128, 128], BF16)
    S("onescol_f", [1, 128], F32)
    S("ident_b", [128, 128], BF16)
    S("perm0_b", [128, 128], BF16)
    S("perm1_b", [128, 128], BF16)
    S("meta_sb", [1, 32], I32)
    S("nonce_sb", [1, 32], I32)
    S("flagrd", [1, 4], I32)
    S("ln_r", [128, TL], F32)
    S("ln_a", [128, TL], F32)
    S("row_a", [128, 512], F32)
    S("row_b", [128, 512], F32)
    S("row_c", [128, 512], F32)
    S("row_d", [128, 512], F32)
    S("row_e", [128, 512], F32)
    S("row_f", [128, 512], F32)
    ctx.sb = sb
    ctx.sem = {}
    for nm in ["mm", "vv", "ss", "gg",
               "d_mst0", "d_mst1", "d_sub0", "d_sub1", "d_wb0", "d_wb1",
               "d_wt0", "d_wt1", "d_cs", "d_kf", "d_wl", "d_bl", "d_bias",
               "d_sp0", "d_sp1", "d_x", "d_g", "d_we0", "d_we1",
               "d_kv0", "d_kv1", "d_v0", "d_v1", "d_acc"]:
        h = stack.enter_context(nc.semaphore(nm))
        ctx.sem[nm] = Sem(h)
    ctx.posems = [Sem(stack.enter_context(nc.semaphore(f"po{i}"))) for i in range(4)]
    ctx.pqsems = [stack.enter_context(nc.semaphore(f"pq{i}")) for i in range(4)]
    g = nc.gpsimd
    ctx.rg = {nm: g.alloc_register(nm) for nm in
              ["rb0", "rb1", "rb2", "rb3", "rr0", "rr1", "rr2", "rr3",
               "rpf0", "rpf1", "rpf2", "rpf3", "rpo0", "rpo1", "rpo2", "rpo3",
               "rtmp", "rtgt", "rw0", "rw1", "re0", "re1", "rn0", "rn1"]}


def phase_init(ctx):
    pl = Plan()
    sb, dt, S = ctx.sb, ctx.dt, ctx.sem
    dg = S["d_g"]
    for nm_s, nm_d in [("ident_b", "ident"), ("perm0_b", "perm0"),
                       ("perm1_b", "perm1")]:
        dma(pl, "g", sb[nm_s][:, :], dt[nm_d][:, :], dg)
    dma(pl, "g", sb["meta_sb"][:, :], dt["meta"][:, :], dg)
    dma(pl, "g", sb["nonce_sb"][:, :], dt["nonce"][:, :], dg)
    pl.g(lambda e: e.memset(sb["ones_b"][:, :], 1.0))
    pl.g(lambda e: e.memset(sb["onescol_f"][:, :], 1.0))
    wait(pl, "g", dg, dg.n)
    pl.run(ctx.nc)



def plan_scalars_slice(ctx, pl, s1_ap, s2_ap, r_ap, a_ap, n, eps=1e-5):
    sb, S = ctx.sb, ctx.sem
    vv, ss = S["vv"], S["ss"]
    ra, rb, rc, rd = sb["row_a"], sb["row_b"], sb["row_c"], sb["row_d"]
    inv_n = 1.0 / n
    pl.v(lambda e, o=ra[:, :], i=s1_ap: e.tensor_scalar(o, i, inv_n, None, ALU.mult))
    pl.v(lambda e, o=rb[:, :], i=s2_ap: e.tensor_scalar(o, i, inv_n, None, ALU.mult))
    pl.v(lambda e, o=rc[:, :], i=ra[:, :], j=ra[:, :]: e.tensor_mul(o, i, j))
    pl.v(lambda e, o=rc[:, :], i=rb[:, :], j=rc[:, :]: e.tensor_sub(o, i, j))
    vt = minc(pl, "v", vv, lambda e, o=rc[:, :], i=rc[:, :]:
              e.tensor_scalar(o, i, eps, None, ALU.add))
    wait(pl, "s", vv, vt)
    st = minc(pl, "s", ss, lambda e, o=rb[:, :], i=rc[:, :]: e.activation(o, i, AF.Sqrt))
    wait(pl, "v", ss, st)
    pl.v(lambda e, o=rd[:, :], i=rb[:, :]: e.reciprocal(o, i))
    pl.v(lambda e, o=rb[:, :], i=rd[:, :], j=rd[:, :]: e.tensor_mul(o, i, j))
    pl.v(lambda e, o=rb[:, :], i=rb[:, :], j=rc[:, :]: e.tensor_mul(o, i, j))
    pl.v(lambda e, o=rb[:, :], i=rb[:, :]: e.tensor_scalar(o, i, -0.5, 1.5, ALU.mult, ALU.add))
    pl.v(lambda e, o=r_ap, i=rd[:, :], j=rb[:, :]: e.tensor_mul(o, i, j))
    pl.v(lambda e, o=ra[:, :], i=ra[:, :]: e.tensor_scalar(o, i, -1.0, None, ALU.mult))
    return minc(pl, "v", vv, lambda e, o=a_ap, i=ra[:, :], j=r_ap: e.tensor_mul(o, i, j))


def bc_row(ctx, pl, psum_ap, row_ap, sem=None):
    sb = ctx.sb
    if sem is None:
        pl.t(lambda e, o=psum_ap, l=sb["onescol_f"][0:1, :], r=row_ap:
             e.matmul(o, l, r, start=True, stop=True))
        return None
    return minc(pl, "t", sem, lambda e, o=psum_ap, l=sb["onescol_f"][0:1, :], r=row_ap:
                e.matmul(o, l, r, start=True, stop=True))


def plan_xnr(ctx, pl, xnr, xtmp, tc, psB, psA_, width=512, full=False):
    sb, S = ctx.sb, ctx.sem
    tgt = None
    for c in range(NCH):
        xc = sb["xsb"][:, c * TL + tc * width:c * TL + (tc + 1) * width]
        oc = (c * TL + tc * width) if full else (c * width)
        pl.v(lambda e, o=xtmp[:, :], i=xc, j=psB: e.tensor_mul(o, i, j))
        if c < NCH - 1:
            pl.v(lambda e, o=xnr[:, oc:oc + width], i=xtmp[:, :], j=psA_:
                 e.tensor_add(o, i, j))
        else:
            tgt = minc(pl, "v", S["vv"],
                       lambda e, o=xnr[:, oc:oc + width], i=xtmp[:, :], j=psA_:
                       e.tensor_add(o, i, j))
    return tgt



def publish_flag(ctx, pl, xpoint):
    sb, dt, rg = ctx.sb, ctx.dt, ctx.rg
    po = ctx.posems[xpoint]
    wait(pl, "g", po, po.n)
    ft = po.inc(16)

    def f(e, rg=rg, k=xpoint, poh=po.h, ms=sb["meta_sb"]):
        r = rg[f"rpf{k}"]
        e.reg_load(r, ms[0:1, 8:9])
        e.reg_add(r, r, 4 * k)
        e.dma_start(out=AP(dt["flags"], RV(r), [[4, 1], [1, 4]]),
                    in_=sb["nonce_sb"][0:1, 4 * k:4 * k + 4]).then_inc(poh, 16)
    pl.g(f)
    wait(pl, "g", po, ft)


def poll_flag(ctx, pl, xpoint):
    sb, dt, rg = ctx.sb, ctx.dt, ctx.rg
    po = ctx.posems[xpoint]

    pq = ctx.pqsems[xpoint]

    def f(e, rg=rg, k=xpoint, pq=pq, ms=sb["meta_sb"]):
        r = rg[f"rpo{k}"]
        e.reg_load(rg["rn0"], sb["nonce_sb"][0:1, 4 * k:4 * k + 1])
        e.reg_load(rg["rn1"], sb["nonce_sb"][0:1, 4 * k + 1:4 * k + 2])
        e.reg_load(r, ms[0:1, 9:10])
        e.reg_add(r, r, 4 * k)
        e.reg_mov(rg["re0"], 0)
        e.reg_mov(rg["rtgt"], 0)
        with e.While(RV(rg["re0"]) != 2):
            e.dma_start(out=sb["flagrd"][0:1, 0:4],
                        in_=AP(dt["flags"], RV(r), [[4, 1], [1, 4]])).then_inc(pq, 16)
            e.reg_add(rg["rtgt"], rg["rtgt"], 16)
            e.wait_ge(pq, RV(rg["rtgt"]))
            e.reg_load(rg["rw0"], sb["flagrd"][0:1, 0:1])
            e.reg_load(rg["rw1"], sb["flagrd"][0:1, 1:2])
            e.reg_alu(rg["re0"], rg["rw0"], RV(rg["rn0"]), ALU.is_equal)
            e.reg_alu(rg["re1"], rg["rw1"], RV(rg["rn1"]), ALU.is_equal)
            e.reg_alu(rg["re0"], rg["re0"], RV(rg["re1"]), ALU.add)
    pl.g(f)


def gdma_shared(ctx, pl, xpoint, dst_tensor, reg_name, meta_col, ap_dims, src_ap):
    rg, sb = ctx.rg, ctx.sb
    po = ctx.posems[xpoint]
    po.inc(16)

    def f(e, rg=rg, dims=ap_dims, src=src_ap, dstt=dst_tensor,
          col=meta_col, rn=reg_name, poh=po.h, ms=sb["meta_sb"]):
        e.reg_load(rg[rn], ms[0:1, col:col + 1])
        e.dma_start(out=AP(dstt, RV(rg[rn]), dims), in_=src).then_inc(poh, 16)
    pl.g(f)


def gwait_po(ctx, pl, xpoint):
    po = ctx.posems[xpoint]
    wait(pl, "g", po, po.n)



def phase_apply(ctx, first=False, last=False, need_stats=True, stats_from_sub=False):
    nc, sb, dt, S = ctx.nc, ctx.sb, ctx.dt, ctx.sem
    mm, vv, ss = S["mm"], S["vv"], S["ss"]
    uid = ctx.uid = getattr(ctx, "uid", 0) + 1
    with contextlib.ExitStack() as stack:
        def T(name, shape, dtype):
            return stack.enter_context(nc.sbuf_tensor(f"{name}_{uid}", shape, dtype))

        def P(name, shape, dtype=F32):
            return stack.enter_context(nc.psum_tensor(f"{name}_{uid}", shape, dtype))

        pl = Plan()
        mst = T("ap_mst", [128, 2 * 512], F32)
        rbb = T("ap_rb", [128, 2 * 512], F32)
        wbf = T("ap_w", [128, 2 * 512], F32)
        ubf = T("ap_u", [128, 512], F32)
        x2b = T("ap_x2", [128, 2 * 512], BF16)
        sbb = T("ap_sbb", [128, 2 * 512], BF16)
        psS1 = P("ap_S1", [128, 512])
        psS2 = P("ap_S2", [128, 512])
        src_mst = dt["xT"] if first else dt["master"]
        dst_mst = dt["outT"] if last else dt["master"]
        free_w = {}
        free_m = {}
        free_r = {}
        free_x = {}
        for tc in range(NTC):
            if not first and stats_from_sub:
                fs = {}
                for mc in range(NCH):
                    sl = mc % 2
                    dsub = S[f"d_sub{sl}"]
                    if sl in fs:
                        wait(pl, "g", mm, fs[sl])
                    elif sl in free_r:
                        wait(pl, "g", vv, free_r[sl])
                    r_t = dma(pl, "g", rbb[:, sl * 512:(sl + 1) * 512],
                              AP(dt["sub_spill"], mc * 128 * TL + tc * 512,
                                 [[TL, 128], [1, 512]]), dsub)
                    wait(pl, "v", dsub, r_t)
                    cb_t = minc(pl, "v", vv, lambda e, o=sbb[:, sl * 512:(sl + 1) * 512],
                                i=rbb[:, sl * 512:(sl + 1) * 512]: e.tensor_copy(o, i))
                    wait(pl, "s", dsub, r_t)
                    sq_tt = minc(pl, "s", ss, lambda e, o=x2b[:, sl * 512:(sl + 1) * 512],
                                 i=rbb[:, sl * 512:(sl + 1) * 512]: e.activation(o, i, AF.Square))
                    wait(pl, "t", vv, cb_t)
                    wait(pl, "t", ss, sq_tt)
                    pl.t(lambda e, o=psS1[:, :], l=sb["ones_b"][:, :],
                         r=sbb[:, sl * 512:(sl + 1) * 512],
                         a=(mc == 0), b=(mc == NCH - 1): e.matmul(o, l, r, start=a, stop=b))
                    s_tt = minc(pl, "t", mm, lambda e, o=psS2[:, :], l=sb["ones_b"][:, :],
                                r=x2b[:, sl * 512:(sl + 1) * 512],
                                a=(mc == 0), b=(mc == NCH - 1): e.matmul(o, l, r, start=a, stop=b))
                    fs[sl] = s_tt
                wait(pl, "v", mm, fs[(NCH - 1) % 2])
                sc2 = plan_scalars_slice(ctx, pl, psS1[:, :], psS2[:, :],
                                         sb["row_e"][:, :], sb["row_f"][:, :], 2048.0)
                wait(pl, "t", vv, sc2)
                wait(pl, "g", mm, fs[(NCH - 1) % 2])
                wait(pl, "s", mm, fs[(NCH - 1) % 2])
            elif not first:
                plan_scalars_slice(ctx, pl,
                                   sb["ln_o1"][:, tc * 512:(tc + 1) * 512],
                                   sb["ln_o2"][:, tc * 512:(tc + 1) * 512],
                                   sb["row_e"][:, :], sb["row_f"][:, :], 2048.0)
            for mc in range(NCH):
                sl = mc % 2
                dm = S[f"d_mst{sl}"]
                dsub = S[f"d_sub{sl}"]
                dwb = S[f"d_wb{sl}"]
                if sl in free_m:
                    wait(pl, "y", vv, free_m[sl])
                m_t = dma(pl, "y", mst[:, sl * 512:(sl + 1) * 512],
                          AP(src_mst, mc * 128 * TL + tc * 512, [[TL, 128], [1, 512]]), dm)
                if not first:
                    if sl in free_r:
                        wait(pl, "g", vv, free_r[sl])
                    r_t = dma(pl, "g", rbb[:, sl * 512:(sl + 1) * 512],
                              AP(dt["sub_spill"], mc * 128 * TL + tc * 512, [[TL, 128], [1, 512]]), dsub)
                wait(pl, "v", dm, m_t)
                if not first:
                    wait(pl, "v", dsub, r_t)
                    pl.v(lambda e, o=ubf[:, :], i=rbb[:, sl * 512:(sl + 1) * 512],
                         j=sb["row_e"][:, :]: e.tensor_mul(o, i, j))
                    pl.v(lambda e, o=ubf[:, :], i=ubf[:, :], j=sb["row_f"][:, :]: e.tensor_add(o, i, j))
                    if sl in free_w:
                        wait(pl, "v", vv, free_w[sl])
                    wv = minc(pl, "v", vv, lambda e, o=wbf[:, sl * 512:(sl + 1) * 512],
                              i=mst[:, sl * 512:(sl + 1) * 512], j=ubf[:, :]:
                              e.tensor_add(o, i, j))
                else:
                    if sl in free_w:
                        wait(pl, "v", vv, free_w[sl])
                    wv = minc(pl, "v", vv, lambda e, o=wbf[:, sl * 512:(sl + 1) * 512],
                              i=mst[:, sl * 512:(sl + 1) * 512]: e.tensor_copy(o, i))
                free_m[sl] = wv
                free_r[sl] = wv
                wait(pl, "y", vv, wv)
                w_t = dma(pl, "y", AP(dst_mst, mc * 128 * TL + tc * 512, [[TL, 128], [1, 512]]),
                          wbf[:, sl * 512:(sl + 1) * 512], dwb)
                if need_stats:
                    cv = minc(pl, "v", vv,
                              lambda e, o=sb["xsb"][:, mc * TL + tc * 512:mc * TL + (tc + 1) * 512],
                              i=wbf[:, sl * 512:(sl + 1) * 512]: e.tensor_copy(o, i))
                    wait(pl, "s", vv, wv)
                    if sl in free_x:
                        wait(pl, "s", mm, free_x[sl])
                    sv = minc(pl, "s", ss, lambda e, o=x2b[:, sl * 512:(sl + 1) * 512],
                              i=wbf[:, sl * 512:(sl + 1) * 512]: e.activation(o, i, AF.Square))
                    wait(pl, "t", vv, cv)
                    wait(pl, "t", ss, sv)
                    pl.t(lambda e, o=psS1[:, :], l=sb["ones_b"][:, :],
                         r=sb["xsb"][:, mc * TL + tc * 512:mc * TL + (tc + 1) * 512],
                         a=(mc == 0), b=(mc == NCH - 1): e.matmul(o, l, r, start=a, stop=b))
                    s_t = minc(pl, "t", mm, lambda e, o=psS2[:, :], l=sb["ones_b"][:, :],
                               r=x2b[:, sl * 512:(sl + 1) * 512],
                               a=(mc == 0), b=(mc == NCH - 1): e.matmul(o, l, r, start=a, stop=b))
                    free_x[sl] = s_t
                    wait(pl, "v", dwb, w_t)
                    wait(pl, "v", mm, s_t)
                    fw = tinc(pl, "v", vv)
                    free_w[sl] = fw
                else:
                    wait(pl, "v", dwb, w_t)
                    fw = tinc(pl, "v", vv)
                    free_w[sl] = fw
            if need_stats:
                mt = tinc(pl, "t", mm)
                wait(pl, "v", mm, mt)
                vvt = plan_scalars_slice(ctx, pl, psS1[:, :], psS2[:, :],
                                         sb["ln_r"][:, tc * 512:(tc + 1) * 512],
                                         sb["ln_a"][:, tc * 512:(tc + 1) * 512], 2048.0)
                wait(pl, "t", vv, vvt)
        wait(pl, "y", S["d_wb0"], S["d_wb0"].n)
        wait(pl, "y", S["d_wb1"], S["d_wb1"].n)
        pl.run(nc)



def phase_l0_attn(ctx, is_last):
    nc, sb = ctx.nc, ctx.sb
    uidA = ctx.uid = getattr(ctx, "uid", 0) + 1
    with contextlib.ExitStack() as astack:
        sb["ln_o1"] = astack.enter_context(nc.sbuf_tensor(f"ln_o1_{uidA}", [128, TL], F32))
        sb["ln_o2"] = astack.enter_context(nc.sbuf_tensor(f"ln_o2_{uidA}", [128, TL], F32))
        sb["masks_b"] = astack.enter_context(nc.sbuf_tensor(f"masks_{uidA}", [128, 8 * 512], BF16))
        pl = Plan()
        mt = dma(pl, "g", sb["masks_b"][:, :], ctx.dt["masks"][:, :], ctx.sem["d_g"])
        wait(pl, "g", ctx.sem["d_g"], mt)
        pl.run(nc)
        plan_l0a_kv(ctx)
        plan_l0a_main(ctx)
        phase_apply(ctx, first=False, last=is_last, need_stats=not is_last)


def plan_l0a_kv(ctx):
    nc, sb, dt, S = ctx.nc, ctx.sb, ctx.dt, ctx.sem
    mm, vv, ss, gg = S["mm"], S["vv"], S["ss"], S["gg"]
    stack = contextlib.ExitStack()
    ctx.l0a_stack = stack
    uid0 = ctx.uid = getattr(ctx, "uid", 0) + 1

    def TP(name, shape, dtype):
        return stack.enter_context(nc.sbuf_tensor(f"{name}_{uid0}", shape, dtype))

    krot = TP("l0a_kr", [128, TG], BF16)
    kvtm = TP("l0a_ktm", [128, 32 * 128], BF16)
    kn2 = TP("l0a_kn2", [128, 8], F32)
    ctx.l0a = dict(krot=krot, kvtm=kvtm, kn2=kn2)

    uid = ctx.uid = getattr(ctx, "uid", 0) + 1
    with contextlib.ExitStack() as st2:
        def T(name, shape, dtype):
            return st2.enter_context(nc.sbuf_tensor(f"{name}_{uid}", shape, dtype))

        def P(name, shape, dtype=F32):
            return st2.enter_context(nc.psum_tensor(f"{name}_{uid}", shape, dtype))

        pl = Plan()
        xnr = T("l0k_xnr", [128, NCH * 512], BF16)
        xtmp = T("l0k_xt", [128, 512], F32)
        wlat = T("l0k_wlat", [128, 2048], BF16)
        blat = T("l0k_blat", [128, 1], F32)
        kvbf = T("l0k_kv", [128, TL], BF16)
        kfull = T("l0k_kf", [128, TG], BF16)
        cs_f = T("l0k_csf", [128, 512], F32)
        sn_f = T("l0k_snf", [128, 512], F32)
        rt_f = T("l0k_rtf", [128, 512], BF16)
        sqn = T("l0k_sqn", [128, 512], BF16)
        rowf = T("l0k_rowf", [128, 8], F32)
        psA = P("l0k_psA", [128, 512])
        psR = P("l0k_psR", [128, 512])
        psT = P("l0k_psT", [128, 512], BF16)
        psN = P("l0k_psN", [128, 512])

        wl_t = dma(pl, "g", AP(wlat, 0, [[2048, 128], [128, 16], [1, 128]]),
                   AP(dt["L0_attn_W_latent"], 0, [[128, 128], [128 * 128, 16], [1, 128]]),
                   S["d_wl"])
        bl_t = dma(pl, "g", blat[:, 0:1], dt["b_latent_c"][:, 0:1], S["d_bl"])
        for tc in range(NTC):
            xt = plan_xnr(ctx, pl, xnr, xtmp, tc,
                          sb["ln_r"][:, tc * 512:(tc + 1) * 512],
                          sb["ln_a"][:, tc * 512:(tc + 1) * 512])
            wait(pl, "t", vv, xt)
            if tc == 0:
                wait(pl, "t", S["d_wl"], wl_t)
            kv_mm = None
            for c in range(NCH):
                if c < NCH - 1:
                    pl.t(lambda e, o=psA[:, :], l=wlat[:, c * 128:(c + 1) * 128],
                         r=xnr[:, c * 512:(c + 1) * 512], a=(c == 0):
                         e.matmul(o, l, r, start=a, stop=False))
                else:
                    kv_mm = minc(pl, "t", mm, lambda e, o=psA[:, :],
                                 l=wlat[:, c * 128:(c + 1) * 128],
                                 r=xnr[:, c * 512:(c + 1) * 512]:
                                 e.matmul(o, l, r, start=False, stop=True))
            wait(pl, "s", mm, kv_mm)
            if tc == 0:
                wait(pl, "s", S["d_bl"], bl_t)
            kv_ev = minc(pl, "s", ss, lambda e, o=kvbf[:, tc * 512:(tc + 1) * 512],
                         i=psA[:, :], b=blat[:, 0:1]:
                         e.activation(o, i, AF.Identity, bias=b))
            wait(pl, "v", mm, kv_mm)
            wait(pl, "g", ss, kv_ev)
            gdma_shared(ctx, pl, 0, dt["kv_sh"], f"rb{tc}", tc, [[TG, 128], [1, 512]],
                        AP(kvbf, tc * 512, [[TL, 128], [1, 512]]))
        ctx.l0a.update(kvbf=kvbf, kfull=kfull, xnr=xnr, wlat=wlat)
        publish_flag(ctx, pl, 0)
        poll_flag(ctx, pl, 0)
        kv_in = dma(pl, "g", kfull[:, :], dt["kv_sh"][:, :], S["d_kf"])
        wait(pl, "g", S["d_kf"], kv_in)
        kv_rdy = tinc(pl, "g", gg)
        kn_zero = minc(pl, "g", gg, lambda e, t=kn2: e.memset(t[:, 0:8], 0.0))
        wait(pl, "t", gg, kv_rdy)
        wait(pl, "v", gg, kv_rdy)
        prev_ktv = None
        prev_nmv = None
        for cc in range(TG // 512):
            dma(pl, "y", cs_f[:, :], dt["cos0"][:, cc * 512:(cc + 1) * 512], S["d_cs"])
            cs_t = dma(pl, "y", sn_f[:, :], dt["sin0"][:, cc * 512:(cc + 1) * 512], S["d_cs"])
            pm = minc(pl, "t", mm, lambda e, o=psR[:, :], l=sb["perm0_b"][:, :],
                      r=kfull[:, cc * 512:(cc + 1) * 512]: e.matmul(o, l, r, start=True, stop=True))
            wait(pl, "v", mm, pm)
            wait(pl, "v", S["d_cs"], cs_t)
            pl.v(lambda e, o=rt_f[:, :], i=psR[:, :], j=sn_f[:, :]: e.tensor_mul(o, i, j))
            pl.v(lambda e, o=ctx.l0a["krot"][:, cc * 512:(cc + 1) * 512],
                 i=kfull[:, cc * 512:(cc + 1) * 512], j=cs_f[:, :]: e.tensor_mul(o, i, j))
            kr_t = minc(pl, "v", vv, lambda e, o=ctx.l0a["krot"][:, cc * 512:(cc + 1) * 512],
                        i=ctx.l0a["krot"][:, cc * 512:(cc + 1) * 512], j=rt_f[:, :]:
                        e.tensor_add(o, i, j))
            wait(pl, "t", vv, kr_t)
            wait(pl, "y", vv, kr_t)
            if prev_ktv is not None:
                wait(pl, "t", vv, prev_ktv)
            for q in range(3):
                kt = cc * 4 + q
                pl.t(lambda e, o=psT[:, q * 128:(q + 1) * 128],
                     i=kfull[:, kt * 128:(kt + 1) * 128], d=sb["ident_b"][:, :]:
                     e.transpose(o, i, d))
            tr = minc(pl, "t", mm, lambda e, o=psT[:, 3 * 128:4 * 128],
                      i=kfull[:, (cc * 4 + 3) * 128:(cc * 4 + 4) * 128], d=sb["ident_b"][:, :]:
                      e.transpose(o, i, d))
            wait(pl, "v", mm, tr)
            prev_ktv = minc(pl, "v", vv,
                            lambda e, o=ctx.l0a["kvtm"][:, cc * 512:(cc + 1) * 512], i=psT[:, :]:
                            e.tensor_copy(o, i))
            wait(pl, "s", vv, kr_t)
            if prev_nmv is not None:
                wait(pl, "s", mm, prev_nm)
            sq_t = minc(pl, "s", ss,
                        lambda e, o=sqn[:, :], i=ctx.l0a["krot"][:, cc * 512:(cc + 1) * 512]:
                        e.activation(o, i, AF.Square))
            wait(pl, "t", ss, sq_t)
            if prev_nmv is not None:
                wait(pl, "t", vv, prev_nmv)
            nm = minc(pl, "t", mm, lambda e, o=psN[:, :], l=sb["ones_b"][:, :], r=sqn[:, :]:
                      e.matmul(o, l, r, start=True, stop=True))
            prev_nm = nm
            wait(pl, "v", mm, nm)
            if cc == 0:
                wait(pl, "v", gg, kn_zero)
            pl.v(lambda e, o=rowf[:, 0:1], i=psN[:, :]: e.reduce_max(o, i, axis=AX.X))
            pl.v(lambda e, o=ctx.l0a["kn2"][:, 1:2], i=ctx.l0a["kn2"][:, 0:1],
                 j=rowf[:, 0:1]: e.tensor_max(o, i, j))
            prev_nmv = minc(pl, "v", vv,
                            lambda e, o=ctx.l0a["kn2"][:, 0:1], i=ctx.l0a["kn2"][:, 1:2]:
                            e.tensor_copy(o, i))
        wait(pl, "s", vv, prev_nmv)
        minc(pl, "s", ss, lambda e, o=ctx.l0a["kn2"][:, 2:3], i=ctx.l0a["kn2"][:, 0:1]:
             e.activation(o, i, AF.Sqrt))
        pl.run(nc)


def plan_l0a_main(ctx):
    nc, sb, dt, S = ctx.nc, ctx.sb, ctx.dt, ctx.sem
    mm, vv, ss, gg = S["mm"], S["vv"], S["ss"], S["gg"]
    krot, kvtm, kn2 = ctx.l0a["krot"], ctx.l0a["kvtm"], ctx.l0a["kn2"]
    uid = ctx.uid = getattr(ctx, "uid", 0) + 1
    with contextlib.ExitStack() as stack:
        def T(name, shape, dtype):
            return stack.enter_context(nc.sbuf_tensor(f"{name}_{uid}", shape, dtype))

        def P(name, shape, dtype=F32):
            return stack.enter_context(nc.psum_tensor(f"{name}_{uid}", shape, dtype))

        pl = Plan()
        xnr = T("l0m_xnr", [128, NCH * 512], BF16)
        xtmp = T("l0m_xt", [128, 512], F32)
        wtile = T("l0m_w", [128, 2 * 2048], BF16)
        bdc = T("l0m_bd", [128, 16], F32)
        bpc = T("l0m_bp", [128, 16], F32)
        cs_o = T("l0m_cs", [128, 512], F32)
        sn_o = T("l0m_sn", [128, 512], F32)
        qtmp = T("l0m_qt", [128, 512], BF16)
        qrot = T("l0m_qr", [128, 512], BF16)
        rt_q = T("l0m_rtq", [128, 512], BF16)
        sqn = T("l0m_sqn", [128, 512], BF16)
        rowf = T("l0m_rowf", [128, 8], F32)
        recip = T("l0m_rcp", [128, 512], F32)
        pbuf = T("l0m_p", [128, 2 * 512], BF16)
        pmsk = T("l0m_pm", [128, 2 * 512], BF16)
        ynb = T("l0m_yn", [128, NHEAD * 512], BF16)
        ysb = T("l0m_ysb", [128, 512], F32)
        sub_t = T("l0m_sub", [128, 2 * 512], F32)
        sub_b = T("l0m_subb", [128, 512], BF16)
        sub_2 = T("l0m_sub2", [128, 512], BF16)
        psS = [P(f"l0m_psS{i}", [128, 512]) for i in range(2)]
        psY = P("l0m_psY", [128, 512])
        psSum = P("l0m_psSum", [128, 512])
        psA = P("l0m_psA", [128, 512])
        psB = P("l0m_psB", [128, 512])
        psO1 = P("l0m_psO1", [128, 512])
        psO2 = P("l0m_psO2", [128, 512])

        dma(pl, "g", bdc[:, :], dt["b_d_c"][:, :], S["d_bias"])
        bias_t = dma(pl, "g", bpc[:, :], dt["b_proj_c"][:, :], S["d_bias"])

        def load_wd(h, slot):
            return dma(pl, "g", AP(wtile, slot * 2048, [[2 * 2048, 128], [128, 16], [1, 128]]),
                       AP(dt["L0_attn_W_d"], h * 128, [[2048, 128], [128 * 2048, 16], [1, 128]]),
                       S[f"d_wt{slot}"])

        def load_wp(mc, slot):
            return dma(pl, "g", AP(wtile, slot * 2048, [[2 * 2048, 128], [128, 16], [1, 128]]),
                       AP(dt["L0_attn_W_proj"], mc * 128, [[2048, 128], [128 * 2048, 16], [1, 128]]),
                       S[f"d_wt{slot}"])

        tile_free = {0: 0, 1: 0}
        prev = {}
        sp_free = {}
        for qc in range(NTC):
            KT = 8 * (qc + 1)
            xt = plan_xnr(ctx, pl, xnr, xtmp, qc,
                          sb["ln_r"][:, qc * 512:(qc + 1) * 512],
                          sb["ln_a"][:, qc * 512:(qc + 1) * 512])
            if qc > 0:
                wait(pl, "y", vv, last_qr)
            dma(pl, "y", cs_o[:, :], dt["cos0o"][:, qc * 512:(qc + 1) * 512], S["d_cs"])
            cs_t = dma(pl, "y", sn_o[:, :], dt["sin0o"][:, qc * 512:(qc + 1) * 512], S["d_cs"])
            wait(pl, "g", mm, tile_free[0])
            wt = load_wd(0, 0)
            for h in range(NHEAD):
                slot = h % 2
                if h + 1 < NHEAD:
                    ns = (h + 1) % 2
                    wait(pl, "g", mm, tile_free[ns])
                    nwt = load_wd(h + 1, ns)
                wait(pl, "t", S[f"d_wt{slot}"], wt)
                if h == 0:
                    wait(pl, "t", vv, xt)
                for c in range(NCH - 1):
                    pl.t(lambda e, o=psA[:, :],
                         l=wtile[:, slot * 2048 + c * 128:slot * 2048 + (c + 1) * 128],
                         r=xnr[:, c * 512:(c + 1) * 512], a=(c == 0):
                         e.matmul(o, l, r, start=a, stop=False))
                pj = minc(pl, "t", mm, lambda e, o=psA[:, :],
                          l=wtile[:, slot * 2048 + 15 * 128:slot * 2048 + 16 * 128],
                          r=xnr[:, 15 * 512:16 * 512]:
                          e.matmul(o, l, r, start=False, stop=True))
                tile_free[slot] = pj
                wait(pl, "s", mm, pj)
                if h == 0 and qc == 0:
                    wait(pl, "s", S["d_bias"], bias_t)
                if prev:
                    wait(pl, "s", vv, prev["qr"])
                qe = minc(pl, "s", ss, lambda e, o=qtmp[:, :], i=psA[:, :], b=bdc[:, h:h + 1]:
                          e.activation(o, i, AF.Identity, bias=b))
                if prev:
                    wait(pl, "s", mm, prev["nm"])
                sq_t = minc(pl, "s", ss, lambda e, o=sqn[:, :], i=qtmp[:, :]:
                            e.activation(o, i, AF.Square))
                wait(pl, "t", ss, sq_t)
                if prev:
                    wait(pl, "t", vv, prev["rc"])
                nm = minc(pl, "t", mm, lambda e, o=psSum[:, :], l=sb["ones_b"][:, :], r=sqn[:, :]:
                          e.matmul(o, l, r, start=True, stop=True))
                wait(pl, "v", mm, nm)
                nv = minc(pl, "v", vv, lambda e, o=rowf[:, 0:1], i=psSum[:, :]:
                          e.reduce_max(o, i, axis=AX.X))
                wait(pl, "s", vv, nv)
                sv = minc(pl, "s", ss, lambda e, o=rowf[:, 1:2], i=rowf[:, 0:1]:
                          e.activation(o, i, AF.Sqrt))
                wait(pl, "v", ss, sv)
                pl.v(lambda e, o=rowf[:, 2:3], i=rowf[:, 1:2], j=kn2[:, 2:3]:
                     e.tensor_mul(o, i, j))
                c0_rdy = minc(pl, "v", vv, lambda e, o=rowf[:, 3:4], i=rowf[:, 2:3]:
                              e.tensor_scalar(o, i, -SCALE0, None, ALU.mult))
                wait(pl, "t", ss, qe)
                pm = minc(pl, "t", mm, lambda e, o=psB[:, :], l=sb["perm0_b"][:, :], r=qtmp[:, :]:
                          e.matmul(o, l, r, start=True, stop=True))
                wait(pl, "v", mm, pm)
                if h == 0:
                    wait(pl, "v", S["d_cs"], cs_t)
                pl.v(lambda e, o=rt_q[:, :], i=psB[:, :], j=sn_o[:, :]: e.tensor_mul(o, i, j))
                pl.v(lambda e, o=qrot[:, :], i=qtmp[:, :], j=cs_o[:, :]: e.tensor_mul(o, i, j))
                qr = minc(pl, "v", vv, lambda e, o=qrot[:, :], i=qrot[:, :], j=rt_q[:, :]:
                          e.tensor_add(o, i, j))
                wait(pl, "t", vv, qr)
                wait(pl, "s", vv, c0_rdy)
                for gi in range(KT // 2):
                    qk = None
                    for u in range(2):
                        kt = gi * 2 + u
                        if u < 1:
                            pl.t(lambda e, o=psS[u][:, :], l=krot[:, kt * 128:(kt + 1) * 128],
                                 r=qrot[:, :]: e.matmul(o, l, r, start=True, stop=True))
                        else:
                            qk = minc(pl, "t", mm,
                                      lambda e, o=psS[u][:, :], l=krot[:, kt * 128:(kt + 1) * 128],
                                      r=qrot[:, :]: e.matmul(o, l, r, start=True, stop=True))
                    wait(pl, "s", mm, qk)
                    ex = None
                    for u in range(2):
                        if u < 1:
                            pl.s(lambda e, o=pbuf[:, u * 512:(u + 1) * 512], i=psS[u][:, :],
                                 c=rowf[:, 3:4]: e.activation(o, i, AF.Exp, bias=c, scale=SCALE0))
                        else:
                            ex = minc(pl, "s", ss,
                                      lambda e, o=pbuf[:, u * 512:(u + 1) * 512], i=psS[u][:, :],
                                      c=rowf[:, 3:4]: e.activation(o, i, AF.Exp, bias=c, scale=SCALE0))
                    wait(pl, "t", ss, ex)
                    mask_us = [u for u in range(2) if (gi * 2 + u) - 8 * qc >= 0]
                    mk = None
                    if mask_us:
                        wait(pl, "v", ss, ex)
                        for u in mask_us:
                            w = (gi * 2 + u) - 8 * qc
                            if u != mask_us[-1]:
                                pl.v(lambda e, o=pmsk[:, u * 512:(u + 1) * 512],
                                     i=pbuf[:, u * 512:(u + 1) * 512],
                                     j=sb["masks_b"][:, w * 512:(w + 1) * 512]:
                                     e.tensor_mul(o, i, j))
                            else:
                                mk = minc(pl, "v", vv,
                                          lambda e, o=pmsk[:, u * 512:(u + 1) * 512],
                                          i=pbuf[:, u * 512:(u + 1) * 512],
                                          j=sb["masks_b"][:, w * 512:(w + 1) * 512]:
                                          e.tensor_mul(o, i, j))
                        wait(pl, "t", vv, mk)
                    av = None
                    for u in range(2):
                        kt = gi * 2 + u
                        w = kt - 8 * qc
                        src = pmsk if w >= 0 else pbuf
                        pl.t(lambda e, o=psY[:, :], l=kvtm[:, kt * 128:(kt + 1) * 128],
                             r=src[:, u * 512:(u + 1) * 512], a=(kt == 0), b=(kt == KT - 1):
                             e.matmul(o, l, r, start=a, stop=b))
                        if u < 1:
                            pl.t(lambda e, o=psSum[:, :], l=sb["ones_b"][:, :],
                                 r=src[:, u * 512:(u + 1) * 512], a=(kt == 0), b=(kt == KT - 1):
                                 e.matmul(o, l, r, start=a, stop=b))
                        else:
                            av = minc(pl, "t", mm,
                                      lambda e, o=psSum[:, :], l=sb["ones_b"][:, :],
                                      r=src[:, u * 512:(u + 1) * 512], a=(kt == 0), b=(kt == KT - 1):
                                      e.matmul(o, l, r, start=a, stop=b))
                    wait(pl, "s", mm, av)
                    wait(pl, "v", mm, av)
                rc = minc(pl, "v", vv, lambda e, o=recip[:, :], i=psSum[:, :]:
                          e.reciprocal(o, i))
                ye = minc(pl, "s", ss, lambda e, o=ysb[:, :], i=psY[:, :]:
                          e.activation(o, i, AF.Identity))
                wait(pl, "v", ss, ye)
                yn = minc(pl, "v", vv, lambda e, o=ynb[:, h * 512:(h + 1) * 512],
                          i=ysb[:, :], j=recip[:, :]: e.tensor_mul(o, i, j))
                wait(pl, "t", vv, yn)
                prev = dict(qr=qr, nm=nm, yn=yn, rc=rc)
                last_qr = qr
                if h + 1 < NHEAD:
                    wt = nwt

            wait(pl, "g", mm, tile_free[0])
            wp = load_wp(0, 0)
            for mc in range(NCH):
                slot = mc % 2
                if mc + 1 < NCH:
                    ns = (mc + 1) % 2
                    wait(pl, "g", mm, tile_free[ns])
                    nwp = load_wp(mc + 1, ns)
                wait(pl, "t", S[f"d_wt{slot}"], wp)
                for hc in range(NHEAD - 1):
                    pl.t(lambda e, o=psA[:, :],
                         l=wtile[:, slot * 2048 + hc * 128:slot * 2048 + (hc + 1) * 128],
                         r=ynb[:, hc * 512:(hc + 1) * 512], a=(hc == 0):
                         e.matmul(o, l, r, start=a, stop=False))
                oj = minc(pl, "t", mm, lambda e, o=psA[:, :],
                          l=wtile[:, slot * 2048 + 15 * 128:slot * 2048 + 16 * 128],
                          r=ynb[:, 15 * 512:16 * 512]:
                          e.matmul(o, l, r, start=False, stop=True))
                tile_free[slot] = oj
                wait(pl, "s", mm, oj)
                if mc == 0 and qc == 0:
                    wait(pl, "s", S["d_bias"], bias_t)
                if slot in sp_free:
                    wait(pl, "s", S[f"d_sp{slot}"], sp_free[slot])
                se = minc(pl, "s", ss, lambda e, o=sub_t[:, slot * 512:(slot + 1) * 512],
                          i=psA[:, :], b=bpc[:, mc:mc + 1]:
                          e.activation(o, i, AF.Identity, bias=b))
                wait(pl, "v", ss, se)
                if qc > 0 or mc >= 1:
                    wait(pl, "v", mm, prev_stt)
                sbt = minc(pl, "v", vv, lambda e, o=sub_b[:, :],
                           i=sub_t[:, slot * 512:(slot + 1) * 512]: e.tensor_copy(o, i))
                if qc > 0 or mc >= 1:
                    wait(pl, "s", mm, prev_stt2)
                s2t = minc(pl, "s", ss, lambda e, o=sub_2[:, :],
                           i=sub_t[:, slot * 512:(slot + 1) * 512]: e.activation(o, i, AF.Square))
                wait(pl, "t", vv, sbt)
                wait(pl, "t", ss, s2t)
                pl.t(lambda e, o=psO1[:, :], l=sb["ones_b"][:, :], r=sub_b[:, :],
                     a=(mc == 0), b=(mc == NCH - 1): e.matmul(o, l, r, start=a, stop=b))
                stt = minc(pl, "t", mm, lambda e, o=psO2[:, :], l=sb["ones_b"][:, :],
                           r=sub_2[:, :], a=(mc == 0), b=(mc == NCH - 1):
                           e.matmul(o, l, r, start=a, stop=b))
                prev_stt = stt
                prev_stt2 = stt
                wait(pl, "y", ss, se)
                spw = dma(pl, "y", AP(dt["sub_spill"], mc * 128 * TL + qc * 512,
                                      [[TL, 128], [1, 512]]),
                          sub_t[:, slot * 512:(slot + 1) * 512], S[f"d_sp{slot}"])
                sp_free[slot] = spw
                if mc + 1 < NCH:
                    wp = nwp
            wait(pl, "v", mm, prev_stt)
            pl.v(lambda e, o=sb["ln_o1"][:, qc * 512:(qc + 1) * 512], i=psO1[:, :]:
                 e.tensor_copy(o, i))
            ot = minc(pl, "v", vv, lambda e, o=sb["ln_o2"][:, qc * 512:(qc + 1) * 512],
                      i=psO2[:, :]: e.tensor_copy(o, i))
            wait(pl, "t", vv, ot)
        pl.run(nc)
    ctx.l0a_stack.close()






def phase_l0_ffn(ctx, is_last):
    plan_moe_gate(ctx)
    plan_moe_experts(ctx)
    phase_apply(ctx, first=False, last=is_last, need_stats=not is_last,
                stats_from_sub=True)


def plan_moe_gate(ctx):
    nc, sb, dt, S = ctx.nc, ctx.sb, ctx.dt, ctx.sem
    mm, vv, ss, gg = S["mm"], S["vv"], S["ss"], S["gg"]
    stack = contextlib.ExitStack()
    ctx.moe_stack = stack
    uid0 = ctx.uid = getattr(ctx, "uid", 0) + 1

    def TP(name, shape, dtype):
        return stack.enter_context(nc.sbuf_tensor(f"{name}_{uid0}", shape, dtype))

    xnr_all = TP("moe_xnr", [128, NCH * TL], BF16)
    combT = TP("moe_combT", [8, TL], F32)
    crow = TP("moe_crow", [1, 512], F32)
    ctx.moe = dict(xnr_all=xnr_all, combT=combT, crow=crow)

    uid = ctx.uid = getattr(ctx, "uid", 0) + 1
    with contextlib.ExitStack() as st2:
        def T(name, shape, dtype):
            return st2.enter_context(nc.sbuf_tensor(f"{name}_{uid}", shape, dtype))

        def P(name, shape, dtype=F32):
            return st2.enter_context(nc.psum_tensor(f"{name}_{uid}", shape, dtype))

        pl = Plan()
        xtmp = T("mg_xt", [128, 512], F32)
        gw = T("mg_gw", [128, 16 * 8], BF16)
        gsb = T("mg_gsb", [8, TL], F32)
        idf = T("mg_idf", [128, 128], F32)
        sc_tm = T("mg_sctm", [128, 16 * 8], F32)
        prob = T("mg_prob", [128, 16 * 8], F32)
        rw1 = T("mg_rw1", [128, 4], F32)
        psG = P("mg_psG", [8, 512])
        psT2 = P("mg_psT2", [128, 8 * 16])
        psC = P("mg_psC", [8, 128])

        g_t = dma(pl, "g", AP(gw, 0, [[128, 128], [8, 16], [1, 8]]),
                  AP(dt["L0_ffn_gate_w"], 0, [[8, 128], [128 * 8, 16], [1, 8]]),
                  S["d_wl"])
        i_t = dma(pl, "g", idf[:, :], dt["ident"][:, :], S["d_bl"])

        for tc in range(NTC):
            xt = plan_xnr(ctx, pl, xnr_all, xtmp, tc,
                          sb["ln_r"][:, tc * 512:(tc + 1) * 512],
                          sb["ln_a"][:, tc * 512:(tc + 1) * 512], full=True)
        wait(pl, "t", S["d_wl"], g_t)
        wait(pl, "t", vv, xt)
        for tc in range(NTC):
            for c in range(NCH - 1):
                pl.t(lambda e, o=psG[:, :], l=gw[:, c * 8:(c + 1) * 8],
                     r=xnr_all[:, c * TL + tc * 512:c * TL + (tc + 1) * 512],
                     a=(c == 0): e.matmul(o, l, r, start=a, stop=False))
            gmt = minc(pl, "t", mm, lambda e, o=psG[:, :], l=gw[:, 15 * 8:16 * 8],
                       r=xnr_all[:, 15 * TL + tc * 512:15 * TL + (tc + 1) * 512]:
                       e.matmul(o, l, r, start=False, stop=True))
            wait(pl, "s", mm, gmt)
            ge = minc(pl, "s", ss, lambda e, o=gsb[0:8, tc * 512:(tc + 1) * 512],
                      i=psG[:, :]: e.activation(o, i, AF.Identity))
            wait(pl, "t", ss, ge)
        wait(pl, "t", S["d_bl"], i_t)
        for pc in range(16):
            lst = (pc == 15)
            fn = (lambda e, o=psT2[:, pc * 8:(pc + 1) * 8],
                  i=gsb[0:8, pc * 128:(pc + 1) * 128], d=idf[0:8, 0:8]:
                  e.transpose(o, i, d))
            if lst:
                t2t = minc(pl, "t", mm, fn)
            else:
                pl.t(fn)
        wait(pl, "v", mm, t2t)
        sc_t = minc(pl, "v", vv, lambda e, o=sc_tm[:, :], i=psT2[:, :]: e.tensor_copy(o, i))

        BIG = 1.0e30
        for pc in range(16):
            sl_ap = sc_tm[:, pc * 8:(pc + 1) * 8]
            pr_ap = prob[:, pc * 8:(pc + 1) * 8]
            if pc == 0:
                wait(pl, "v", vv, sc_t)
            m1 = minc(pl, "v", vv, lambda e, o=rw1[:, 0:1], i=sl_ap:
                      e.reduce_max(o, i, axis=AX.X))
            wait(pl, "v", vv, m1)
            nm1 = minc(pl, "v", vv, lambda e, o=rw1[:, 1:2], i=rw1[:, 0:1]:
                       e.tensor_scalar(o, i, -1.0, None, ALU.mult))
            wait(pl, "s", vv, nm1)
            ex = minc(pl, "s", ss, lambda e, o=pr_ap, i=sl_ap, b=rw1[:, 1:2]:
                      e.activation(o, i, AF.Exp, bias=b))
            wait(pl, "v", ss, ex)
            sm = minc(pl, "v", vv, lambda e, o=rw1[:, 2:3], i=pr_ap:
                      e.reduce_sum(o, i, axis=AX.X))
            wait(pl, "v", vv, sm)
            rc = minc(pl, "v", vv, lambda e, o=rw1[:, 3:4], i=rw1[:, 2:3]:
                      e.reciprocal(o, i))
            wait(pl, "v", vv, rc)
            pn = minc(pl, "v", vv, lambda e, o=pr_ap, i=pr_ap, j=rw1[:, 3:4]:
                      e.tensor_scalar(o, i, j, None, ALU.mult))
            wait(pl, "v", vv, pn)
            m1b = minc(pl, "v", vv, lambda e, o=rw1[:, 0:1], i=pr_ap:
                       e.reduce_max(o, i, axis=AX.X))
            wait(pl, "v", vv, m1b)
            eq = minc(pl, "v", vv, lambda e, o=sl_ap, i=pr_ap, j=rw1[:, 0:1]:
                      e.tensor_scalar(o, i, j, -BIG, ALU.is_ge, ALU.mult))
            wait(pl, "v", vv, eq)
            msk = minc(pl, "v", vv, lambda e, o=sl_ap, i=pr_ap, j=sl_ap:
                       e.tensor_add(o, i, j))
            wait(pl, "v", vv, msk)
            m2 = minc(pl, "v", vv, lambda e, o=rw1[:, 1:2], i=sl_ap:
                      e.reduce_max(o, i, axis=AX.X))
            wait(pl, "v", vv, m2)
            kp = minc(pl, "v", vv, lambda e, o=sl_ap, i=pr_ap, j=rw1[:, 1:2]:
                      e.tensor_scalar(o, i, j, None, ALU.is_ge))
            wait(pl, "v", vv, kp)
            cm = minc(pl, "v", vv, lambda e, o=pr_ap, i=pr_ap, j=sl_ap:
                      e.tensor_mul(o, i, j))
            wait(pl, "v", vv, cm)
        wait(pl, "t", vv, ctx.sem["vv"].n)
        for pc in range(16):
            lst = (pc == 15)
            fn = (lambda e, o=psC[:, :], i=prob[:, pc * 8:(pc + 1) * 8], d=idf[:, :]:
                  e.transpose(o, i, d))
            if lst:
                ct = minc(pl, "t", mm, fn)
            else:
                ct = minc(pl, "t", mm, fn)
            wait(pl, "s", mm, ct)
            cte = minc(pl, "s", ss, lambda e, o=combT[0:8, pc * 128:(pc + 1) * 128],
                       i=psC[:, :]: e.activation(o, i, AF.Identity))
            wait(pl, "t", ss, cte)
        wait(pl, "g", ss, ctx.sem["ss"].n)
        cw = dma(pl, "g", dt["comb_dram"][:, :], combT[:, :], S["d_x"])
        wait(pl, "g", S["d_x"], cw)
        pl.run(nc)


def plan_moe_experts(ctx):
    nc, sb, dt, S = ctx.nc, ctx.sb, ctx.dt, ctx.sem
    mm, vv, ss, gg = S["mm"], S["vv"], S["ss"], S["gg"]
    xnr_all = ctx.moe["xnr_all"]
    combT = ctx.moe["combT"]
    crow = ctx.moe["crow"]
    uid = ctx.uid = getattr(ctx, "uid", 0) + 1
    with contextlib.ExitStack() as stack:
        def T(name, shape, dtype):
            return stack.enter_context(nc.sbuf_tensor(f"{name}_{uid}", shape, dtype))

        def P(name, shape, dtype=F32):
            return stack.enter_context(nc.psum_tensor(f"{name}_{uid}", shape, dtype))

        pl = Plan()
        xsb_addr = nc.lookup_mloc(sb["xsb"]).addr
        hbuf = nc.alloc_sbuf_tensor_at(f"moe_h_{uid}", [128, 16 * TL], BF16,
                                       offset=xsb_addr)
        wg = T("me_wg", [128, 2 * 2048], BF16)
        wu = T("me_wu", [128, 2 * 2048], BF16)
        wd = T("me_wd", [128, 2 * 11 * 128 * 2], BF16)
        t1b = T("me_t1", [128, 512], BF16)
        t2b = T("me_t2", [128, 512], BF16)
        comb_sb = T("me_cmb", [128, TL], BF16)
        dsb = T("me_dsb", [128, 2 * 512], F32)
        psA = P("me_psA", [128, 512])
        psB = P("me_psB", [128, 512])
        psC = P("me_psC", [128, 512])

        d_acc = S["d_acc"]
        prev_acc_tgt = d_acc.n

        def load_gu(ei, m, slot, HCH):
            if ei < 8:
                srcs = [("L0_ffn_w_gate", wg), ("L0_ffn_w_up", wu)]
                H = 1408
                base = ei * D * H
            else:
                srcs = [("L0_ffn_sw_gate", wg), ("L0_ffn_sw_up", wu)]
                H = 2816
                base = 0
            ts = []
            for nm, dstb in srcs:
                ts.append(dma(pl, "g",
                              AP(dstb, slot * 2048, [[2 * 2048, 128], [128, 16], [1, 128]]),
                              AP(dt[nm], base + m * 128, [[H, 128], [128 * H, 16], [1, 128]]),
                              S[f"d_we{slot}"]))
            return ts[-1]

        def load_wd(ei, mc, slot, HCH):
            if ei < 8:
                nm, H, base = "L0_ffn_w_down", 1408, ei * 1408 * D
            else:
                nm, H, base = "L0_ffn_sw_down", 2816, 0
            return dma(pl, "g",
                       AP(wd, slot * (22 * 128), [[2 * 22 * 128, 128], [128, HCH], [1, 128]]),
                       AP(dt[nm], base + mc * 128, [[D, 128], [128 * D, HCH], [1, 128]]),
                       S[f"d_wt{slot}"])

        gu_free = {0: 0, 1: 0}
        wd_free = {0: 0, 1: 0}
        dsb_free = {}
        for ei in range(MOE_EXPERTS):
            HCH = 11 if ei < 8 else 22
            shared = (ei == 8)
            if not shared:
                for tc in range(NTC):
                    ct = dma(pl, "g", crow[0:1, :],
                             dt["comb_dram"][ei:ei + 1, tc * 512:(tc + 1) * 512], S["d_x"])
                    wait(pl, "t", S["d_x"], ct)
                    cbt = bc_row(ctx, pl, psC[:, :], crow[0:1, :], sem=ctx.sem["mm"])
                    wait(pl, "v", mm, cbt)
                    cse = minc(pl, "v", vv, lambda e, o=comb_sb[:, tc * 512:(tc + 1) * 512],
                               i=psC[:, :]: e.tensor_copy(o, i))
                    wait(pl, "t", vv, cse)
                    wait(pl, "g", vv, cse)
            if MOE_SKIP_GU:
                continue
            halves = 1 if not shared else 2
            wait(pl, "g", d_acc, prev_acc_tgt)
            for hf in range(halves):
                tcs = list(range(NTC)) if halves == 1 else [2 * hf, 2 * hf + 1]
                hw = len(tcs) * 512

                def hcol(m, tc, hw=hw, t0=tcs[0]):
                    return m * hw + (tc - t0) * 512

                wait(pl, "g", mm, gu_free[0])
                gt = load_gu(ei, 0, 0, HCH)
                for m in range(HCH):
                    slot = m % 2
                    if m + 1 < HCH:
                        ns = (m + 1) % 2
                        wait(pl, "g", mm, gu_free[ns])
                        ngt = load_gu(ei, m + 1, ns, HCH)
                    wait(pl, "t", S[f"d_we{slot}"], gt)
                    for tc in tcs:
                        for c in range(NCH):
                            pl.t(lambda e, o=psA[:, :],
                                 l=wg[:, slot * 2048 + c * 128:slot * 2048 + (c + 1) * 128],
                                 r=xnr_all[:, c * TL + tc * 512:c * TL + (tc + 1) * 512],
                                 a=(c == 0), b=(c == NCH - 1): e.matmul(o, l, r, start=a, stop=b))
                        for c in range(NCH - 1):
                            pl.t(lambda e, o=psB[:, :],
                                 l=wu[:, slot * 2048 + c * 128:slot * 2048 + (c + 1) * 128],
                                 r=xnr_all[:, c * TL + tc * 512:c * TL + (tc + 1) * 512],
                                 a=(c == 0): e.matmul(o, l, r, start=a, stop=False))
                        gut = minc(pl, "t", mm, lambda e, o=psB[:, :],
                                   l=wu[:, slot * 2048 + 15 * 128:slot * 2048 + 16 * 128],
                                   r=xnr_all[:, 15 * TL + tc * 512:15 * TL + (tc + 1) * 512]:
                                   e.matmul(o, l, r, start=False, stop=True))
                        gu_free[slot] = gut
                        wait(pl, "s", mm, gut)
                        si = minc(pl, "s", ss, lambda e, o=t1b[:, :], i=psA[:, :]:
                                  e.activation(o, i, AF.Silu))
                        wait(pl, "v", ss, si)
                        u2 = minc(pl, "v", vv, lambda e, o=t2b[:, :], i=t1b[:, :], j=psB[:, :]:
                                  e.tensor_mul(o, i, j))
                        wait(pl, "v", vv, u2)
                        hc = hcol(m, tc)
                        if shared:
                            hv = minc(pl, "v", vv,
                                      lambda e, o=hbuf[:, hc:hc + 512],
                                      i=t2b[:, :]: e.tensor_copy(o, i))
                        else:
                            hv = minc(pl, "v", vv,
                                      lambda e, o=hbuf[:, hc:hc + 512],
                                      i=t2b[:, :], j=comb_sb[:, tc * 512:(tc + 1) * 512]:
                                      e.tensor_mul(o, i, j))
                        wait(pl, "t", ss, si)
                        wait(pl, "t", vv, u2)
                    if m + 1 < HCH:
                        gt = ngt
                h_done = ctx.sem["vv"].n
                if MOE_SKIP_DOWN:
                    continue
                wait(pl, "g", mm, wd_free[0])
                wt = load_wd(ei, 0, 0, HCH)
                for mc in range(NCH):
                    slot = mc % 2
                    if mc + 1 < NCH:
                        ns = (mc + 1) % 2
                        wait(pl, "g", mm, wd_free[ns])
                        nwt = load_wd(ei, mc + 1, ns, HCH)
                    wait(pl, "t", S[f"d_wt{slot}"], wt)
                    wait(pl, "t", vv, h_done)
                    for tc in tcs:
                        for hk in range(HCH - 1):
                            pl.t(lambda e, o=psA[:, :],
                                 l=wd[:, slot * (22 * 128) + hk * 128:slot * (22 * 128) + (hk + 1) * 128],
                                 r=hbuf[:, hcol(hk, tc):hcol(hk, tc) + 512],
                                 a=(hk == 0): e.matmul(o, l, r, start=a, stop=False))
                        dmt = minc(pl, "t", mm, lambda e, o=psA[:, :],
                                   l=wd[:, slot * (22 * 128) + (HCH - 1) * 128:slot * (22 * 128) + HCH * 128],
                                   r=hbuf[:, hcol(HCH - 1, tc):hcol(HCH - 1, tc) + 512]:
                                   e.matmul(o, l, r, start=False, stop=True))
                        wd_free[slot] = dmt
                        wait(pl, "s", mm, dmt)
                        dsl = (mc * NTC + tc) % 2
                        if dsl in dsb_free:
                            wait(pl, "s", S["d_acc"], dsb_free[dsl])
                        de = minc(pl, "s", ss, lambda e, o=dsb[:, dsl * 512:(dsl + 1) * 512],
                                  i=psA[:, :]: e.activation(o, i, AF.Identity))
                        wait(pl, "t", ss, de)
                        wait(pl, "g", ss, de)
                        at = dma(pl, "g", AP(dt["sub_spill"], mc * 128 * TL + tc * 512,
                                             [[TL, 128], [1, 512]]),
                                 dsb[:, dsl * 512:(dsl + 1) * 512], d_acc,
                                 **({} if ei == 0 else dict(accum_op=ALU.add)))
                        dsb_free[dsl] = at
                    if mc + 1 < NCH:
                        wt = nwt
            prev_acc_tgt = d_acc.n
        wait(pl, "g", d_acc, d_acc.n)
        pl.run(nc)
    ctx.moe_stack.close()
    ctx.moe = None





def phase_l1_attn(ctx, is_last):
    nc, sb = ctx.nc, ctx.sb
    uidA = ctx.uid = getattr(ctx, "uid", 0) + 1
    with contextlib.ExitStack() as astack:
        sb["ln_o1"] = astack.enter_context(nc.sbuf_tensor(f"ln_o1_{uidA}", [128, TL], BF16))
        sb["ln_o2"] = astack.enter_context(nc.sbuf_tensor(f"ln_o2_{uidA}", [128, TL], BF16))
        sb["masks_b"] = astack.enter_context(nc.sbuf_tensor(f"masks_{uidA}", [128, 8 * 512], BF16))
        pl = Plan()
        mt = dma(pl, "g", sb["masks_b"][:, :], ctx.dt["masks"][:, :], ctx.sem["d_g"])
        wait(pl, "g", ctx.sem["d_g"], mt)
        pl.run(nc)
        plan_l1a_kv(ctx)
        plan_l1a_main(ctx)
        phase_apply(ctx, first=False, last=is_last, need_stats=not is_last)
    del sb["ln_o1"], sb["ln_o2"], sb["masks_b"]


def load_l1w(ctx, pl, name, mc, wtile, slot):
    return dma(pl, "g", AP(wtile, slot * 2048, [[2 * 2048, 128], [128, 16], [1, 128]]),
               AP(ctx.dt[name], mc * 128, [[2048, 128], [128 * 2048, 16], [1, 128]]),
               ctx.sem[f"d_wt{slot}"])


def plan_l1a_kv(ctx):
    nc, sb, dt, S = ctx.nc, ctx.sb, ctx.dt, ctx.sem
    mm, vv, ss, gg = S["mm"], S["vv"], S["ss"], S["gg"]
    po = ctx.posems[1]
    uid = ctx.uid = getattr(ctx, "uid", 0) + 1
    with contextlib.ExitStack() as stack:
        def T(name, shape, dtype):
            return stack.enter_context(nc.sbuf_tensor(f"{name}_{uid}", shape, dtype))

        def P(name, shape, dtype=F32):
            return stack.enter_context(nc.psum_tensor(f"{name}_{uid}", shape, dtype))

        pl = Plan()
        xnr = T("l1k_xnr", [128, NCH * 512], BF16)
        xtmp = T("l1k_xt", [128, 512], F32)
        wtile = T("l1k_w", [128, 2 * 2048], BF16)
        klocal = T("l1k_kl", [128, 16 * 512], BF16)
        vlocal = T("l1k_vl", [128, 4 * 2048], BF16)
        cs_o = T("l1k_cs", [128, 512], F32)
        sn_o = T("l1k_sn", [128, 512], F32)
        ktmp = T("l1k_kt", [128, 512], BF16)
        rt_k = T("l1k_rt", [128, 512], BF16)
        psA = P("l1k_psA", [128, 512])
        psB = P("l1k_psB", [128, 512])
        psT = P("l1k_psT", [128, 512], BF16)

        tile_free = {0: 0, 1: 0}
        cs_used = None
        po_prev = None
        for tc in range(NTC):
            xt = plan_xnr(ctx, pl, xnr, xtmp, tc,
                          sb["ln_r"][:, tc * 512:(tc + 1) * 512],
                          sb["ln_a"][:, tc * 512:(tc + 1) * 512])
            if cs_used is not None:
                wait(pl, "y", vv, cs_used)
            dma(pl, "y", cs_o[:, :], dt["cos1o"][:, tc * 512:(tc + 1) * 512], S["d_cs"])
            cs_t = dma(pl, "y", sn_o[:, :], dt["sin1o"][:, tc * 512:(tc + 1) * 512], S["d_cs"])
            if po_prev is not None:
                wait(pl, "v", po, po_prev)
            wait(pl, "g", mm, tile_free[0])
            wt = load_l1w(ctx, pl, "L1_attn_WkT", 0, wtile, 0)
            kr = None
            for mc in range(NCH):
                slot = mc % 2
                if mc + 1 < NCH:
                    ns = (mc + 1) % 2
                    wait(pl, "g", mm, tile_free[ns])
                    nwt = load_l1w(ctx, pl, "L1_attn_WkT", mc + 1, wtile, ns)
                wait(pl, "t", S[f"d_wt{slot}"], wt)
                if mc == 0:
                    wait(pl, "t", vv, xt)
                for c in range(NCH - 1):
                    pl.t(lambda e, o=psA[:, :],
                         l=wtile[:, slot * 2048 + c * 128:slot * 2048 + (c + 1) * 128],
                         r=xnr[:, c * 512:(c + 1) * 512], a=(c == 0):
                         e.matmul(o, l, r, start=a, stop=False))
                pj = minc(pl, "t", mm, lambda e, o=psA[:, :],
                          l=wtile[:, slot * 2048 + 15 * 128:slot * 2048 + 16 * 128],
                          r=xnr[:, 15 * 512:16 * 512]: e.matmul(o, l, r, start=False, stop=True))
                tile_free[slot] = pj
                wait(pl, "s", mm, pj)
                if kr is not None:
                    wait(pl, "s", vv, kr)
                ke = minc(pl, "s", ss, lambda e, o=ktmp[:, :], i=psA[:, :]:
                          e.activation(o, i, AF.Identity))
                wait(pl, "t", ss, ke)
                pm = minc(pl, "t", mm, lambda e, o=psB[:, :], l=sb["perm1_b"][:, :],
                          r=ktmp[:, :]: e.matmul(o, l, r, start=True, stop=True))
                wait(pl, "v", mm, pm)
                if mc == 0:
                    wait(pl, "v", S["d_cs"], cs_t)
                pl.v(lambda e, o=rt_k[:, :], i=psB[:, :], j=sn_o[:, :]: e.tensor_mul(o, i, j))
                pl.v(lambda e, o=klocal[:, mc * 512:(mc + 1) * 512], i=ktmp[:, :], j=cs_o[:, :]:
                     e.tensor_mul(o, i, j))
                kr = minc(pl, "v", vv, lambda e, o=klocal[:, mc * 512:(mc + 1) * 512],
                          i=klocal[:, mc * 512:(mc + 1) * 512], j=rt_k[:, :]:
                          e.tensor_add(o, i, j))
                wait(pl, "t", vv, kr)
                if mc + 1 < NCH:
                    wt = nwt
            cs_used = kr
            wait(pl, "g", vv, kr)
            gdma_shared(ctx, pl, 1, dt["k_sh"], f"rb{tc}", tc,
                        [[TG, 128], [128 * TG, 16], [1, 512]],
                        AP(klocal, 0, [[16 * 512, 128], [512, 16], [1, 512]]))
            wait(pl, "g", mm, tile_free[0])
            wt = load_l1w(ctx, pl, "L1_attn_WvT", 0, wtile, 0)
            ve = None
            for mc in range(NCH):
                slot = mc % 2
                if mc + 1 < NCH:
                    ns = (mc + 1) % 2
                    wait(pl, "g", mm, tile_free[ns])
                    nwt = load_l1w(ctx, pl, "L1_attn_WvT", mc + 1, wtile, ns)
                wait(pl, "t", S[f"d_wt{slot}"], wt)
                for c in range(NCH - 1):
                    pl.t(lambda e, o=psA[:, :],
                         l=wtile[:, slot * 2048 + c * 128:slot * 2048 + (c + 1) * 128],
                         r=xnr[:, c * 512:(c + 1) * 512], a=(c == 0):
                         e.matmul(o, l, r, start=a, stop=False))
                pj = minc(pl, "t", mm, lambda e, o=psA[:, :],
                          l=wtile[:, slot * 2048 + 15 * 128:slot * 2048 + 16 * 128],
                          r=xnr[:, 15 * 512:16 * 512]: e.matmul(o, l, r, start=False, stop=True))
                tile_free[slot] = pj
                wait(pl, "s", mm, pj)
                vev = minc(pl, "s", ss, lambda e, o=ktmp[:, :], i=psA[:, :]:
                           e.activation(o, i, AF.Identity))
                wait(pl, "t", ss, vev)
                for st_i in range(3):
                    pl.t(lambda e, o=psT[:, st_i * 128:(st_i + 1) * 128],
                         i=ktmp[:, st_i * 128:(st_i + 1) * 128], d=sb["ident_b"][:, :]:
                         e.transpose(o, i, d))
                trt = minc(pl, "t", mm, lambda e, o=psT[:, 3 * 128:4 * 128],
                           i=ktmp[:, 3 * 128:4 * 128], d=sb["ident_b"][:, :]:
                           e.transpose(o, i, d))
                wait(pl, "v", mm, trt)
                ve = minc(pl, "v", vv, lambda e,
                          o=AP(vlocal, mc * 128, [[4 * 2048, 128], [2048, 4], [1, 128]]),
                          i=AP(psT, 0, [[512, 128], [128, 4], [1, 128]]):
                          e.tensor_copy(o, i))
                wait(pl, "t", vv, ve)
                wait(pl, "s", vv, ve)
                if mc + 1 < NCH:
                    wt = nwt
            wait(pl, "g", vv, ve)
            gdma_shared(ctx, pl, 1, dt["v_sh"], f"rr{tc}", 4 + tc,
                        [[D, 128], [128 * D, 4], [1, 2048]],
                        AP(vlocal, 0, [[4 * 2048, 128], [2048, 4], [1, 2048]]))
            po_prev = po.n
        publish_flag(ctx, pl, 1)
        poll_flag(ctx, pl, 1)
        pl.run(nc)


def plan_l1a_main(ctx):
    nc, sb, dt, S = ctx.nc, ctx.sb, ctx.dt, ctx.sem
    mm, vv, ss, gg = S["mm"], S["vv"], S["ss"], S["gg"]
    QC = 256
    NQC = TL // QC
    uid = ctx.uid = getattr(ctx, "uid", 0) + 1
    with contextlib.ExitStack() as stack:
        def T(name, shape, dtype):
            return stack.enter_context(nc.sbuf_tensor(f"{name}_{uid}", shape, dtype))

        def P(name, shape, dtype=F32):
            return stack.enter_context(nc.psum_tensor(f"{name}_{uid}", shape, dtype))

        pl = Plan()
        xtmp = T("l1m_xt", [128, QC], F32)
        wtile = T("l1m_w", [128, 2 * 2048], BF16)
        kblk = T("l1m_kb", [128, 2 * 16 * 256], BF16)
        vblk = T("l1m_vb", [128, 2 * 2 * 2048], BF16)
        kblk_addr = nc.lookup_mloc(kblk).addr
        xnr = nc.alloc_sbuf_tensor_at(f"l1m_xnr_{uid}", [128, NCH * QC], BF16,
                                      offset=kblk_addr)
        qrot_all = T("l1m_qr", [128, NHEAD * QC], BF16)
        qtmp = T("l1m_qt", [128, QC], BF16)
        rt_q = T("l1m_rtq", [128, QC], BF16)
        cs_o = T("l1m_cs", [128, QC], F32)
        sn_o = T("l1m_sn", [128, QC], F32)
        sq1 = T("l1m_sq1", [128, 2 * QC], BF16)
        sqm = T("l1m_sqm", [128, 2 * QC], BF16)
        pb = T("l1m_pb", [128, 2 * QC], BF16)
        y_acc = T("l1m_ya", [128, NHEAD * QC], F32)
        s_acc = T("l1m_sa", [128, NHEAD * QC], F32)
        s_acc_addr = nc.lookup_mloc(s_acc).addr
        ynb = nc.alloc_sbuf_tensor_at(f"l1m_yn_{uid}", [128, NHEAD * QC], BF16,
                                      offset=s_acc_addr)
        rcp = T("l1m_rcp", [128, QC], F32)
        ysb = T("l1m_ysb", [128, QC], F32)
        sub_t = T("l1m_sub", [128, 2 * QC], F32)
        sub_b = T("l1m_subb", [128, QC], BF16)
        sub_2 = T("l1m_sub2", [128, QC], BF16)
        psS = [P(f"l1m_psS{i}", [128, QC]) for i in range(2)]
        psY = P("l1m_psY", [128, QC])
        psSum = P("l1m_psSum", [128, QC])
        psA = P("l1m_psA", [128, QC])
        psB = P("l1m_psB", [128, QC])
        psO1 = P("l1m_psO1", [128, QC])
        psO2 = P("l1m_psO2", [128, QC])

        tile_free = {0: 0, 1: 0}
        kv_free = {0: 0, 1: 0}
        sp_free = {}
        prev_qr = None
        prev_stt = None
        for qc in range(NQC):
            KT = 8 * (qc // 2) + 8
            NSB = KT // 2
            xt = plan_xnr(ctx, pl, xnr, xtmp, qc,
                          sb["ln_r"][:, qc * QC:(qc + 1) * QC],
                          sb["ln_a"][:, qc * QC:(qc + 1) * QC], width=QC)
            if prev_qr is not None:
                wait(pl, "y", vv, prev_qr)
            dma(pl, "y", cs_o[:, :], dt["cos1o"][:, qc * QC:(qc + 1) * QC], S["d_cs"])
            cs_t = dma(pl, "y", sn_o[:, :], dt["sin1o"][:, qc * QC:(qc + 1) * QC], S["d_cs"])
            wait(pl, "g", mm, tile_free[0])
            wt = load_l1w(ctx, pl, "L1_attn_WqT", 0, wtile, 0)
            for h in range(NHEAD):
                slot = h % 2
                if h + 1 < NHEAD:
                    ns = (h + 1) % 2
                    wait(pl, "g", mm, tile_free[ns])
                    nwt = load_l1w(ctx, pl, "L1_attn_WqT", h + 1, wtile, ns)
                wait(pl, "t", S[f"d_wt{slot}"], wt)
                if h == 0:
                    wait(pl, "t", vv, xt)
                for c in range(NCH - 1):
                    pl.t(lambda e, o=psA[:, :],
                         l=wtile[:, slot * 2048 + c * 128:slot * 2048 + (c + 1) * 128],
                         r=xnr[:, c * QC:(c + 1) * QC], a=(c == 0):
                         e.matmul(o, l, r, start=a, stop=False))
                pj = minc(pl, "t", mm, lambda e, o=psA[:, :],
                          l=wtile[:, slot * 2048 + 15 * 128:slot * 2048 + 16 * 128],
                          r=xnr[:, 15 * QC:16 * QC]: e.matmul(o, l, r, start=False, stop=True))
                tile_free[slot] = pj
                wait(pl, "s", mm, pj)
                if prev_qr is not None:
                    wait(pl, "s", vv, prev_qr)
                qe = minc(pl, "s", ss, lambda e, o=qtmp[:, :], i=psA[:, :]:
                          e.activation(o, i, AF.Identity))
                wait(pl, "t", ss, qe)
                qpm = minc(pl, "t", mm, lambda e, o=psB[:, :], l=sb["perm1_b"][:, :],
                           r=qtmp[:, :]: e.matmul(o, l, r, start=True, stop=True))
                wait(pl, "v", mm, qpm)
                if h == 0:
                    wait(pl, "v", S["d_cs"], cs_t)
                pl.v(lambda e, o=rt_q[:, :], i=psB[:, :], j=sn_o[:, :]: e.tensor_mul(o, i, j))
                pl.v(lambda e, o=qrot_all[:, h * QC:(h + 1) * QC], i=qtmp[:, :], j=cs_o[:, :]:
                     e.tensor_mul(o, i, j))
                qr = minc(pl, "v", vv, lambda e, o=qrot_all[:, h * QC:(h + 1) * QC],
                          i=qrot_all[:, h * QC:(h + 1) * QC], j=rt_q[:, :]:
                          e.tensor_add(o, i, j))
                wait(pl, "t", vv, qr)
                prev_qr = qr
                if h + 1 < NHEAD:
                    wt = nwt
            qproj_done = tile_free[1]
            for sbk in range(NSB):
                k0 = qc // 2 * 0 + sbk * 256
                slot = sbk % 2
                wait(pl, "y", vv, kv_free[slot])
                if sbk < 2:
                    wait(pl, "y", mm, qproj_done)
                k_t = dma(pl, "y", AP(kblk, slot * (16 * 256),
                                      [[2 * 16 * 256, 128], [256, 16], [1, 256]]),
                          AP(dt["k_sh"], k0, [[TG, 128], [128 * TG, 16], [1, 256]]),
                          S[f"d_kv{slot}"])
                v_t = dma(pl, "y", AP(vblk, slot * (2 * 2048),
                                      [[2 * 2 * 2048, 128], [2048, 2], [1, 2048]]),
                          AP(dt["v_sh"], k0 * D, [[D, 128], [128 * D, 2], [1, 2048]]),
                          S[f"d_v{slot}"])
                wait(pl, "t", S[f"d_kv{slot}"], k_t)
                wait(pl, "t", S[f"d_v{slot}"], v_t)
                pa = None
                for h in range(NHEAD):
                    qk = None
                    for u in range(2):
                        fn = (lambda e, o=psS[u][:, :],
                              l=AP(kblk, slot * (16 * 256) + h * 256 + u * 128,
                                   [[2 * 16 * 256, 128], [1, 128]]),
                              r=qrot_all[:, h * QC:(h + 1) * QC]:
                              e.matmul(o, l, r, start=True, stop=True))
                        if u == 1:
                            qk = minc(pl, "t", mm, fn)
                        else:
                            pl.t(fn)
                    wait(pl, "s", mm, qk)
                    sqt = None
                    for u in range(2):
                        fn = (lambda e, o=sq1[:, u * QC:(u + 1) * QC], i=psS[u][:, :]:
                              e.activation(o, i, AF.Square))
                        if u == 1:
                            sqt = minc(pl, "s", ss, fn)
                        else:
                            pl.s(fn)
                    wait(pl, "v", ss, sqt)
                    pt = None
                    for u in range(2):
                        kt = sbk * 2 + u
                        w = kt - 8 * (qc // 2)
                        h2 = qc % 2
                        if w >= 0:
                            pl.v(lambda e, o=sqm[:, u * QC:(u + 1) * QC],
                                 i=sq1[:, u * QC:(u + 1) * QC],
                                 j=sb["masks_b"][:, w * 512 + h2 * 256:w * 512 + h2 * 256 + 256]:
                                 e.tensor_mul(o, i, j))
                            src2 = sqm
                        else:
                            src2 = sq1
                        pt = minc(pl, "v", vv, lambda e, o=pb[:, u * QC:(u + 1) * QC],
                                  i=sq1[:, u * QC:(u + 1) * QC],
                                  j=src2[:, u * QC:(u + 1) * QC]: e.tensor_mul(o, i, j))
                    wait(pl, "t", vv, pt)
                    av = None
                    for u in range(2):
                        pl.t(lambda e, o=psY[:, :],
                             l=AP(vblk, slot * (2 * 2048) + u * 2048 + h * 128,
                                  [[2 * 2 * 2048, 128], [1, 128]]),
                             r=pb[:, u * QC:(u + 1) * QC], a=(u == 0), b=(u == 1):
                             e.matmul(o, l, r, start=a, stop=b))
                        fn = (lambda e, o=psSum[:, :], l=sb["ones_b"][:, :],
                              r=pb[:, u * QC:(u + 1) * QC], a=(u == 0), b=(u == 1):
                              e.matmul(o, l, r, start=a, stop=b))
                        if u == 1:
                            av = minc(pl, "t", mm, fn)
                        else:
                            pl.t(fn)
                    wait(pl, "v", mm, av)
                    if sbk == 0:
                        pl.v(lambda e, o=y_acc[:, h * QC:(h + 1) * QC], i=psY[:, :]:
                             e.tensor_copy(o, i))
                        pa = minc(pl, "v", vv,
                                  lambda e, o=s_acc[:, h * QC:(h + 1) * QC], i=psSum[:, :]:
                                  e.tensor_copy(o, i))
                    else:
                        pl.v(lambda e, o=y_acc[:, h * QC:(h + 1) * QC],
                             i=y_acc[:, h * QC:(h + 1) * QC], j=psY[:, :]: e.tensor_add(o, i, j))
                        pa = minc(pl, "v", vv,
                                  lambda e, o=s_acc[:, h * QC:(h + 1) * QC],
                                  i=s_acc[:, h * QC:(h + 1) * QC], j=psSum[:, :]:
                                  e.tensor_add(o, i, j))
                    wait(pl, "t", vv, pa)
                    wait(pl, "s", vv, pa)
                kv_free[slot] = pa
            yv = None
            for h in range(NHEAD):
                pl.v(lambda e, o=rcp[:, :], i=s_acc[:, h * QC:(h + 1) * QC]:
                     e.tensor_scalar(o, i, 1e-6, None, ALU.max))
                pl.v(lambda e, o=ysb[:, :], i=rcp[:, :]: e.reciprocal(o, i))
                yv = minc(pl, "v", vv, lambda e, o=ynb[:, h * QC:(h + 1) * QC],
                          i=y_acc[:, h * QC:(h + 1) * QC], j=ysb[:, :]: e.tensor_mul(o, i, j))
            wait(pl, "t", vv, yv)
            wait(pl, "g", mm, tile_free[0])
            wp = load_l1w(ctx, pl, "L1_attn_WoT", 0, wtile, 0)
            for mc in range(NCH):
                slot = mc % 2
                if mc + 1 < NCH:
                    ns = (mc + 1) % 2
                    wait(pl, "g", mm, tile_free[ns])
                    nwp = load_l1w(ctx, pl, "L1_attn_WoT", mc + 1, wtile, ns)
                wait(pl, "t", S[f"d_wt{slot}"], wp)
                for hc in range(NHEAD - 1):
                    pl.t(lambda e, o=psA[:, :],
                         l=wtile[:, slot * 2048 + hc * 128:slot * 2048 + (hc + 1) * 128],
                         r=ynb[:, hc * QC:(hc + 1) * QC], a=(hc == 0):
                         e.matmul(o, l, r, start=a, stop=False))
                oj = minc(pl, "t", mm, lambda e, o=psA[:, :],
                          l=wtile[:, slot * 2048 + 15 * 128:slot * 2048 + 16 * 128],
                          r=ynb[:, 15 * QC:16 * QC]: e.matmul(o, l, r, start=False, stop=True))
                tile_free[slot] = oj
                wait(pl, "s", mm, oj)
                if slot in sp_free:
                    wait(pl, "s", S[f"d_sp{slot}"], sp_free[slot])
                se = minc(pl, "s", ss, lambda e, o=sub_t[:, slot * QC:(slot + 1) * QC],
                          i=psA[:, :]: e.activation(o, i, AF.Identity))
                wait(pl, "v", ss, se)
                if prev_stt is not None:
                    wait(pl, "v", mm, prev_stt)
                sbt = minc(pl, "v", vv, lambda e, o=sub_b[:, :],
                           i=sub_t[:, slot * QC:(slot + 1) * QC]: e.tensor_copy(o, i))
                if prev_stt is not None:
                    wait(pl, "s", mm, prev_stt)
                s2t = minc(pl, "s", ss, lambda e, o=sub_2[:, :],
                           i=sub_t[:, slot * QC:(slot + 1) * QC]: e.activation(o, i, AF.Square))
                wait(pl, "t", vv, sbt)
                wait(pl, "t", ss, s2t)
                pl.t(lambda e, o=psO1[:, :], l=sb["ones_b"][:, :], r=sub_b[:, :],
                     a=(mc == 0), b=(mc == NCH - 1): e.matmul(o, l, r, start=a, stop=b))
                stt = minc(pl, "t", mm, lambda e, o=psO2[:, :], l=sb["ones_b"][:, :],
                           r=sub_2[:, :], a=(mc == 0), b=(mc == NCH - 1):
                           e.matmul(o, l, r, start=a, stop=b))
                prev_stt = stt
                wait(pl, "y", ss, se)
                spw = dma(pl, "y", AP(dt["sub_spill"], mc * 128 * TL + qc * QC,
                                      [[TL, 128], [1, QC]]),
                          sub_t[:, slot * QC:(slot + 1) * QC], S[f"d_sp{slot}"])
                sp_free[slot] = spw
                if mc + 1 < NCH:
                    wp = nwp
            wait(pl, "v", mm, prev_stt)
            pl.v(lambda e, o=sb["ln_o1"][:, qc * QC:(qc + 1) * QC], i=psO1[:, :]:
                 e.tensor_copy(o, i))
            ot = minc(pl, "v", vv, lambda e, o=sb["ln_o2"][:, qc * QC:(qc + 1) * QC],
                      i=psO2[:, :]: e.tensor_copy(o, i))
            wait(pl, "t", vv, ot)
        pl.run(nc)



DFF = 8192
N_W = float(DFF * D)


def phase_l1_ffn(ctx, is_last):
    nc, sb = ctx.nc, ctx.sb
    uidF = ctx.uid = getattr(ctx, "uid", 0) + 1
    with contextlib.ExitStack() as fstack:
        sb["ln_o1"] = fstack.enter_context(nc.sbuf_tensor(f"ln_o1_{uidF}", [128, TL], F32))
        sb["ln_o2"] = fstack.enter_context(nc.sbuf_tensor(f"ln_o2_{uidF}", [128, TL], F32))
        plan_l1f_prep(ctx)
        plan_l1f_pass1(ctx)
        plan_l1f_pass2(ctx)
        phase_apply(ctx, first=False, last=is_last, need_stats=not is_last)
    del sb["ln_o1"], sb["ln_o2"]


def _l1f_pieces(wi):
    out = []
    if wi == 0:
        for pi in range(64):
            out.append((pi * 128 * D, "w1T_scratch", pi * 128, D))
    else:
        for pi in range(64):
            ri, pc = divmod(pi, 4)
            out.append((ri * 128 * DFF + pc * 2048, "w2T_scratch",
                        pc * 2048 * D + ri * 128, DFF))
    return out


def plan_l1f_prep(ctx):
    nc, sb, dt, S = ctx.nc, ctx.sb, ctx.dt, ctx.sem
    mm, vv, ss = S["mm"], S["vv"], S["ss"]
    uid = ctx.uid = getattr(ctx, "uid", 0) + 1
    with contextlib.ExitStack() as stack:
        def T(name, shape, dtype):
            return stack.enter_context(nc.sbuf_tensor(f"{name}_{uid}", shape, dtype))

        def P(name, shape, dtype=F32):
            return stack.enter_context(nc.psum_tensor(f"{name}_{uid}", shape, dtype))

        pl = Plan()
        wbuf = T("fp_wb", [128, 2 * 2048], BF16)
        absb = T("fp_ab", [128, 2 * 2048], BF16)
        wf = T("fp_wf", [128, 2 * 2048], F32)
        geb = T("fp_ge", [128, 2048], BF16)
        leb = T("fp_le", [128, 2048], BF16)
        gei = T("fp_gei", [128, 2048], BF16)
        lei = T("fp_lei", [128, 2048], BF16)
        ternb = T("fp_tn", [128, 2 * 2048], BF16)
        tb = T("fp_tb", [128, 2 * 2048], BF16)
        srow = T("fp_sr", [128, 4], F32)
        thr = T("fp_th", [128, 4], F32)
        eps_col = T("fp_eps", [128, 4], F32)
        psS = P("fp_psS", [128, 512])
        psT = P("fp_psT", [128, 512], BF16)

        gg = S["gg"]
        eps_m = minc(pl, "g", gg, lambda e, o=eps_col[:, :]: e.memset(o, 1e-8))
        abs_free = {}
        sum_free = {}
        red_prev = None
        for wi, nm in [(0, "L1_ffn_w1"), (1, "L1_ffn_w2")]:
            pieces = _l1f_pieces(wi)
            for pi, (soff, _, _, srcw) in enumerate(pieces):
                slot = pi % 2
                if slot in abs_free:
                    wait(pl, "g", ss, abs_free[slot])
                ld = dma(pl, "g", wbuf[:, slot * 2048:(slot + 1) * 2048],
                         AP(dt[nm], soff, [[srcw, 128], [1, 2048]]), S[f"d_we{slot}"])
                wait(pl, "s", S[f"d_we{slot}"], ld)
                if slot in sum_free:
                    wait(pl, "s", mm, sum_free[slot])
                ab = minc(pl, "s", ss, lambda e, o=absb[:, slot * 2048:(slot + 1) * 2048],
                          i=wbuf[:, slot * 2048:(slot + 1) * 2048]: e.activation(o, i, AF.Abs))
                abs_free[slot] = ab
                wait(pl, "t", ss, ab)
                if pi == 0 and red_prev is not None:
                    wait(pl, "t", vv, red_prev)
                for q in range(3):
                    pl.t(lambda e, o=psS[:, :], l=sb["ones_b"][:, :],
                         r=absb[:, slot * 2048 + q * 512:slot * 2048 + (q + 1) * 512],
                         a=(pi == 0 and q == 0):
                         e.matmul(o, l, r, start=a, stop=False))
                sm = minc(pl, "t", mm, lambda e, o=psS[:, :], l=sb["ones_b"][:, :],
                          r=absb[:, slot * 2048 + 3 * 512:slot * 2048 + 4 * 512],
                          b=(pi == len(pieces) - 1):
                          e.matmul(o, l, r, start=False, stop=b))
                sum_free[slot] = sm
            wait(pl, "v", mm, sm)
            red_prev = minc(pl, "v", vv, lambda e, o=srow[:, 0:1], i=psS[:, :]:
                            e.reduce_sum(o, i, axis=AX.X))
            wait(pl, "v", vv, red_prev)
            f1 = minc(pl, "v", vv, lambda e, o=srow[:, 1:2], i=srow[:, 0:1]:
                      e.tensor_scalar(o, i, 1.0 / N_W, None, ALU.mult))
            wait(pl, "v", vv, f1)
            if wi == 0:
                wait(pl, "v", gg, eps_m)
            f2 = minc(pl, "v", vv, lambda e, o=srow[:, 2:3], i=srow[:, 1:2],
                      j=eps_col[:, 0:1]: e.tensor_max(o, i, j))
            wait(pl, "v", vv, f2)
            pl.v(lambda e, o=thr[:, 2 * wi:2 * wi + 1], i=srow[:, 2:3]:
                 e.tensor_scalar(o, i, 0.5, None, ALU.mult))
            thr_m = minc(pl, "v", vv, lambda e, o=thr[:, 2 * wi + 1:2 * wi + 2],
                         i=srow[:, 2:3]: e.tensor_scalar(o, i, -0.5, None, ALU.mult))
            wait(pl, "v", vv, thr_m)

        wait(pl, "s", vv, thr_m)
        wf_free = {}
        tern_free = {}
        out_free = {}
        ev_free = {}
        tn_prev = None
        for wi, nm in [(0, "L1_ffn_w1"), (1, "L1_ffn_w2")]:
            for pi, (soff, dstn, doff, srcw) in enumerate(_l1f_pieces(wi)):
                slot = pi % 2
                if slot in wf_free:
                    wait(pl, "g", ss, wf_free[slot])
                ld = dma(pl, "g", wf[:, slot * 2048:(slot + 1) * 2048],
                         AP(dt[nm], soff, [[srcw, 128], [1, 2048]]), S[f"d_wt{slot}"])
                wait(pl, "s", S[f"d_wt{slot}"], ld)
                if tn_prev is not None:
                    wait(pl, "s", vv, tn_prev)
                pl.s(lambda e, o=gei[:, :], i=wf[:, slot * 2048:(slot + 1) * 2048],
                     b=thr[:, 2 * wi + 1:2 * wi + 2]:
                     e.activation(o, i, AF.Identity, bias=b))
                ac = minc(pl, "s", ss, lambda e, o=lei[:, :],
                          i=wf[:, slot * 2048:(slot + 1) * 2048],
                          b=thr[:, 2 * wi:2 * wi + 1]:
                          e.activation(o, i, AF.Identity, bias=b))
                wf_free[slot] = ac
                wait(pl, "v", ss, ac)
                if slot in tern_free:
                    wait(pl, "v", mm, tern_free[slot])
                pl.v(lambda e, o=geb[:, :], i=gei[:, :]:
                     e.tensor_scalar(o, i, 0.0, None, ALU.is_ge))
                pl.v(lambda e, o=leb[:, :], i=lei[:, :]:
                     e.tensor_scalar(o, i, 0.0, None, ALU.is_le))
                tn = minc(pl, "v", vv, lambda e, o=ternb[:, slot * 2048:(slot + 1) * 2048],
                          i=geb[:, :], j=leb[:, :]: e.tensor_sub(o, i, j))
                tn_prev = tn
                wait(pl, "t", vv, tn)
                if slot in out_free:
                    wait(pl, "s", S[f"d_wb{slot}"], out_free[slot])
                for grp in range(4):
                    if ev_free.get("last") is not None:
                        wait(pl, "t", ss, ev_free["last"])
                    for q in range(3):
                        c = grp * 4 + q
                        pl.t(lambda e, o=psT[:, q * 128:(q + 1) * 128],
                             i=ternb[:, slot * 2048 + c * 128:slot * 2048 + (c + 1) * 128],
                             d=sb["ident_b"][:, :]: e.transpose(o, i, d))
                    tr = minc(pl, "t", mm, lambda e, o=psT[:, 3 * 128:4 * 128],
                              i=ternb[:, slot * 2048 + (grp * 4 + 3) * 128:
                                      slot * 2048 + (grp * 4 + 4) * 128],
                              d=sb["ident_b"][:, :]: e.transpose(o, i, d))
                    wait(pl, "s", mm, tr)
                    ev = minc(pl, "s", ss, lambda e,
                              o=tb[:, slot * 2048 + grp * 512:slot * 2048 + (grp + 1) * 512],
                              i=psT[:, :]: e.activation(o, i, AF.Identity))
                    ev_free["last"] = ev
                tern_free[slot] = tr
                wait(pl, "y", ss, ev)
                ow = dma(pl, "y", AP(dt[dstn], doff, [[DFF if wi == 0 else D, 128],
                                                      [128 * (DFF if wi == 0 else D), 16],
                                                      [1, 128]]),
                         AP(tb, slot * 2048, [[2 * 2048, 128], [128, 16], [1, 128]]),
                         S[f"d_wb{slot}"])
                out_free[slot] = ow
        if "l1f_thr" in DBG_DRAM_OUT:
            dbg = nc.dram_tensor("dbg_thr", [128, 8], F32, kind="ExternalOutput")
            wait(pl, "g", vv, thr_m)
            dg1 = dma(pl, "g", AP(dbg, 0, [[8, 128], [1, 4]]), thr[:, :], S["d_g"])
            dg2 = dma(pl, "g", AP(dbg, 4, [[8, 128], [1, 4]]), srow[:, :], S["d_g"])
            wait(pl, "g", S["d_g"], dg2)
        wait(pl, "y", S["d_wb0"], S["d_wb0"].n)
        wait(pl, "y", S["d_wb1"], S["d_wb1"].n)
        pl.run(nc)


def plan_l1f_pass1(ctx):
    nc, sb, dt, S = ctx.nc, ctx.sb, ctx.dt, ctx.sem
    mm, vv, ss = S["mm"], S["vv"], S["ss"]
    uid = ctx.uid = getattr(ctx, "uid", 0) + 1
    with contextlib.ExitStack() as stack:
        def T(name, shape, dtype):
            return stack.enter_context(nc.sbuf_tensor(f"{name}_{uid}", shape, dtype))

        def P(name, shape, dtype=F32):
            return stack.enter_context(nc.psum_tensor(f"{name}_{uid}", shape, dtype))

        pl = Plan()
        xnr = T("f1_xnr", [128, NCH * 512], BF16)
        xtmp = T("f1_xt", [128, 512], F32)
        wtile = T("f1_w", [128, 2 * 2048], BF16)
        hg = T("f1_hg", [128, 2 * 512], BF16)
        b1c = T("f1_b1", [128, 64], F32)
        psA = [P(f"f1_psA{i}", [128, 512]) for i in range(2)]

        b_t = dma(pl, "g", b1c[:, :], dt["b1_c"][:, :], S["d_bias"])
        tile_free = {}
        ge_free = {}
        hw_free = {}
        last_pj = None
        for tc in range(NTC):
            if last_pj is not None:
                wait(pl, "v", mm, last_pj)
            xt = plan_xnr(ctx, pl, xnr, xtmp, tc,
                          sb["ln_r"][:, tc * 512:(tc + 1) * 512],
                          sb["ln_a"][:, tc * 512:(tc + 1) * 512])
            for fi in range(DFF // 128):
                slot = fi % 2
                if slot in tile_free:
                    wait(pl, "g", mm, tile_free[slot])
                wt = dma(pl, "g", AP(wtile, slot * 2048, [[2 * 2048, 128], [128, 16], [1, 128]]),
                         AP(dt["w1T_scratch"], fi * 128, [[DFF, 128], [128 * DFF, 16], [1, 128]]),
                         S[f"d_we{slot}"])
                wait(pl, "t", S[f"d_we{slot}"], wt)
                if fi == 0:
                    wait(pl, "t", vv, xt)
                if slot in ge_free:
                    wait(pl, "t", ss, ge_free[slot])
                for c in range(NCH - 1):
                    pl.t(lambda e, o=psA[slot][:, :],
                         l=wtile[:, slot * 2048 + c * 128:slot * 2048 + (c + 1) * 128],
                         r=xnr[:, c * 512:(c + 1) * 512], a=(c == 0):
                         e.matmul(o, l, r, start=a, stop=False))
                pj = minc(pl, "t", mm, lambda e, o=psA[slot][:, :],
                          l=wtile[:, slot * 2048 + 15 * 128:slot * 2048 + 16 * 128],
                          r=xnr[:, 15 * 512:16 * 512]: e.matmul(o, l, r, start=False, stop=True))
                tile_free[slot] = pj
                last_pj = pj
                wait(pl, "s", mm, pj)
                if tc == 0 and fi == 0:
                    wait(pl, "s", S["d_bias"], b_t)
                if slot in hw_free:
                    wait(pl, "s", S[f"d_sp{slot}"], hw_free[slot])
                ge = minc(pl, "s", ss, lambda e, o=hg[:, slot * 512:(slot + 1) * 512],
                          i=psA[slot][:, :], b=b1c[:, fi:fi + 1]:
                          e.activation(o, i, AF.Gelu, bias=b))
                ge_free[slot] = ge
                wait(pl, "y", ss, ge)
                hw = dma(pl, "y", AP(dt["h_dram"], fi * 128 * TL + tc * 512,
                                     [[TL, 128], [1, 512]]),
                         hg[:, slot * 512:(slot + 1) * 512], S[f"d_sp{slot}"])
                hw_free[slot] = hw
        wait(pl, "y", S["d_sp0"], S["d_sp0"].n)
        wait(pl, "y", S["d_sp1"], S["d_sp1"].n)
        pl.run(nc)


def plan_l1f_pass2(ctx):
    nc, sb, dt, S = ctx.nc, ctx.sb, ctx.dt, ctx.sem
    mm, vv, ss = S["mm"], S["vv"], S["ss"]
    uid = ctx.uid = getattr(ctx, "uid", 0) + 1
    with contextlib.ExitStack() as stack:
        def T(name, shape, dtype):
            return stack.enter_context(nc.sbuf_tensor(f"{name}_{uid}", shape, dtype))

        def P(name, shape, dtype=F32):
            return stack.enter_context(nc.psum_tensor(f"{name}_{uid}", shape, dtype))

        pl = Plan()
        xsb_addr = nc.lookup_mloc(sb["xsb"]).addr
        hblk = nc.alloc_sbuf_tensor_at(f"f2_h_{uid}", [128, 64 * 512], BF16,
                                       offset=xsb_addr)
        wt2 = T("f2_w", [128, 2 * 64 * 128], BF16)
        sub_t = T("f2_sub", [128, 2 * 512], F32)
        sub_b = T("f2_subb", [128, 512], BF16)
        sub_2 = T("f2_sub2", [128, 512], BF16)
        b2c = T("f2_b2", [128, 16], F32)
        psA = [P(f"f2_psA{i}", [128, 512]) for i in range(2)]
        psO1 = P("f2_psO1", [128, 512])
        psO2 = P("f2_psO2", [128, 512])

        b_t = dma(pl, "g", b2c[:, :], dt["b2_c"][:, :], S["d_bias"])
        tile_free = {}
        se_free = {}
        sp_free = {}
        prev_stt = None
        oj_last = None
        for tc in range(NTC):
            if oj_last is not None:
                wait(pl, "y", mm, oj_last)
            hld = dma(pl, "y", hblk[:, :],
                      AP(dt["h_dram"], tc * 512, [[TL, 128], [128 * TL, 64], [1, 512]]),
                      S["d_kf"])
            for mc in range(NCH):
                slot = mc % 2
                if slot in tile_free:
                    wait(pl, "g", mm, tile_free[slot])
                wt = dma(pl, "g", AP(wt2, slot * (64 * 128),
                                     [[2 * 64 * 128, 128], [128, 64], [1, 128]]),
                         AP(dt["w2T_scratch"], mc * 128, [[D, 128], [128 * D, 64], [1, 128]]),
                         S[f"d_we{slot}"])
                wait(pl, "t", S[f"d_we{slot}"], wt)
                if mc == 0:
                    wait(pl, "t", S["d_kf"], hld)
                if slot in se_free:
                    wait(pl, "t", ss, se_free[slot])
                for fc in range(63):
                    pl.t(lambda e, o=psA[slot][:, :],
                         l=wt2[:, slot * (64 * 128) + fc * 128:slot * (64 * 128) + (fc + 1) * 128],
                         r=hblk[:, fc * 512:(fc + 1) * 512], a=(fc == 0):
                         e.matmul(o, l, r, start=a, stop=False))
                oj = minc(pl, "t", mm, lambda e, o=psA[slot][:, :],
                          l=wt2[:, slot * (64 * 128) + 63 * 128:slot * (64 * 128) + 64 * 128],
                          r=hblk[:, 63 * 512:64 * 512]: e.matmul(o, l, r, start=False, stop=True))
                tile_free[slot] = oj
                oj_last = oj
                wait(pl, "s", mm, oj)
                if tc == 0 and mc == 0:
                    wait(pl, "s", S["d_bias"], b_t)
                if slot in sp_free:
                    wait(pl, "s", S[f"d_sp{slot}"], sp_free[slot])
                se = minc(pl, "s", ss, lambda e, o=sub_t[:, slot * 512:(slot + 1) * 512],
                          i=psA[slot][:, :], b=b2c[:, mc:mc + 1]:
                          e.activation(o, i, AF.Identity, bias=b))
                se_free[slot] = se
                wait(pl, "v", ss, se)
                if prev_stt is not None:
                    wait(pl, "v", mm, prev_stt)
                sbt = minc(pl, "v", vv, lambda e, o=sub_b[:, :],
                           i=sub_t[:, slot * 512:(slot + 1) * 512]: e.tensor_copy(o, i))
                if prev_stt is not None:
                    wait(pl, "s", mm, prev_stt)
                s2t = minc(pl, "s", ss, lambda e, o=sub_2[:, :],
                           i=sub_t[:, slot * 512:(slot + 1) * 512]: e.activation(o, i, AF.Square))
                wait(pl, "t", vv, sbt)
                wait(pl, "t", ss, s2t)
                pl.t(lambda e, o=psO1[:, :], l=sb["ones_b"][:, :], r=sub_b[:, :],
                     a=(mc == 0), b=(mc == NCH - 1): e.matmul(o, l, r, start=a, stop=b))
                stt = minc(pl, "t", mm, lambda e, o=psO2[:, :], l=sb["ones_b"][:, :],
                           r=sub_2[:, :], a=(mc == 0), b=(mc == NCH - 1):
                           e.matmul(o, l, r, start=a, stop=b))
                prev_stt = stt
                wait(pl, "y", ss, se)
                spw = dma(pl, "y", AP(dt["sub_spill"], mc * 128 * TL + tc * 512,
                                      [[TL, 128], [1, 512]]),
                          sub_t[:, slot * 512:(slot + 1) * 512], S[f"d_sp{slot}"])
                sp_free[slot] = spw
            wait(pl, "v", mm, prev_stt)
            pl.v(lambda e, o=sb["ln_o1"][:, tc * 512:(tc + 1) * 512], i=psO1[:, :]:
                 e.tensor_copy(o, i))
            ot = minc(pl, "v", vv, lambda e, o=sb["ln_o2"][:, tc * 512:(tc + 1) * 512],
                      i=psO2[:, :]: e.tensor_copy(o, i))
            wait(pl, "t", vv, ot)
        pl.run(nc)



PHASES = {
    "l0a": phase_l0_attn,
    "l0f": phase_l0_ffn,
    "l1a": phase_l1_attn,
    "l1f": phase_l1_ffn,
}


def phase_debug_dump(ctx, items):
    pl = Plan()
    S = ctx.sem
    dx = S["d_x"]
    for name, hnd, shp in items:
        t = ctx.nc.dram_tensor("dbg_" + name, shp, F32, kind="ExternalOutput")
        tb = ctx.nc.dram_tensor("dbgi_" + name, shp,
                                hnd.dtype if hasattr(hnd, "dtype") else F32)
        dma(pl, "g", t.ap(), hnd[:shp[0], :shp[1]], dx)
    wait(pl, "g", dx, dx.n)
    pl.run(ctx.nc)


def build_program(sub_blocks):
    ctx = Ctx()
    nc = bass.Bass("TRN2", target_bir_lowering=False, num_devices=8)
    ctx.nc = nc
    ctx.sub_blocks = sub_blocks
    declare_dram(ctx)
    with contextlib.ExitStack() as stack:
        alloc_persistent(ctx, stack)
        phase_init(ctx)
        phase_apply(ctx, first=True, last=False, need_stats=True)
        for i, sbk in enumerate(sub_blocks):
            PHASES[sbk](ctx, is_last=(i == len(sub_blocks) - 1))
    return ctx


def kernel(**inputs):
    in_maps = prepare_inputs(inputs)
    launches = LAUNCHES if LAUNCHES is not None else [list(SUBBLOCKS)]
    cur = in_maps
    res = None
    for li, grp in enumerate(launches):
        ctx = build_program(grp)
        res = run_bass_kernel_spmd(ctx.nc, cur, core_ids=list(range(8)))
        if li + 1 < len(launches):
            cur = [dict(m, xT=np.ascontiguousarray(res.results[c]["outT"]))
                   for c, m in enumerate(in_maps)]
    return assemble_output(res.results)


# version 03
# speedup vs baseline: 1.0525x; 1.0525x over previous
import contextlib

import numpy as np
import concourse.bass as bass
import concourse.mybir as mybir
from concourse.bass_utils import run_bass_kernel_spmd

F32 = mybir.dt.float32
BF16 = mybir.dt.bfloat16
I32 = mybir.dt.int32
AF = mybir.ActivationFunctionType
ALU = mybir.AluOpType
AX = mybir.AxisListType
AP = bass.AP
RV = bass.RuntimeValue

D = 2048
TL = 2048
TG = 4096
NCH = 16
NTC = 4
NHEAD = 16
SCALE0 = float(1.0 / np.sqrt(128.0))

ENGS = ("t", "v", "s", "g", "y")

SUBBLOCKS = ["l0a", "l0f", "l1a", "l1f"]
LAUNCHES = None
MOE_EXPERTS = 9
DBG_DRAM_OUT = []
MOE_SKIP_DOWN = False
MOE_SKIP_GU = False


class Plan:
    def __init__(self):
        self.q = {e: [] for e in ENGS}

    def t(self, fn): self.q["t"].append(fn)
    def v(self, fn): self.q["v"].append(fn)
    def s(self, fn): self.q["s"].append(fn)
    def g(self, fn): self.q["g"].append(fn)
    def y(self, fn): self.q["y"].append(fn)

    def run(self, nc):
        with nc.Block() as blk:
            if self.q["g"]:
                @blk.gpsimd
                def _(eng):
                    for fn in self.q["g"]:
                        fn(eng)
            if self.q["t"]:
                @blk.tensor
                def _(eng):
                    for fn in self.q["t"]:
                        fn(eng)
            if self.q["v"]:
                @blk.vector
                def _(eng):
                    for fn in self.q["v"]:
                        fn(eng)
            if self.q["s"]:
                @blk.scalar
                def _(eng):
                    for fn in self.q["s"]:
                        fn(eng)
            if self.q["y"]:
                @blk.sync
                def _(eng):
                    for fn in self.q["y"]:
                        fn(eng)


class Sem:
    def __init__(self, h):
        self.h = h
        self.n = 0

    def inc(self, k=1):
        self.n += k
        return self.n


def dma(pl, eng, out_ap, in_ap, sem, **kw):
    tgt = sem.inc(16)
    h = sem.h
    getattr(pl, eng)(lambda e, o=out_ap, i=in_ap, k=dict(kw):
                     e.dma_start(out=o, in_=i, **k).then_inc(h, 16))
    return tgt


def wait(pl, eng, sem, tgt):
    h = sem.h
    getattr(pl, eng)(lambda e, ht=h, tg=tgt: e.wait_ge(ht, tg))


def tinc(pl, eng, sem, k=1):
    tgt = sem.inc(k)
    h = sem.h
    getattr(pl, eng)(lambda e, ht=h, kk=k: e.nop().then_inc(ht, kk))
    return tgt


def minc(pl, eng, sem, fn):
    tgt = sem.inc(1)
    h = sem.h
    getattr(pl, eng)(lambda e, f=fn, hh=h: f(e).then_inc(hh, 1))
    return tgt



def _rope_tables_L0():
    half = 64
    inv = 10000.0 ** (-np.arange(half, dtype=np.float64) / half)
    ang = np.arange(TG, dtype=np.float64)[None, :] * inv[:, None]
    return (np.repeat(np.cos(ang), 2, axis=0).astype(np.float32),
            np.repeat(np.sin(ang), 2, axis=0).astype(np.float32))


def _rope_tables_L1():
    half = 64
    inv = 10000.0 ** (-np.arange(half, dtype=np.float64) * 2.0 / 128.0)
    ang = np.arange(TG, dtype=np.float64)[None, :] * inv[:, None]
    cos = np.cos(ang).astype(np.float32)
    sin = np.sin(ang).astype(np.float32)
    return np.concatenate([cos, cos], 0), np.concatenate([sin, sin], 0)


def _perm_L0():
    R = np.zeros((128, 128), np.float32)
    for i in range(64):
        R[2 * i, 2 * i + 1] = -1.0
        R[2 * i + 1, 2 * i] = 1.0
    return np.ascontiguousarray(R.T)


def _perm_L1():
    R = np.zeros((128, 128), np.float32)
    for i in range(64):
        R[i, i + 64] = -1.0
        R[i + 64, i] = 1.0
    return np.ascontiguousarray(R.T)


def _masks(parity):
    k = np.arange(128)[:, None]
    q = np.arange(512)[None, :]
    return np.concatenate(
        [((512 * parity + q) >= (128 * w + k)).astype(np.float32) for w in range(8)],
        axis=1)


def _bias_cols(b):
    b = np.asarray(b, np.float32).reshape(-1, 128)
    return np.ascontiguousarray(b.T)


def prepare_inputs(inputs):
    x = np.asarray(inputs["L0_attn_x"], dtype=np.float32)
    weights = {k: np.ascontiguousarray(np.asarray(v, dtype=np.float32))
               for k, v in inputs.items()
               if k != "L0_attn_x" and not k.endswith(("b_latent", "b_d", "b_proj", "b1", "b2"))}
    cos0, sin0 = _rope_tables_L0()
    cos1, sin1 = _rope_tables_L1()
    consts = {
        "cos0": cos0, "sin0": sin0, "cos1": cos1, "sin1": sin1,
        "ident": np.eye(128, dtype=np.float32),
        "perm0": _perm_L0(), "perm1": _perm_L1(),
        "b_latent_c": _bias_cols(inputs["L0_attn_b_latent"]),
        "b_d_c": _bias_cols(inputs["L0_attn_b_d"]),
        "b_proj_c": _bias_cols(inputs["L0_attn_b_proj"]),
        "b1_c": _bias_cols(inputs["L1_ffn_b1"]),
        "b2_c": _bias_cols(inputs["L1_ffn_b2"]),
        "L1_attn_WqT": np.ascontiguousarray(np.asarray(inputs["L1_attn_Wq"], np.float32).T),
        "L1_attn_WkT": np.ascontiguousarray(np.asarray(inputs["L1_attn_Wk"], np.float32).T),
        "L1_attn_WvT": np.ascontiguousarray(np.asarray(inputs["L1_attn_Wv"], np.float32).T),
        "L1_attn_WoT": np.ascontiguousarray(np.asarray(inputs["L1_attn_Wo"], np.float32).T),
    }
    nonce = np.random.randint(1, 2**31 - 2, size=(1, 32)).astype(np.int32)
    in_maps = []
    for c in range(8):
        pair, parity = divmod(c, 2)
        blocks = [2 * j + parity for j in range(4)]
        tok = np.concatenate([np.arange(512 * g, 512 * g + 512) for g in blocks])
        meta = np.zeros((1, 32), np.int32)
        for j in range(4):
            meta[0, j] = blocks[j] * 512
            meta[0, 4 + j] = blocks[j] * 512 * D
        meta[0, 8] = parity * 32
        meta[0, 9] = (1 - parity) * 32
        m = {
            "xT": np.ascontiguousarray(x[pair][tok].T),
            "cos0o": np.ascontiguousarray(cos0[:, tok]),
            "sin0o": np.ascontiguousarray(sin0[:, tok]),
            "cos1o": np.ascontiguousarray(cos1[:, tok]),
            "sin1o": np.ascontiguousarray(sin1[:, tok]),
            "masks": _masks(parity),
            "meta": meta,
            "nonce": nonce,
        }
        m.update(consts)
        m.update(weights)
        in_maps.append(m)
    return in_maps


def assemble_output(results):
    out = np.empty((4, 4096, 2048), np.float32)
    for c in range(8):
        pair, parity = divmod(c, 2)
        o = results[c]["outT"].T
        for j in range(4):
            g = 2 * j + parity
            out[pair, 512 * g:512 * g + 512, :] = o[512 * j:512 * j + 512, :]
    return out



class Ctx:
    pass


class LazyDT(dict):
    def __init__(self, nc, shapes):
        super().__init__()
        self.nc = nc
        self.shapes = dict(shapes)

    def __missing__(self, k):
        shp = self.shapes[k]
        h = self.nc.dram_tensor(k, shp, F32, kind="ExternalInput")
        self[k] = h
        return h


def declare_dram(ctx):
    nc = ctx.nc
    shapes = [
        ("L0_attn_W_latent", [D, 128]),
        ("L0_attn_W_d", [D, D]),
        ("L0_attn_W_proj", [D, D]),
        ("L0_ffn_gate_w", [D, 8]),
        ("L0_ffn_w_gate", [8, D, 1408]), ("L0_ffn_w_up", [8, D, 1408]),
        ("L0_ffn_w_down", [8, 1408, D]),
        ("L0_ffn_sw_gate", [D, 2816]), ("L0_ffn_sw_up", [D, 2816]),
        ("L0_ffn_sw_down", [2816, D]),
        ("L1_attn_Wq", [D, D]), ("L1_attn_Wk", [D, D]),
        ("L1_attn_Wv", [D, D]), ("L1_attn_Wo", [D, D]),
        ("L1_attn_WqT", [D, D]), ("L1_attn_WkT", [D, D]),
        ("L1_attn_WvT", [D, D]), ("L1_attn_WoT", [D, D]),
        ("L1_ffn_w1", [8192, D]),
        ("L1_ffn_w2", [D, 8192]),
        ("cos0", [128, TG]), ("sin0", [128, TG]),
        ("cos1", [128, TG]), ("sin1", [128, TG]),
        ("cos0o", [128, TL]), ("sin0o", [128, TL]),
        ("cos1o", [128, TL]), ("sin1o", [128, TL]),
        ("ident", [128, 128]), ("perm0", [128, 128]), ("perm1", [128, 128]),
        ("masks", [128, 8 * 512]),
        ("b_latent_c", [128, 1]), ("b_d_c", [128, 16]), ("b_proj_c", [128, 16]),
        ("b1_c", [128, 64]), ("b2_c", [128, 16]),
    ]
    t = LazyDT(nc, shapes)
    t["xT"] = nc.dram_tensor("xT", [D, TL], F32, kind="ExternalInput")
    t["outT"] = nc.dram_tensor("outT", [D, TL], F32, kind="ExternalOutput")
    t["meta"] = nc.dram_tensor("meta", [1, 32], I32, kind="ExternalInput")
    t["nonce"] = nc.dram_tensor("nonce", [1, 32], I32, kind="ExternalInput")
    def internal(name, shape, dtype):
        kind = "ExternalOutput" if name in DBG_DRAM_OUT else "Internal"
        return nc.dram_tensor(name, shape, dtype, kind=kind)
    t["master"] = internal("master", [D, TL], F32)
    t["sub_spill"] = internal("sub_spill", [D, TL], F32)
    t["w1T_scratch"] = internal("w1T_scratch", [D, 8192], BF16)
    t["w2T_scratch"] = internal("w2T_scratch", [8192, D], BF16)
    t["wT_scratch"] = internal("wT_scratch", [4 * D, D], BF16)
    t["comb_dram"] = internal("comb_dram", [8, TL], F32)
    t["h_dram"] = internal("h_dram", [8192, TL], BF16)
    t["kv_sh"] = nc.dram_tensor("kv_sh", [128, TG], BF16, addr_space="Shared")
    t["k_sh"] = nc.dram_tensor("k_sh", [D, TG], BF16, addr_space="Shared")
    t["v_sh"] = nc.dram_tensor("v_sh", [TG, D], BF16, addr_space="Shared")
    t["flags"] = nc.dram_tensor("flags", [2, 32], I32, addr_space="Shared")
    ctx.dt = t


def alloc_persistent(ctx, stack):
    nc = ctx.nc
    sb = {}

    def S(name, shape, dtype):
        sb[name] = stack.enter_context(nc.sbuf_tensor(name, shape, dtype))

    S("xsb", [128, NCH * TL], BF16)
    S("ones_b", [128, 128], BF16)
    S("onescol_f", [1, 128], F32)
    S("ident_b", [128, 128], BF16)
    S("perm0_b", [128, 128], BF16)
    S("perm1_b", [128, 128], BF16)
    S("meta_sb", [1, 32], I32)
    S("nonce_sb", [1, 32], I32)
    S("flagrd", [1, 4], I32)
    S("ln_r", [128, TL], F32)
    S("ln_a", [128, TL], F32)
    S("row_a", [128, 512], F32)
    S("row_b", [128, 512], F32)
    S("row_c", [128, 512], F32)
    S("row_d", [128, 512], F32)
    S("row_e", [128, 512], F32)
    S("row_f", [128, 512], F32)
    ctx.sb = sb
    ctx.sem = {}
    for nm in ["mm", "vv", "ss", "gg",
               "d_mst0", "d_mst1", "d_sub0", "d_sub1", "d_wb0", "d_wb1",
               "d_wt0", "d_wt1", "d_cs", "d_kf", "d_wl", "d_bl", "d_bias",
               "d_sp0", "d_sp1", "d_x", "d_g", "d_we0", "d_we1",
               "d_kv0", "d_kv1", "d_v0", "d_v1", "d_acc"]:
        h = stack.enter_context(nc.semaphore(nm))
        ctx.sem[nm] = Sem(h)
    ctx.posems = [Sem(stack.enter_context(nc.semaphore(f"po{i}"))) for i in range(4)]
    ctx.pqsems = [stack.enter_context(nc.semaphore(f"pq{i}")) for i in range(4)]
    g = nc.gpsimd
    ctx.rg = {nm: g.alloc_register(nm) for nm in
              ["rb0", "rb1", "rb2", "rb3", "rr0", "rr1", "rr2", "rr3",
               "rpf0", "rpf1", "rpf2", "rpf3", "rpo0", "rpo1", "rpo2", "rpo3",
               "rtmp", "rtgt", "rw0", "rw1", "re0", "re1", "rn0", "rn1"]}


def phase_init(ctx):
    pl = Plan()
    sb, dt, S = ctx.sb, ctx.dt, ctx.sem
    dg = S["d_g"]
    for nm_s, nm_d in [("ident_b", "ident"), ("perm0_b", "perm0"),
                       ("perm1_b", "perm1")]:
        dma(pl, "g", sb[nm_s][:, :], dt[nm_d][:, :], dg)
    dma(pl, "g", sb["meta_sb"][:, :], dt["meta"][:, :], dg)
    dma(pl, "g", sb["nonce_sb"][:, :], dt["nonce"][:, :], dg)
    pl.g(lambda e: e.memset(sb["ones_b"][:, :], 1.0))
    pl.g(lambda e: e.memset(sb["onescol_f"][:, :], 1.0))
    wait(pl, "g", dg, dg.n)
    pl.run(ctx.nc)



def plan_scalars_slice(ctx, pl, s1_ap, s2_ap, r_ap, a_ap, n, eps=1e-5):
    sb, S = ctx.sb, ctx.sem
    vv, ss = S["vv"], S["ss"]
    ra, rb, rc, rd = sb["row_a"], sb["row_b"], sb["row_c"], sb["row_d"]
    inv_n = 1.0 / n
    pl.v(lambda e, o=ra[:, :], i=s1_ap: e.tensor_scalar(o, i, inv_n, None, ALU.mult))
    pl.v(lambda e, o=rb[:, :], i=s2_ap: e.tensor_scalar(o, i, inv_n, None, ALU.mult))
    pl.v(lambda e, o=rc[:, :], i=ra[:, :], j=ra[:, :]: e.tensor_mul(o, i, j))
    pl.v(lambda e, o=rc[:, :], i=rb[:, :], j=rc[:, :]: e.tensor_sub(o, i, j))
    vt = minc(pl, "v", vv, lambda e, o=rc[:, :], i=rc[:, :]:
              e.tensor_scalar(o, i, eps, None, ALU.add))
    wait(pl, "s", vv, vt)
    st = minc(pl, "s", ss, lambda e, o=rb[:, :], i=rc[:, :]: e.activation(o, i, AF.Sqrt))
    wait(pl, "v", ss, st)
    pl.v(lambda e, o=rd[:, :], i=rb[:, :]: e.reciprocal(o, i))
    pl.v(lambda e, o=rb[:, :], i=rd[:, :], j=rd[:, :]: e.tensor_mul(o, i, j))
    pl.v(lambda e, o=rb[:, :], i=rb[:, :], j=rc[:, :]: e.tensor_mul(o, i, j))
    pl.v(lambda e, o=rb[:, :], i=rb[:, :]: e.tensor_scalar(o, i, -0.5, 1.5, ALU.mult, ALU.add))
    pl.v(lambda e, o=r_ap, i=rd[:, :], j=rb[:, :]: e.tensor_mul(o, i, j))
    pl.v(lambda e, o=ra[:, :], i=ra[:, :]: e.tensor_scalar(o, i, -1.0, None, ALU.mult))
    return minc(pl, "v", vv, lambda e, o=a_ap, i=ra[:, :], j=r_ap: e.tensor_mul(o, i, j))


def bc_row(ctx, pl, psum_ap, row_ap, sem=None):
    sb = ctx.sb
    if sem is None:
        pl.t(lambda e, o=psum_ap, l=sb["onescol_f"][0:1, :], r=row_ap:
             e.matmul(o, l, r, start=True, stop=True))
        return None
    return minc(pl, "t", sem, lambda e, o=psum_ap, l=sb["onescol_f"][0:1, :], r=row_ap:
                e.matmul(o, l, r, start=True, stop=True))


def plan_xnr(ctx, pl, xnr, xtmp, tc, psB, psA_, width=512, full=False):
    sb, S = ctx.sb, ctx.sem
    tgt = None
    for c in range(NCH):
        xc = sb["xsb"][:, c * TL + tc * width:c * TL + (tc + 1) * width]
        oc = (c * TL + tc * width) if full else (c * width)
        pl.v(lambda e, o=xtmp[:, :], i=xc, j=psB: e.tensor_mul(o, i, j))
        if c < NCH - 1:
            pl.v(lambda e, o=xnr[:, oc:oc + width], i=xtmp[:, :], j=psA_:
                 e.tensor_add(o, i, j))
        else:
            tgt = minc(pl, "v", S["vv"],
                       lambda e, o=xnr[:, oc:oc + width], i=xtmp[:, :], j=psA_:
                       e.tensor_add(o, i, j))
    return tgt



def publish_flag(ctx, pl, xpoint):
    sb, dt, rg = ctx.sb, ctx.dt, ctx.rg
    po = ctx.posems[xpoint]
    wait(pl, "g", po, po.n)
    ft = po.inc(16)

    def f(e, rg=rg, k=xpoint, poh=po.h, ms=sb["meta_sb"]):
        r = rg[f"rpf{k}"]
        e.reg_load(r, ms[0:1, 8:9])
        e.reg_add(r, r, 4 * k)
        e.dma_start(out=AP(dt["flags"], RV(r), [[4, 1], [1, 4]]),
                    in_=sb["nonce_sb"][0:1, 4 * k:4 * k + 4]).then_inc(poh, 16)
    pl.g(f)
    wait(pl, "g", po, ft)


def poll_flag(ctx, pl, xpoint):
    sb, dt, rg = ctx.sb, ctx.dt, ctx.rg
    po = ctx.posems[xpoint]

    pq = ctx.pqsems[xpoint]

    def f(e, rg=rg, k=xpoint, pq=pq, ms=sb["meta_sb"]):
        r = rg[f"rpo{k}"]
        e.reg_load(rg["rn0"], sb["nonce_sb"][0:1, 4 * k:4 * k + 1])
        e.reg_load(rg["rn1"], sb["nonce_sb"][0:1, 4 * k + 1:4 * k + 2])
        e.reg_load(r, ms[0:1, 9:10])
        e.reg_add(r, r, 4 * k)
        e.reg_mov(rg["re0"], 0)
        e.reg_mov(rg["rtgt"], 0)
        with e.While(RV(rg["re0"]) != 2):
            e.dma_start(out=sb["flagrd"][0:1, 0:4],
                        in_=AP(dt["flags"], RV(r), [[4, 1], [1, 4]])).then_inc(pq, 16)
            e.reg_add(rg["rtgt"], rg["rtgt"], 16)
            e.wait_ge(pq, RV(rg["rtgt"]))
            e.reg_load(rg["rw0"], sb["flagrd"][0:1, 0:1])
            e.reg_load(rg["rw1"], sb["flagrd"][0:1, 1:2])
            e.reg_alu(rg["re0"], rg["rw0"], RV(rg["rn0"]), ALU.is_equal)
            e.reg_alu(rg["re1"], rg["rw1"], RV(rg["rn1"]), ALU.is_equal)
            e.reg_alu(rg["re0"], rg["re0"], RV(rg["re1"]), ALU.add)
    pl.g(f)


def gdma_shared(ctx, pl, xpoint, dst_tensor, reg_name, meta_col, ap_dims, src_ap):
    rg, sb = ctx.rg, ctx.sb
    po = ctx.posems[xpoint]
    po.inc(16)

    def f(e, rg=rg, dims=ap_dims, src=src_ap, dstt=dst_tensor,
          col=meta_col, rn=reg_name, poh=po.h, ms=sb["meta_sb"]):
        e.reg_load(rg[rn], ms[0:1, col:col + 1])
        e.dma_start(out=AP(dstt, RV(rg[rn]), dims), in_=src).then_inc(poh, 16)
    pl.g(f)


def gwait_po(ctx, pl, xpoint):
    po = ctx.posems[xpoint]
    wait(pl, "g", po, po.n)



def phase_apply(ctx, first=False, last=False, need_stats=True, stats_from_sub=False):
    nc, sb, dt, S = ctx.nc, ctx.sb, ctx.dt, ctx.sem
    mm, vv, ss = S["mm"], S["vv"], S["ss"]
    uid = ctx.uid = getattr(ctx, "uid", 0) + 1
    with contextlib.ExitStack() as stack:
        def T(name, shape, dtype):
            return stack.enter_context(nc.sbuf_tensor(f"{name}_{uid}", shape, dtype))

        def P(name, shape, dtype=F32):
            return stack.enter_context(nc.psum_tensor(f"{name}_{uid}", shape, dtype))

        pl = Plan()
        mst = T("ap_mst", [128, 2 * 512], F32)
        rbb = T("ap_rb", [128, 2 * 512], F32)
        wbf = T("ap_w", [128, 2 * 512], F32)
        ubf = T("ap_u", [128, 512], F32)
        x2b = T("ap_x2", [128, 2 * 512], BF16)
        sbb = T("ap_sbb", [128, 2 * 512], BF16)
        psS1 = P("ap_S1", [128, 512])
        psS2 = P("ap_S2", [128, 512])
        src_mst = dt["xT"] if first else dt["master"]
        dst_mst = dt["outT"] if last else dt["master"]
        free_w = {}
        free_m = {}
        free_r = {}
        free_x = {}
        for tc in range(NTC):
            if not first and stats_from_sub:
                fs = {}
                for mc in range(NCH):
                    sl = mc % 2
                    dsub = S[f"d_sub{sl}"]
                    if sl in fs:
                        wait(pl, "g", mm, fs[sl])
                    elif sl in free_r:
                        wait(pl, "g", vv, free_r[sl])
                    r_t = dma(pl, "g", rbb[:, sl * 512:(sl + 1) * 512],
                              AP(dt["sub_spill"], mc * 128 * TL + tc * 512,
                                 [[TL, 128], [1, 512]]), dsub)
                    wait(pl, "v", dsub, r_t)
                    cb_t = minc(pl, "v", vv, lambda e, o=sbb[:, sl * 512:(sl + 1) * 512],
                                i=rbb[:, sl * 512:(sl + 1) * 512]: e.tensor_copy(o, i))
                    wait(pl, "s", dsub, r_t)
                    sq_tt = minc(pl, "s", ss, lambda e, o=x2b[:, sl * 512:(sl + 1) * 512],
                                 i=rbb[:, sl * 512:(sl + 1) * 512]: e.activation(o, i, AF.Square))
                    wait(pl, "t", vv, cb_t)
                    wait(pl, "t", ss, sq_tt)
                    pl.t(lambda e, o=psS1[:, :], l=sb["ones_b"][:, :],
                         r=sbb[:, sl * 512:(sl + 1) * 512],
                         a=(mc == 0), b=(mc == NCH - 1): e.matmul(o, l, r, start=a, stop=b))
                    s_tt = minc(pl, "t", mm, lambda e, o=psS2[:, :], l=sb["ones_b"][:, :],
                                r=x2b[:, sl * 512:(sl + 1) * 512],
                                a=(mc == 0), b=(mc == NCH - 1): e.matmul(o, l, r, start=a, stop=b))
                    fs[sl] = s_tt
                wait(pl, "v", mm, fs[(NCH - 1) % 2])
                sc2 = plan_scalars_slice(ctx, pl, psS1[:, :], psS2[:, :],
                                         sb["row_e"][:, :], sb["row_f"][:, :], 2048.0)
                wait(pl, "t", vv, sc2)
                wait(pl, "g", mm, fs[(NCH - 1) % 2])
                wait(pl, "s", mm, fs[(NCH - 1) % 2])
            elif not first:
                plan_scalars_slice(ctx, pl,
                                   sb["ln_o1"][:, tc * 512:(tc + 1) * 512],
                                   sb["ln_o2"][:, tc * 512:(tc + 1) * 512],
                                   sb["row_e"][:, :], sb["row_f"][:, :], 2048.0)
            for mc in range(NCH):
                sl = mc % 2
                dm = S[f"d_mst{sl}"]
                dsub = S[f"d_sub{sl}"]
                dwb = S[f"d_wb{sl}"]
                if sl in free_m:
                    wait(pl, "y", vv, free_m[sl])
                m_t = dma(pl, "y", mst[:, sl * 512:(sl + 1) * 512],
                          AP(src_mst, mc * 128 * TL + tc * 512, [[TL, 128], [1, 512]]), dm)
                if not first:
                    if sl in free_r:
                        wait(pl, "g", vv, free_r[sl])
                    r_t = dma(pl, "g", rbb[:, sl * 512:(sl + 1) * 512],
                              AP(dt["sub_spill"], mc * 128 * TL + tc * 512, [[TL, 128], [1, 512]]), dsub)
                wait(pl, "v", dm, m_t)
                if not first:
                    wait(pl, "v", dsub, r_t)
                    pl.v(lambda e, o=ubf[:, :], i=rbb[:, sl * 512:(sl + 1) * 512],
                         j=sb["row_e"][:, :]: e.tensor_mul(o, i, j))
                    pl.v(lambda e, o=ubf[:, :], i=ubf[:, :], j=sb["row_f"][:, :]: e.tensor_add(o, i, j))
                    if sl in free_w:
                        wait(pl, "v", vv, free_w[sl])
                    wv = minc(pl, "v", vv, lambda e, o=wbf[:, sl * 512:(sl + 1) * 512],
                              i=mst[:, sl * 512:(sl + 1) * 512], j=ubf[:, :]:
                              e.tensor_add(o, i, j))
                else:
                    if sl in free_w:
                        wait(pl, "v", vv, free_w[sl])
                    wv = minc(pl, "v", vv, lambda e, o=wbf[:, sl * 512:(sl + 1) * 512],
                              i=mst[:, sl * 512:(sl + 1) * 512]: e.tensor_copy(o, i))
                free_m[sl] = wv
                free_r[sl] = wv
                wait(pl, "y", vv, wv)
                w_t = dma(pl, "y", AP(dst_mst, mc * 128 * TL + tc * 512, [[TL, 128], [1, 512]]),
                          wbf[:, sl * 512:(sl + 1) * 512], dwb)
                if need_stats:
                    cv = minc(pl, "v", vv,
                              lambda e, o=sb["xsb"][:, mc * TL + tc * 512:mc * TL + (tc + 1) * 512],
                              i=wbf[:, sl * 512:(sl + 1) * 512]: e.tensor_copy(o, i))
                    wait(pl, "s", vv, wv)
                    if sl in free_x:
                        wait(pl, "s", mm, free_x[sl])
                    sv = minc(pl, "s", ss, lambda e, o=x2b[:, sl * 512:(sl + 1) * 512],
                              i=wbf[:, sl * 512:(sl + 1) * 512]: e.activation(o, i, AF.Square))
                    wait(pl, "t", vv, cv)
                    wait(pl, "t", ss, sv)
                    pl.t(lambda e, o=psS1[:, :], l=sb["ones_b"][:, :],
                         r=sb["xsb"][:, mc * TL + tc * 512:mc * TL + (tc + 1) * 512],
                         a=(mc == 0), b=(mc == NCH - 1): e.matmul(o, l, r, start=a, stop=b))
                    s_t = minc(pl, "t", mm, lambda e, o=psS2[:, :], l=sb["ones_b"][:, :],
                               r=x2b[:, sl * 512:(sl + 1) * 512],
                               a=(mc == 0), b=(mc == NCH - 1): e.matmul(o, l, r, start=a, stop=b))
                    free_x[sl] = s_t
                    wait(pl, "v", dwb, w_t)
                    wait(pl, "v", mm, s_t)
                    fw = tinc(pl, "v", vv)
                    free_w[sl] = fw
                else:
                    wait(pl, "v", dwb, w_t)
                    fw = tinc(pl, "v", vv)
                    free_w[sl] = fw
            if need_stats:
                mt = tinc(pl, "t", mm)
                wait(pl, "v", mm, mt)
                vvt = plan_scalars_slice(ctx, pl, psS1[:, :], psS2[:, :],
                                         sb["ln_r"][:, tc * 512:(tc + 1) * 512],
                                         sb["ln_a"][:, tc * 512:(tc + 1) * 512], 2048.0)
                wait(pl, "t", vv, vvt)
        wait(pl, "y", S["d_wb0"], S["d_wb0"].n)
        wait(pl, "y", S["d_wb1"], S["d_wb1"].n)
        pl.run(nc)



def phase_l0_attn(ctx, is_last):
    nc, sb = ctx.nc, ctx.sb
    uidA = ctx.uid = getattr(ctx, "uid", 0) + 1
    with contextlib.ExitStack() as astack:
        sb["ln_o1"] = astack.enter_context(nc.sbuf_tensor(f"ln_o1_{uidA}", [128, TL], F32))
        sb["ln_o2"] = astack.enter_context(nc.sbuf_tensor(f"ln_o2_{uidA}", [128, TL], F32))
        sb["masks_b"] = astack.enter_context(nc.sbuf_tensor(f"masks_{uidA}", [128, 8 * 512], BF16))
        pl = Plan()
        mt = dma(pl, "g", sb["masks_b"][:, :], ctx.dt["masks"][:, :], ctx.sem["d_g"])
        wait(pl, "g", ctx.sem["d_g"], mt)
        pl.run(nc)
        plan_l0a_kv(ctx)
        plan_l0a_main(ctx)
        phase_apply(ctx, first=False, last=is_last, need_stats=not is_last)


def plan_l0a_kv(ctx):
    nc, sb, dt, S = ctx.nc, ctx.sb, ctx.dt, ctx.sem
    mm, vv, ss, gg = S["mm"], S["vv"], S["ss"], S["gg"]
    stack = contextlib.ExitStack()
    ctx.l0a_stack = stack
    uid0 = ctx.uid = getattr(ctx, "uid", 0) + 1

    def TP(name, shape, dtype):
        return stack.enter_context(nc.sbuf_tensor(f"{name}_{uid0}", shape, dtype))

    krot = TP("l0a_kr", [128, TG], BF16)
    kvtm = TP("l0a_ktm", [128, 32 * 128], BF16)
    kn2 = TP("l0a_kn2", [128, 8], F32)
    ctx.l0a = dict(krot=krot, kvtm=kvtm, kn2=kn2)

    uid = ctx.uid = getattr(ctx, "uid", 0) + 1
    with contextlib.ExitStack() as st2:
        def T(name, shape, dtype):
            return st2.enter_context(nc.sbuf_tensor(f"{name}_{uid}", shape, dtype))

        def P(name, shape, dtype=F32):
            return st2.enter_context(nc.psum_tensor(f"{name}_{uid}", shape, dtype))

        pl = Plan()
        xnr = T("l0k_xnr", [128, NCH * 512], BF16)
        xtmp = T("l0k_xt", [128, 512], F32)
        wlat = T("l0k_wlat", [128, 2048], BF16)
        blat = T("l0k_blat", [128, 1], F32)
        kvbf = T("l0k_kv", [128, TL], BF16)
        kfull = T("l0k_kf", [128, TG], BF16)
        cs_f = T("l0k_csf", [128, 512], F32)
        sn_f = T("l0k_snf", [128, 512], F32)
        rt_f = T("l0k_rtf", [128, 512], BF16)
        sqn = T("l0k_sqn", [128, 512], BF16)
        rowf = T("l0k_rowf", [128, 8], F32)
        psA = P("l0k_psA", [128, 512])
        psR = P("l0k_psR", [128, 512])
        psT = P("l0k_psT", [128, 512], BF16)
        psN = P("l0k_psN", [128, 512])

        wl_t = dma(pl, "g", AP(wlat, 0, [[2048, 128], [128, 16], [1, 128]]),
                   AP(dt["L0_attn_W_latent"], 0, [[128, 128], [128 * 128, 16], [1, 128]]),
                   S["d_wl"])
        bl_t = dma(pl, "g", blat[:, 0:1], dt["b_latent_c"][:, 0:1], S["d_bl"])
        for tc in range(NTC):
            xt = plan_xnr(ctx, pl, xnr, xtmp, tc,
                          sb["ln_r"][:, tc * 512:(tc + 1) * 512],
                          sb["ln_a"][:, tc * 512:(tc + 1) * 512])
            wait(pl, "t", vv, xt)
            if tc == 0:
                wait(pl, "t", S["d_wl"], wl_t)
            kv_mm = None
            for c in range(NCH):
                if c < NCH - 1:
                    pl.t(lambda e, o=psA[:, :], l=wlat[:, c * 128:(c + 1) * 128],
                         r=xnr[:, c * 512:(c + 1) * 512], a=(c == 0):
                         e.matmul(o, l, r, start=a, stop=False))
                else:
                    kv_mm = minc(pl, "t", mm, lambda e, o=psA[:, :],
                                 l=wlat[:, c * 128:(c + 1) * 128],
                                 r=xnr[:, c * 512:(c + 1) * 512]:
                                 e.matmul(o, l, r, start=False, stop=True))
            wait(pl, "s", mm, kv_mm)
            if tc == 0:
                wait(pl, "s", S["d_bl"], bl_t)
            kv_ev = minc(pl, "s", ss, lambda e, o=kvbf[:, tc * 512:(tc + 1) * 512],
                         i=psA[:, :], b=blat[:, 0:1]:
                         e.activation(o, i, AF.Identity, bias=b))
            wait(pl, "v", mm, kv_mm)
            wait(pl, "g", ss, kv_ev)
            gdma_shared(ctx, pl, 0, dt["kv_sh"], f"rb{tc}", tc, [[TG, 128], [1, 512]],
                        AP(kvbf, tc * 512, [[TL, 128], [1, 512]]))
        ctx.l0a.update(kvbf=kvbf, kfull=kfull, xnr=xnr, wlat=wlat)
        publish_flag(ctx, pl, 0)
        poll_flag(ctx, pl, 0)
        kv_in = dma(pl, "g", kfull[:, :], dt["kv_sh"][:, :], S["d_kf"])
        wait(pl, "g", S["d_kf"], kv_in)
        kv_rdy = tinc(pl, "g", gg)
        kn_zero = minc(pl, "g", gg, lambda e, t=kn2: e.memset(t[:, 0:8], 0.0))
        wait(pl, "t", gg, kv_rdy)
        wait(pl, "v", gg, kv_rdy)
        prev_ktv = None
        prev_nmv = None
        for cc in range(TG // 512):
            dma(pl, "y", cs_f[:, :], dt["cos0"][:, cc * 512:(cc + 1) * 512], S["d_cs"])
            cs_t = dma(pl, "y", sn_f[:, :], dt["sin0"][:, cc * 512:(cc + 1) * 512], S["d_cs"])
            pm = minc(pl, "t", mm, lambda e, o=psR[:, :], l=sb["perm0_b"][:, :],
                      r=kfull[:, cc * 512:(cc + 1) * 512]: e.matmul(o, l, r, start=True, stop=True))
            wait(pl, "v", mm, pm)
            wait(pl, "v", S["d_cs"], cs_t)
            pl.v(lambda e, o=rt_f[:, :], i=psR[:, :], j=sn_f[:, :]: e.tensor_mul(o, i, j))
            pl.v(lambda e, o=ctx.l0a["krot"][:, cc * 512:(cc + 1) * 512],
                 i=kfull[:, cc * 512:(cc + 1) * 512], j=cs_f[:, :]: e.tensor_mul(o, i, j))
            kr_t = minc(pl, "v", vv, lambda e, o=ctx.l0a["krot"][:, cc * 512:(cc + 1) * 512],
                        i=ctx.l0a["krot"][:, cc * 512:(cc + 1) * 512], j=rt_f[:, :]:
                        e.tensor_add(o, i, j))
            wait(pl, "t", vv, kr_t)
            wait(pl, "y", vv, kr_t)
            if prev_ktv is not None:
                wait(pl, "t", vv, prev_ktv)
            for q in range(3):
                kt = cc * 4 + q
                pl.t(lambda e, o=psT[:, q * 128:(q + 1) * 128],
                     i=kfull[:, kt * 128:(kt + 1) * 128], d=sb["ident_b"][:, :]:
                     e.transpose(o, i, d))
            tr = minc(pl, "t", mm, lambda e, o=psT[:, 3 * 128:4 * 128],
                      i=kfull[:, (cc * 4 + 3) * 128:(cc * 4 + 4) * 128], d=sb["ident_b"][:, :]:
                      e.transpose(o, i, d))
            wait(pl, "v", mm, tr)
            prev_ktv = minc(pl, "v", vv,
                            lambda e, o=ctx.l0a["kvtm"][:, cc * 512:(cc + 1) * 512], i=psT[:, :]:
                            e.tensor_copy(o, i))
            wait(pl, "s", vv, kr_t)
            if prev_nmv is not None:
                wait(pl, "s", mm, prev_nm)
            sq_t = minc(pl, "s", ss,
                        lambda e, o=sqn[:, :], i=ctx.l0a["krot"][:, cc * 512:(cc + 1) * 512]:
                        e.activation(o, i, AF.Square))
            wait(pl, "t", ss, sq_t)
            if prev_nmv is not None:
                wait(pl, "t", vv, prev_nmv)
            nm = minc(pl, "t", mm, lambda e, o=psN[:, :], l=sb["ones_b"][:, :], r=sqn[:, :]:
                      e.matmul(o, l, r, start=True, stop=True))
            prev_nm = nm
            wait(pl, "v", mm, nm)
            if cc == 0:
                wait(pl, "v", gg, kn_zero)
            pl.v(lambda e, o=rowf[:, 0:1], i=psN[:, :]: e.reduce_max(o, i, axis=AX.X))
            pl.v(lambda e, o=ctx.l0a["kn2"][:, 1:2], i=ctx.l0a["kn2"][:, 0:1],
                 j=rowf[:, 0:1]: e.tensor_max(o, i, j))
            prev_nmv = minc(pl, "v", vv,
                            lambda e, o=ctx.l0a["kn2"][:, 0:1], i=ctx.l0a["kn2"][:, 1:2]:
                            e.tensor_copy(o, i))
        wait(pl, "s", vv, prev_nmv)
        minc(pl, "s", ss, lambda e, o=ctx.l0a["kn2"][:, 2:3], i=ctx.l0a["kn2"][:, 0:1]:
             e.activation(o, i, AF.Sqrt))
        pl.run(nc)


def plan_l0a_main(ctx):
    nc, sb, dt, S = ctx.nc, ctx.sb, ctx.dt, ctx.sem
    mm, vv, ss, gg = S["mm"], S["vv"], S["ss"], S["gg"]
    krot, kvtm, kn2 = ctx.l0a["krot"], ctx.l0a["kvtm"], ctx.l0a["kn2"]
    uid = ctx.uid = getattr(ctx, "uid", 0) + 1
    with contextlib.ExitStack() as stack:
        def T(name, shape, dtype):
            return stack.enter_context(nc.sbuf_tensor(f"{name}_{uid}", shape, dtype))

        def P(name, shape, dtype=F32):
            return stack.enter_context(nc.psum_tensor(f"{name}_{uid}", shape, dtype))

        pl = Plan()
        xnr = T("l0m_xnr", [128, NCH * 512], BF16)
        xtmp = T("l0m_xt", [128, 512], F32)
        wtile = T("l0m_w", [128, 2 * 2048], BF16)
        bdc = T("l0m_bd", [128, 16], F32)
        bpc = T("l0m_bp", [128, 16], F32)
        cs_o = T("l0m_cs", [128, 512], F32)
        sn_o = T("l0m_sn", [128, 512], F32)
        qtmp = T("l0m_qt", [128, 512], BF16)
        qrot = T("l0m_qr", [128, 512], BF16)
        rt_q = T("l0m_rtq", [128, 512], BF16)
        sqn = T("l0m_sqn", [128, 512], BF16)
        rowf = T("l0m_rowf", [128, 8], F32)
        recip = T("l0m_rcp", [128, 512], F32)
        pbuf = T("l0m_p", [128, 2 * 512], BF16)
        pmsk = T("l0m_pm", [128, 2 * 512], BF16)
        ynb = T("l0m_yn", [128, NHEAD * 512], BF16)
        ysb = T("l0m_ysb", [128, 512], F32)
        sub_t = T("l0m_sub", [128, 2 * 512], F32)
        sub_b = T("l0m_subb", [128, 512], BF16)
        sub_2 = T("l0m_sub2", [128, 512], BF16)
        psS = [P(f"l0m_psS{i}", [128, 512]) for i in range(2)]
        psY = P("l0m_psY", [128, 512])
        psSum = P("l0m_psSum", [128, 512])
        psA = P("l0m_psA", [128, 512])
        psB = P("l0m_psB", [128, 512])
        psO1 = P("l0m_psO1", [128, 512])
        psO2 = P("l0m_psO2", [128, 512])

        dma(pl, "g", bdc[:, :], dt["b_d_c"][:, :], S["d_bias"])
        bias_t = dma(pl, "g", bpc[:, :], dt["b_proj_c"][:, :], S["d_bias"])

        def load_wd(h, slot):
            return dma(pl, "g", AP(wtile, slot * 2048, [[2 * 2048, 128], [128, 16], [1, 128]]),
                       AP(dt["L0_attn_W_d"], h * 128, [[2048, 128], [128 * 2048, 16], [1, 128]]),
                       S[f"d_wt{slot}"])

        def load_wp(mc, slot):
            return dma(pl, "g", AP(wtile, slot * 2048, [[2 * 2048, 128], [128, 16], [1, 128]]),
                       AP(dt["L0_attn_W_proj"], mc * 128, [[2048, 128], [128 * 2048, 16], [1, 128]]),
                       S[f"d_wt{slot}"])

        tile_free = {0: 0, 1: 0}
        prev = {}
        sp_free = {}
        for qc in range(NTC):
            KT = 8 * (qc + 1)
            xt = plan_xnr(ctx, pl, xnr, xtmp, qc,
                          sb["ln_r"][:, qc * 512:(qc + 1) * 512],
                          sb["ln_a"][:, qc * 512:(qc + 1) * 512])
            if qc > 0:
                wait(pl, "y", vv, last_qr)
            dma(pl, "y", cs_o[:, :], dt["cos0o"][:, qc * 512:(qc + 1) * 512], S["d_cs"])
            cs_t = dma(pl, "y", sn_o[:, :], dt["sin0o"][:, qc * 512:(qc + 1) * 512], S["d_cs"])
            wait(pl, "g", mm, tile_free[0])
            wt = load_wd(0, 0)
            for h in range(NHEAD):
                slot = h % 2
                if h + 1 < NHEAD:
                    ns = (h + 1) % 2
                    wait(pl, "g", mm, tile_free[ns])
                    nwt = load_wd(h + 1, ns)
                wait(pl, "t", S[f"d_wt{slot}"], wt)
                if h == 0:
                    wait(pl, "t", vv, xt)
                for c in range(NCH - 1):
                    pl.t(lambda e, o=psA[:, :],
                         l=wtile[:, slot * 2048 + c * 128:slot * 2048 + (c + 1) * 128],
                         r=xnr[:, c * 512:(c + 1) * 512], a=(c == 0):
                         e.matmul(o, l, r, start=a, stop=False))
                pj = minc(pl, "t", mm, lambda e, o=psA[:, :],
                          l=wtile[:, slot * 2048 + 15 * 128:slot * 2048 + 16 * 128],
                          r=xnr[:, 15 * 512:16 * 512]:
                          e.matmul(o, l, r, start=False, stop=True))
                tile_free[slot] = pj
                wait(pl, "s", mm, pj)
                if h == 0 and qc == 0:
                    wait(pl, "s", S["d_bias"], bias_t)
                if prev:
                    wait(pl, "s", vv, prev["qr"])
                qe = minc(pl, "s", ss, lambda e, o=qtmp[:, :], i=psA[:, :], b=bdc[:, h:h + 1]:
                          e.activation(o, i, AF.Identity, bias=b))
                if prev:
                    wait(pl, "s", mm, prev["nm"])
                sq_t = minc(pl, "s", ss, lambda e, o=sqn[:, :], i=qtmp[:, :]:
                            e.activation(o, i, AF.Square))
                wait(pl, "t", ss, sq_t)
                if prev:
                    wait(pl, "t", vv, prev["rc"])
                nm = minc(pl, "t", mm, lambda e, o=psSum[:, :], l=sb["ones_b"][:, :], r=sqn[:, :]:
                          e.matmul(o, l, r, start=True, stop=True))
                wait(pl, "v", mm, nm)
                nv = minc(pl, "v", vv, lambda e, o=rowf[:, 0:1], i=psSum[:, :]:
                          e.reduce_max(o, i, axis=AX.X))
                wait(pl, "s", vv, nv)
                sv = minc(pl, "s", ss, lambda e, o=rowf[:, 1:2], i=rowf[:, 0:1]:
                          e.activation(o, i, AF.Sqrt))
                wait(pl, "v", ss, sv)
                pl.v(lambda e, o=rowf[:, 2:3], i=rowf[:, 1:2], j=kn2[:, 2:3]:
                     e.tensor_mul(o, i, j))
                c0_rdy = minc(pl, "v", vv, lambda e, o=rowf[:, 3:4], i=rowf[:, 2:3]:
                              e.tensor_scalar(o, i, -SCALE0, None, ALU.mult))
                wait(pl, "t", ss, qe)
                pm = minc(pl, "t", mm, lambda e, o=psB[:, :], l=sb["perm0_b"][:, :], r=qtmp[:, :]:
                          e.matmul(o, l, r, start=True, stop=True))
                wait(pl, "v", mm, pm)
                if h == 0:
                    wait(pl, "v", S["d_cs"], cs_t)
                pl.v(lambda e, o=rt_q[:, :], i=psB[:, :], j=sn_o[:, :]: e.tensor_mul(o, i, j))
                pl.v(lambda e, o=qrot[:, :], i=qtmp[:, :], j=cs_o[:, :]: e.tensor_mul(o, i, j))
                qr = minc(pl, "v", vv, lambda e, o=qrot[:, :], i=qrot[:, :], j=rt_q[:, :]:
                          e.tensor_add(o, i, j))
                wait(pl, "t", vv, qr)
                wait(pl, "s", vv, c0_rdy)
                for gi in range(KT // 2):
                    qk = None
                    for u in range(2):
                        kt = gi * 2 + u
                        if u < 1:
                            pl.t(lambda e, o=psS[u][:, :], l=krot[:, kt * 128:(kt + 1) * 128],
                                 r=qrot[:, :]: e.matmul(o, l, r, start=True, stop=True))
                        else:
                            qk = minc(pl, "t", mm,
                                      lambda e, o=psS[u][:, :], l=krot[:, kt * 128:(kt + 1) * 128],
                                      r=qrot[:, :]: e.matmul(o, l, r, start=True, stop=True))
                    wait(pl, "s", mm, qk)
                    ex = None
                    for u in range(2):
                        if u < 1:
                            pl.s(lambda e, o=pbuf[:, u * 512:(u + 1) * 512], i=psS[u][:, :],
                                 c=rowf[:, 3:4]: e.activation(o, i, AF.Exp, bias=c, scale=SCALE0))
                        else:
                            ex = minc(pl, "s", ss,
                                      lambda e, o=pbuf[:, u * 512:(u + 1) * 512], i=psS[u][:, :],
                                      c=rowf[:, 3:4]: e.activation(o, i, AF.Exp, bias=c, scale=SCALE0))
                    wait(pl, "t", ss, ex)
                    mask_us = [u for u in range(2) if (gi * 2 + u) - 8 * qc >= 0]
                    mk = None
                    if mask_us:
                        wait(pl, "v", ss, ex)
                        for u in mask_us:
                            w = (gi * 2 + u) - 8 * qc
                            if u != mask_us[-1]:
                                pl.v(lambda e, o=pmsk[:, u * 512:(u + 1) * 512],
                                     i=pbuf[:, u * 512:(u + 1) * 512],
                                     j=sb["masks_b"][:, w * 512:(w + 1) * 512]:
                                     e.tensor_mul(o, i, j))
                            else:
                                mk = minc(pl, "v", vv,
                                          lambda e, o=pmsk[:, u * 512:(u + 1) * 512],
                                          i=pbuf[:, u * 512:(u + 1) * 512],
                                          j=sb["masks_b"][:, w * 512:(w + 1) * 512]:
                                          e.tensor_mul(o, i, j))
                        wait(pl, "t", vv, mk)
                    av = None
                    for u in range(2):
                        kt = gi * 2 + u
                        w = kt - 8 * qc
                        src = pmsk if w >= 0 else pbuf
                        pl.t(lambda e, o=psY[:, :], l=kvtm[:, kt * 128:(kt + 1) * 128],
                             r=src[:, u * 512:(u + 1) * 512], a=(kt == 0), b=(kt == KT - 1):
                             e.matmul(o, l, r, start=a, stop=b))
                        if u < 1:
                            pl.t(lambda e, o=psSum[:, :], l=sb["ones_b"][:, :],
                                 r=src[:, u * 512:(u + 1) * 512], a=(kt == 0), b=(kt == KT - 1):
                                 e.matmul(o, l, r, start=a, stop=b))
                        else:
                            av = minc(pl, "t", mm,
                                      lambda e, o=psSum[:, :], l=sb["ones_b"][:, :],
                                      r=src[:, u * 512:(u + 1) * 512], a=(kt == 0), b=(kt == KT - 1):
                                      e.matmul(o, l, r, start=a, stop=b))
                    wait(pl, "s", mm, av)
                    wait(pl, "v", mm, av)
                rc = minc(pl, "v", vv, lambda e, o=recip[:, :], i=psSum[:, :]:
                          e.reciprocal(o, i))
                ye = minc(pl, "s", ss, lambda e, o=ysb[:, :], i=psY[:, :]:
                          e.activation(o, i, AF.Identity))
                wait(pl, "v", ss, ye)
                yn = minc(pl, "v", vv, lambda e, o=ynb[:, h * 512:(h + 1) * 512],
                          i=ysb[:, :], j=recip[:, :]: e.tensor_mul(o, i, j))
                wait(pl, "t", vv, yn)
                prev = dict(qr=qr, nm=nm, yn=yn, rc=rc)
                last_qr = qr
                if h + 1 < NHEAD:
                    wt = nwt

            wait(pl, "g", mm, tile_free[0])
            wp = load_wp(0, 0)
            for mc in range(NCH):
                slot = mc % 2
                if mc + 1 < NCH:
                    ns = (mc + 1) % 2
                    wait(pl, "g", mm, tile_free[ns])
                    nwp = load_wp(mc + 1, ns)
                wait(pl, "t", S[f"d_wt{slot}"], wp)
                for hc in range(NHEAD - 1):
                    pl.t(lambda e, o=psA[:, :],
                         l=wtile[:, slot * 2048 + hc * 128:slot * 2048 + (hc + 1) * 128],
                         r=ynb[:, hc * 512:(hc + 1) * 512], a=(hc == 0):
                         e.matmul(o, l, r, start=a, stop=False))
                oj = minc(pl, "t", mm, lambda e, o=psA[:, :],
                          l=wtile[:, slot * 2048 + 15 * 128:slot * 2048 + 16 * 128],
                          r=ynb[:, 15 * 512:16 * 512]:
                          e.matmul(o, l, r, start=False, stop=True))
                tile_free[slot] = oj
                wait(pl, "s", mm, oj)
                if mc == 0 and qc == 0:
                    wait(pl, "s", S["d_bias"], bias_t)
                if slot in sp_free:
                    wait(pl, "s", S[f"d_sp{slot}"], sp_free[slot])
                se = minc(pl, "s", ss, lambda e, o=sub_t[:, slot * 512:(slot + 1) * 512],
                          i=psA[:, :], b=bpc[:, mc:mc + 1]:
                          e.activation(o, i, AF.Identity, bias=b))
                wait(pl, "v", ss, se)
                if qc > 0 or mc >= 1:
                    wait(pl, "v", mm, prev_stt)
                sbt = minc(pl, "v", vv, lambda e, o=sub_b[:, :],
                           i=sub_t[:, slot * 512:(slot + 1) * 512]: e.tensor_copy(o, i))
                if qc > 0 or mc >= 1:
                    wait(pl, "s", mm, prev_stt2)
                s2t = minc(pl, "s", ss, lambda e, o=sub_2[:, :],
                           i=sub_t[:, slot * 512:(slot + 1) * 512]: e.activation(o, i, AF.Square))
                wait(pl, "t", vv, sbt)
                wait(pl, "t", ss, s2t)
                pl.t(lambda e, o=psO1[:, :], l=sb["ones_b"][:, :], r=sub_b[:, :],
                     a=(mc == 0), b=(mc == NCH - 1): e.matmul(o, l, r, start=a, stop=b))
                stt = minc(pl, "t", mm, lambda e, o=psO2[:, :], l=sb["ones_b"][:, :],
                           r=sub_2[:, :], a=(mc == 0), b=(mc == NCH - 1):
                           e.matmul(o, l, r, start=a, stop=b))
                prev_stt = stt
                prev_stt2 = stt
                wait(pl, "y", ss, se)
                spw = dma(pl, "y", AP(dt["sub_spill"], mc * 128 * TL + qc * 512,
                                      [[TL, 128], [1, 512]]),
                          sub_t[:, slot * 512:(slot + 1) * 512], S[f"d_sp{slot}"])
                sp_free[slot] = spw
                if mc + 1 < NCH:
                    wp = nwp
            wait(pl, "v", mm, prev_stt)
            pl.v(lambda e, o=sb["ln_o1"][:, qc * 512:(qc + 1) * 512], i=psO1[:, :]:
                 e.tensor_copy(o, i))
            ot = minc(pl, "v", vv, lambda e, o=sb["ln_o2"][:, qc * 512:(qc + 1) * 512],
                      i=psO2[:, :]: e.tensor_copy(o, i))
            wait(pl, "t", vv, ot)
        pl.run(nc)
    ctx.l0a_stack.close()






def phase_l0_ffn(ctx, is_last):
    plan_moe_gate(ctx)
    plan_moe_experts(ctx)
    phase_apply(ctx, first=False, last=is_last, need_stats=not is_last,
                stats_from_sub=True)


def plan_moe_gate(ctx):
    nc, sb, dt, S = ctx.nc, ctx.sb, ctx.dt, ctx.sem
    mm, vv, ss, gg = S["mm"], S["vv"], S["ss"], S["gg"]
    stack = contextlib.ExitStack()
    ctx.moe_stack = stack
    uid0 = ctx.uid = getattr(ctx, "uid", 0) + 1

    def TP(name, shape, dtype):
        return stack.enter_context(nc.sbuf_tensor(f"{name}_{uid0}", shape, dtype))

    xnr_all = TP("moe_xnr", [128, NCH * TL], BF16)
    combT = TP("moe_combT", [8, TL], F32)
    crow = TP("moe_crow", [1, 512], F32)
    ctx.moe = dict(xnr_all=xnr_all, combT=combT, crow=crow)

    uid = ctx.uid = getattr(ctx, "uid", 0) + 1
    with contextlib.ExitStack() as st2:
        def T(name, shape, dtype):
            return st2.enter_context(nc.sbuf_tensor(f"{name}_{uid}", shape, dtype))

        def P(name, shape, dtype=F32):
            return st2.enter_context(nc.psum_tensor(f"{name}_{uid}", shape, dtype))

        pl = Plan()
        xtmp = T("mg_xt", [128, 512], F32)
        gw = T("mg_gw", [128, 16 * 8], BF16)
        gsb = T("mg_gsb", [8, TL], F32)
        idf = T("mg_idf", [128, 128], F32)
        sc_tm = T("mg_sctm", [128, 16 * 8], F32)
        prob = T("mg_prob", [128, 16 * 8], F32)
        rw1 = T("mg_rw1", [128, 4], F32)
        psG = P("mg_psG", [8, 512])
        psT2 = P("mg_psT2", [128, 8 * 16])
        psC = P("mg_psC", [8, 128])

        g_t = dma(pl, "g", AP(gw, 0, [[128, 128], [8, 16], [1, 8]]),
                  AP(dt["L0_ffn_gate_w"], 0, [[8, 128], [128 * 8, 16], [1, 8]]),
                  S["d_wl"])
        i_t = dma(pl, "g", idf[:, :], dt["ident"][:, :], S["d_bl"])

        for tc in range(NTC):
            xt = plan_xnr(ctx, pl, xnr_all, xtmp, tc,
                          sb["ln_r"][:, tc * 512:(tc + 1) * 512],
                          sb["ln_a"][:, tc * 512:(tc + 1) * 512], full=True)
        wait(pl, "t", S["d_wl"], g_t)
        wait(pl, "t", vv, xt)
        for tc in range(NTC):
            for c in range(NCH - 1):
                pl.t(lambda e, o=psG[:, :], l=gw[:, c * 8:(c + 1) * 8],
                     r=xnr_all[:, c * TL + tc * 512:c * TL + (tc + 1) * 512],
                     a=(c == 0): e.matmul(o, l, r, start=a, stop=False))
            gmt = minc(pl, "t", mm, lambda e, o=psG[:, :], l=gw[:, 15 * 8:16 * 8],
                       r=xnr_all[:, 15 * TL + tc * 512:15 * TL + (tc + 1) * 512]:
                       e.matmul(o, l, r, start=False, stop=True))
            wait(pl, "s", mm, gmt)
            ge = minc(pl, "s", ss, lambda e, o=gsb[0:8, tc * 512:(tc + 1) * 512],
                      i=psG[:, :]: e.activation(o, i, AF.Identity))
            wait(pl, "t", ss, ge)
        wait(pl, "t", S["d_bl"], i_t)
        for pc in range(16):
            lst = (pc == 15)
            fn = (lambda e, o=psT2[:, pc * 8:(pc + 1) * 8],
                  i=gsb[0:8, pc * 128:(pc + 1) * 128], d=idf[0:8, 0:8]:
                  e.transpose(o, i, d))
            if lst:
                t2t = minc(pl, "t", mm, fn)
            else:
                pl.t(fn)
        wait(pl, "v", mm, t2t)
        sc_t = minc(pl, "v", vv, lambda e, o=sc_tm[:, :], i=psT2[:, :]: e.tensor_copy(o, i))

        BIG = 1.0e30
        for pc in range(16):
            sl_ap = sc_tm[:, pc * 8:(pc + 1) * 8]
            pr_ap = prob[:, pc * 8:(pc + 1) * 8]
            if pc == 0:
                wait(pl, "v", vv, sc_t)
            m1 = minc(pl, "v", vv, lambda e, o=rw1[:, 0:1], i=sl_ap:
                      e.reduce_max(o, i, axis=AX.X))
            wait(pl, "v", vv, m1)
            nm1 = minc(pl, "v", vv, lambda e, o=rw1[:, 1:2], i=rw1[:, 0:1]:
                       e.tensor_scalar(o, i, -1.0, None, ALU.mult))
            wait(pl, "s", vv, nm1)
            ex = minc(pl, "s", ss, lambda e, o=pr_ap, i=sl_ap, b=rw1[:, 1:2]:
                      e.activation(o, i, AF.Exp, bias=b))
            wait(pl, "v", ss, ex)
            sm = minc(pl, "v", vv, lambda e, o=rw1[:, 2:3], i=pr_ap:
                      e.reduce_sum(o, i, axis=AX.X))
            wait(pl, "v", vv, sm)
            rc = minc(pl, "v", vv, lambda e, o=rw1[:, 3:4], i=rw1[:, 2:3]:
                      e.reciprocal(o, i))
            wait(pl, "v", vv, rc)
            pn = minc(pl, "v", vv, lambda e, o=pr_ap, i=pr_ap, j=rw1[:, 3:4]:
                      e.tensor_scalar(o, i, j, None, ALU.mult))
            wait(pl, "v", vv, pn)
            m1b = minc(pl, "v", vv, lambda e, o=rw1[:, 0:1], i=pr_ap:
                       e.reduce_max(o, i, axis=AX.X))
            wait(pl, "v", vv, m1b)
            eq = minc(pl, "v", vv, lambda e, o=sl_ap, i=pr_ap, j=rw1[:, 0:1]:
                      e.tensor_scalar(o, i, j, -BIG, ALU.is_ge, ALU.mult))
            wait(pl, "v", vv, eq)
            msk = minc(pl, "v", vv, lambda e, o=sl_ap, i=pr_ap, j=sl_ap:
                       e.tensor_add(o, i, j))
            wait(pl, "v", vv, msk)
            m2 = minc(pl, "v", vv, lambda e, o=rw1[:, 1:2], i=sl_ap:
                      e.reduce_max(o, i, axis=AX.X))
            wait(pl, "v", vv, m2)
            kp = minc(pl, "v", vv, lambda e, o=sl_ap, i=pr_ap, j=rw1[:, 1:2]:
                      e.tensor_scalar(o, i, j, None, ALU.is_ge))
            wait(pl, "v", vv, kp)
            cm = minc(pl, "v", vv, lambda e, o=pr_ap, i=pr_ap, j=sl_ap:
                      e.tensor_mul(o, i, j))
            wait(pl, "v", vv, cm)
        wait(pl, "t", vv, ctx.sem["vv"].n)
        for pc in range(16):
            lst = (pc == 15)
            fn = (lambda e, o=psC[:, :], i=prob[:, pc * 8:(pc + 1) * 8], d=idf[:, :]:
                  e.transpose(o, i, d))
            if lst:
                ct = minc(pl, "t", mm, fn)
            else:
                ct = minc(pl, "t", mm, fn)
            wait(pl, "s", mm, ct)
            cte = minc(pl, "s", ss, lambda e, o=combT[0:8, pc * 128:(pc + 1) * 128],
                       i=psC[:, :]: e.activation(o, i, AF.Identity))
            wait(pl, "t", ss, cte)
        wait(pl, "g", ss, ctx.sem["ss"].n)
        cw = dma(pl, "g", dt["comb_dram"][:, :], combT[:, :], S["d_x"])
        wait(pl, "g", S["d_x"], cw)
        pl.run(nc)


def plan_moe_experts(ctx):
    nc, sb, dt, S = ctx.nc, ctx.sb, ctx.dt, ctx.sem
    mm, vv, ss, gg = S["mm"], S["vv"], S["ss"], S["gg"]
    xnr_all = ctx.moe["xnr_all"]
    combT = ctx.moe["combT"]
    crow = ctx.moe["crow"]
    uid = ctx.uid = getattr(ctx, "uid", 0) + 1
    with contextlib.ExitStack() as stack:
        def T(name, shape, dtype):
            return stack.enter_context(nc.sbuf_tensor(f"{name}_{uid}", shape, dtype))

        def P(name, shape, dtype=F32):
            return stack.enter_context(nc.psum_tensor(f"{name}_{uid}", shape, dtype))

        pl = Plan()
        xsb_addr = nc.lookup_mloc(sb["xsb"]).addr
        hbuf = nc.alloc_sbuf_tensor_at(f"moe_h_{uid}", [128, 16 * TL], BF16,
                                       offset=xsb_addr)
        wg = T("me_wg", [128, 2 * 2048], BF16)
        wu = T("me_wu", [128, 2 * 2048], BF16)
        wd = T("me_wd", [128, 2 * 11 * 128 * 2], BF16)
        t1b = T("me_t1", [128, 512], BF16)
        t2b = T("me_t2", [128, 512], BF16)
        comb_sb = T("me_cmb", [128, TL], BF16)
        dsb = T("me_dsb", [128, 2 * 512], F32)
        psA = P("me_psA", [128, 512])
        psB = P("me_psB", [128, 512])
        psC = P("me_psC", [128, 512])

        d_acc = S["d_acc"]
        prev_acc_tgt = d_acc.n

        def load_gu(ei, m, slot, HCH):
            if ei < 8:
                srcs = [("L0_ffn_w_gate", wg), ("L0_ffn_w_up", wu)]
                H = 1408
                base = ei * D * H
            else:
                srcs = [("L0_ffn_sw_gate", wg), ("L0_ffn_sw_up", wu)]
                H = 2816
                base = 0
            ts = []
            for nm, dstb in srcs:
                ts.append(dma(pl, "g",
                              AP(dstb, slot * 2048, [[2 * 2048, 128], [128, 16], [1, 128]]),
                              AP(dt[nm], base + m * 128, [[H, 128], [128 * H, 16], [1, 128]]),
                              S[f"d_we{slot}"]))
            return ts[-1]

        def load_wd(ei, mc, slot, HCH):
            if ei < 8:
                nm, H, base = "L0_ffn_w_down", 1408, ei * 1408 * D
            else:
                nm, H, base = "L0_ffn_sw_down", 2816, 0
            return dma(pl, "g",
                       AP(wd, slot * (22 * 128), [[2 * 22 * 128, 128], [128, HCH], [1, 128]]),
                       AP(dt[nm], base + mc * 128, [[D, 128], [128 * D, HCH], [1, 128]]),
                       S[f"d_wt{slot}"])

        gu_free = {0: 0, 1: 0}
        wd_free = {0: 0, 1: 0}
        dsb_free = {}
        for ei in range(MOE_EXPERTS):
            HCH = 11 if ei < 8 else 22
            shared = (ei == 8)
            if not shared:
                for tc in range(NTC):
                    ct = dma(pl, "g", crow[0:1, :],
                             dt["comb_dram"][ei:ei + 1, tc * 512:(tc + 1) * 512], S["d_x"])
                    wait(pl, "t", S["d_x"], ct)
                    cbt = bc_row(ctx, pl, psC[:, :], crow[0:1, :], sem=ctx.sem["mm"])
                    wait(pl, "v", mm, cbt)
                    cse = minc(pl, "v", vv, lambda e, o=comb_sb[:, tc * 512:(tc + 1) * 512],
                               i=psC[:, :]: e.tensor_copy(o, i))
                    wait(pl, "t", vv, cse)
                    wait(pl, "g", vv, cse)
            if MOE_SKIP_GU:
                continue
            halves = 1 if not shared else 2
            wait(pl, "g", d_acc, prev_acc_tgt)
            for hf in range(halves):
                tcs = list(range(NTC)) if halves == 1 else [2 * hf, 2 * hf + 1]
                hw = len(tcs) * 512

                def hcol(m, tc, hw=hw, t0=tcs[0]):
                    return m * hw + (tc - t0) * 512

                wait(pl, "g", mm, gu_free[0])
                gt = load_gu(ei, 0, 0, HCH)
                for m in range(HCH):
                    slot = m % 2
                    if m + 1 < HCH:
                        ns = (m + 1) % 2
                        wait(pl, "g", mm, gu_free[ns])
                        ngt = load_gu(ei, m + 1, ns, HCH)
                    wait(pl, "t", S[f"d_we{slot}"], gt)
                    for tc in tcs:
                        for c in range(NCH):
                            pl.t(lambda e, o=psA[:, :],
                                 l=wg[:, slot * 2048 + c * 128:slot * 2048 + (c + 1) * 128],
                                 r=xnr_all[:, c * TL + tc * 512:c * TL + (tc + 1) * 512],
                                 a=(c == 0), b=(c == NCH - 1): e.matmul(o, l, r, start=a, stop=b))
                        for c in range(NCH - 1):
                            pl.t(lambda e, o=psB[:, :],
                                 l=wu[:, slot * 2048 + c * 128:slot * 2048 + (c + 1) * 128],
                                 r=xnr_all[:, c * TL + tc * 512:c * TL + (tc + 1) * 512],
                                 a=(c == 0): e.matmul(o, l, r, start=a, stop=False))
                        gut = minc(pl, "t", mm, lambda e, o=psB[:, :],
                                   l=wu[:, slot * 2048 + 15 * 128:slot * 2048 + 16 * 128],
                                   r=xnr_all[:, 15 * TL + tc * 512:15 * TL + (tc + 1) * 512]:
                                   e.matmul(o, l, r, start=False, stop=True))
                        gu_free[slot] = gut
                        wait(pl, "s", mm, gut)
                        si = minc(pl, "s", ss, lambda e, o=t1b[:, :], i=psA[:, :]:
                                  e.activation(o, i, AF.Silu))
                        wait(pl, "v", ss, si)
                        u2 = minc(pl, "v", vv, lambda e, o=t2b[:, :], i=t1b[:, :], j=psB[:, :]:
                                  e.tensor_mul(o, i, j))
                        wait(pl, "v", vv, u2)
                        hc = hcol(m, tc)
                        if shared:
                            hv = minc(pl, "v", vv,
                                      lambda e, o=hbuf[:, hc:hc + 512],
                                      i=t2b[:, :]: e.tensor_copy(o, i))
                        else:
                            hv = minc(pl, "v", vv,
                                      lambda e, o=hbuf[:, hc:hc + 512],
                                      i=t2b[:, :], j=comb_sb[:, tc * 512:(tc + 1) * 512]:
                                      e.tensor_mul(o, i, j))
                        wait(pl, "t", ss, si)
                        wait(pl, "t", vv, u2)
                    if m + 1 < HCH:
                        gt = ngt
                h_done = ctx.sem["vv"].n
                if MOE_SKIP_DOWN:
                    continue
                wait(pl, "g", mm, wd_free[0])
                wt = load_wd(ei, 0, 0, HCH)
                for mc in range(NCH):
                    slot = mc % 2
                    if mc + 1 < NCH:
                        ns = (mc + 1) % 2
                        wait(pl, "g", mm, wd_free[ns])
                        nwt = load_wd(ei, mc + 1, ns, HCH)
                    wait(pl, "t", S[f"d_wt{slot}"], wt)
                    wait(pl, "t", vv, h_done)
                    for tc in tcs:
                        for hk in range(HCH - 1):
                            pl.t(lambda e, o=psA[:, :],
                                 l=wd[:, slot * (22 * 128) + hk * 128:slot * (22 * 128) + (hk + 1) * 128],
                                 r=hbuf[:, hcol(hk, tc):hcol(hk, tc) + 512],
                                 a=(hk == 0): e.matmul(o, l, r, start=a, stop=False))
                        dmt = minc(pl, "t", mm, lambda e, o=psA[:, :],
                                   l=wd[:, slot * (22 * 128) + (HCH - 1) * 128:slot * (22 * 128) + HCH * 128],
                                   r=hbuf[:, hcol(HCH - 1, tc):hcol(HCH - 1, tc) + 512]:
                                   e.matmul(o, l, r, start=False, stop=True))
                        wd_free[slot] = dmt
                        wait(pl, "s", mm, dmt)
                        dsl = (mc * NTC + tc) % 2
                        if dsl in dsb_free:
                            wait(pl, "s", S["d_acc"], dsb_free[dsl])
                        de = minc(pl, "s", ss, lambda e, o=dsb[:, dsl * 512:(dsl + 1) * 512],
                                  i=psA[:, :]: e.activation(o, i, AF.Identity))
                        wait(pl, "t", ss, de)
                        wait(pl, "g", ss, de)
                        at = dma(pl, "g", AP(dt["sub_spill"], mc * 128 * TL + tc * 512,
                                             [[TL, 128], [1, 512]]),
                                 dsb[:, dsl * 512:(dsl + 1) * 512], d_acc,
                                 **({} if ei == 0 else dict(accum_op=ALU.add)))
                        dsb_free[dsl] = at
                    if mc + 1 < NCH:
                        wt = nwt
            prev_acc_tgt = d_acc.n
        wait(pl, "g", d_acc, d_acc.n)
        pl.run(nc)
    ctx.moe_stack.close()
    ctx.moe = None





def phase_l1_attn(ctx, is_last):
    nc, sb = ctx.nc, ctx.sb
    uidA = ctx.uid = getattr(ctx, "uid", 0) + 1
    with contextlib.ExitStack() as astack:
        sb["ln_o1"] = astack.enter_context(nc.sbuf_tensor(f"ln_o1_{uidA}", [128, TL], BF16))
        sb["ln_o2"] = astack.enter_context(nc.sbuf_tensor(f"ln_o2_{uidA}", [128, TL], BF16))
        sb["masks_b"] = astack.enter_context(nc.sbuf_tensor(f"masks_{uidA}", [128, 8 * 512], BF16))
        pl = Plan()
        mt = dma(pl, "g", sb["masks_b"][:, :], ctx.dt["masks"][:, :], ctx.sem["d_g"])
        wait(pl, "g", ctx.sem["d_g"], mt)
        pl.run(nc)
        plan_l1a_kv(ctx)
        plan_l1a_main(ctx)
        phase_apply(ctx, first=False, last=is_last, need_stats=not is_last)
    del sb["ln_o1"], sb["ln_o2"], sb["masks_b"]


def load_l1w(ctx, pl, name, mc, wtile, slot):
    return dma(pl, "g", AP(wtile, slot * 2048, [[2 * 2048, 128], [128, 16], [1, 128]]),
               AP(ctx.dt[name], mc * 128, [[2048, 128], [128 * 2048, 16], [1, 128]]),
               ctx.sem[f"d_wt{slot}"])


def plan_l1a_kv(ctx):
    nc, sb, dt, S = ctx.nc, ctx.sb, ctx.dt, ctx.sem
    mm, vv, ss, gg = S["mm"], S["vv"], S["ss"], S["gg"]
    po = ctx.posems[1]
    uid = ctx.uid = getattr(ctx, "uid", 0) + 1
    with contextlib.ExitStack() as stack:
        def T(name, shape, dtype):
            return stack.enter_context(nc.sbuf_tensor(f"{name}_{uid}", shape, dtype))

        def P(name, shape, dtype=F32):
            return stack.enter_context(nc.psum_tensor(f"{name}_{uid}", shape, dtype))

        pl = Plan()
        xnr = T("l1k_xnr", [128, NCH * 512], BF16)
        xtmp = T("l1k_xt", [128, 512], F32)
        wtile = T("l1k_w", [128, 2 * 2048], BF16)
        klocal = T("l1k_kl", [128, 16 * 512], BF16)
        vlocal = T("l1k_vl", [128, 4 * 2048], BF16)
        cs_o = T("l1k_cs", [128, 512], F32)
        sn_o = T("l1k_sn", [128, 512], F32)
        ktmp = T("l1k_kt", [128, 512], BF16)
        rt_k = T("l1k_rt", [128, 512], BF16)
        psA = P("l1k_psA", [128, 512])
        psB = P("l1k_psB", [128, 512])
        psT = P("l1k_psT", [128, 512], BF16)

        tile_free = {0: 0, 1: 0}
        cs_used = None
        po_prev = None
        for tc in range(NTC):
            xt = plan_xnr(ctx, pl, xnr, xtmp, tc,
                          sb["ln_r"][:, tc * 512:(tc + 1) * 512],
                          sb["ln_a"][:, tc * 512:(tc + 1) * 512])
            if cs_used is not None:
                wait(pl, "y", vv, cs_used)
            dma(pl, "y", cs_o[:, :], dt["cos1o"][:, tc * 512:(tc + 1) * 512], S["d_cs"])
            cs_t = dma(pl, "y", sn_o[:, :], dt["sin1o"][:, tc * 512:(tc + 1) * 512], S["d_cs"])
            if po_prev is not None:
                wait(pl, "v", po, po_prev)
            wait(pl, "g", mm, tile_free[0])
            wt = load_l1w(ctx, pl, "L1_attn_WkT", 0, wtile, 0)
            kr = None
            for mc in range(NCH):
                slot = mc % 2
                if mc + 1 < NCH:
                    ns = (mc + 1) % 2
                    wait(pl, "g", mm, tile_free[ns])
                    nwt = load_l1w(ctx, pl, "L1_attn_WkT", mc + 1, wtile, ns)
                wait(pl, "t", S[f"d_wt{slot}"], wt)
                if mc == 0:
                    wait(pl, "t", vv, xt)
                for c in range(NCH - 1):
                    pl.t(lambda e, o=psA[:, :],
                         l=wtile[:, slot * 2048 + c * 128:slot * 2048 + (c + 1) * 128],
                         r=xnr[:, c * 512:(c + 1) * 512], a=(c == 0):
                         e.matmul(o, l, r, start=a, stop=False))
                pj = minc(pl, "t", mm, lambda e, o=psA[:, :],
                          l=wtile[:, slot * 2048 + 15 * 128:slot * 2048 + 16 * 128],
                          r=xnr[:, 15 * 512:16 * 512]: e.matmul(o, l, r, start=False, stop=True))
                tile_free[slot] = pj
                wait(pl, "s", mm, pj)
                if kr is not None:
                    wait(pl, "s", vv, kr)
                ke = minc(pl, "s", ss, lambda e, o=ktmp[:, :], i=psA[:, :]:
                          e.activation(o, i, AF.Identity))
                wait(pl, "t", ss, ke)
                pm = minc(pl, "t", mm, lambda e, o=psB[:, :], l=sb["perm1_b"][:, :],
                          r=ktmp[:, :]: e.matmul(o, l, r, start=True, stop=True))
                wait(pl, "v", mm, pm)
                if mc == 0:
                    wait(pl, "v", S["d_cs"], cs_t)
                pl.v(lambda e, o=rt_k[:, :], i=psB[:, :], j=sn_o[:, :]: e.tensor_mul(o, i, j))
                pl.v(lambda e, o=klocal[:, mc * 512:(mc + 1) * 512], i=ktmp[:, :], j=cs_o[:, :]:
                     e.tensor_mul(o, i, j))
                kr = minc(pl, "v", vv, lambda e, o=klocal[:, mc * 512:(mc + 1) * 512],
                          i=klocal[:, mc * 512:(mc + 1) * 512], j=rt_k[:, :]:
                          e.tensor_add(o, i, j))
                wait(pl, "t", vv, kr)
                if mc + 1 < NCH:
                    wt = nwt
            cs_used = kr
            wait(pl, "g", vv, kr)
            gdma_shared(ctx, pl, 1, dt["k_sh"], f"rb{tc}", tc,
                        [[TG, 128], [128 * TG, 16], [1, 512]],
                        AP(klocal, 0, [[16 * 512, 128], [512, 16], [1, 512]]))
            wait(pl, "g", mm, tile_free[0])
            wt = load_l1w(ctx, pl, "L1_attn_WvT", 0, wtile, 0)
            ve = None
            for mc in range(NCH):
                slot = mc % 2
                if mc + 1 < NCH:
                    ns = (mc + 1) % 2
                    wait(pl, "g", mm, tile_free[ns])
                    nwt = load_l1w(ctx, pl, "L1_attn_WvT", mc + 1, wtile, ns)
                wait(pl, "t", S[f"d_wt{slot}"], wt)
                for c in range(NCH - 1):
                    pl.t(lambda e, o=psA[:, :],
                         l=wtile[:, slot * 2048 + c * 128:slot * 2048 + (c + 1) * 128],
                         r=xnr[:, c * 512:(c + 1) * 512], a=(c == 0):
                         e.matmul(o, l, r, start=a, stop=False))
                pj = minc(pl, "t", mm, lambda e, o=psA[:, :],
                          l=wtile[:, slot * 2048 + 15 * 128:slot * 2048 + 16 * 128],
                          r=xnr[:, 15 * 512:16 * 512]: e.matmul(o, l, r, start=False, stop=True))
                tile_free[slot] = pj
                wait(pl, "s", mm, pj)
                vev = minc(pl, "s", ss, lambda e, o=ktmp[:, :], i=psA[:, :]:
                           e.activation(o, i, AF.Identity))
                wait(pl, "t", ss, vev)
                for st_i in range(3):
                    pl.t(lambda e, o=psT[:, st_i * 128:(st_i + 1) * 128],
                         i=ktmp[:, st_i * 128:(st_i + 1) * 128], d=sb["ident_b"][:, :]:
                         e.transpose(o, i, d))
                trt = minc(pl, "t", mm, lambda e, o=psT[:, 3 * 128:4 * 128],
                           i=ktmp[:, 3 * 128:4 * 128], d=sb["ident_b"][:, :]:
                           e.transpose(o, i, d))
                wait(pl, "v", mm, trt)
                ve = minc(pl, "v", vv, lambda e,
                          o=AP(vlocal, mc * 128, [[4 * 2048, 128], [2048, 4], [1, 128]]),
                          i=AP(psT, 0, [[512, 128], [128, 4], [1, 128]]):
                          e.tensor_copy(o, i))
                wait(pl, "t", vv, ve)
                wait(pl, "s", vv, ve)
                if mc + 1 < NCH:
                    wt = nwt
            wait(pl, "g", vv, ve)
            gdma_shared(ctx, pl, 1, dt["v_sh"], f"rr{tc}", 4 + tc,
                        [[D, 128], [128 * D, 4], [1, 2048]],
                        AP(vlocal, 0, [[4 * 2048, 128], [2048, 4], [1, 2048]]))
            po_prev = po.n
        publish_flag(ctx, pl, 1)
        poll_flag(ctx, pl, 1)
        pl.run(nc)


def plan_l1a_main(ctx):
    nc, sb, dt, S = ctx.nc, ctx.sb, ctx.dt, ctx.sem
    mm, vv, ss, gg = S["mm"], S["vv"], S["ss"], S["gg"]
    QC = 256
    NQC = TL // QC
    uid = ctx.uid = getattr(ctx, "uid", 0) + 1
    with contextlib.ExitStack() as stack:
        def T(name, shape, dtype):
            return stack.enter_context(nc.sbuf_tensor(f"{name}_{uid}", shape, dtype))

        def P(name, shape, dtype=F32):
            return stack.enter_context(nc.psum_tensor(f"{name}_{uid}", shape, dtype))

        pl = Plan()
        xtmp = T("l1m_xt", [128, QC], F32)
        wtile = T("l1m_w", [128, 2 * 2048], BF16)
        kblk = T("l1m_kb", [128, 2 * 16 * 256], BF16)
        vblk = T("l1m_vb", [128, 2 * 2 * 2048], BF16)
        kblk_addr = nc.lookup_mloc(kblk).addr
        xnr = nc.alloc_sbuf_tensor_at(f"l1m_xnr_{uid}", [128, NCH * QC], BF16,
                                      offset=kblk_addr)
        qrot_all = T("l1m_qr", [128, NHEAD * QC], BF16)
        qtmp = T("l1m_qt", [128, QC], BF16)
        rt_q = T("l1m_rtq", [128, QC], BF16)
        cs_o = T("l1m_cs", [128, QC], F32)
        sn_o = T("l1m_sn", [128, QC], F32)
        sq1 = T("l1m_sq1", [128, 2 * QC], BF16)
        sqm = T("l1m_sqm", [128, 2 * QC], BF16)
        pb = T("l1m_pb", [128, 2 * QC], BF16)
        y_acc = T("l1m_ya", [128, NHEAD * QC], F32)
        s_acc = T("l1m_sa", [128, NHEAD * QC], F32)
        s_acc_addr = nc.lookup_mloc(s_acc).addr
        ynb = nc.alloc_sbuf_tensor_at(f"l1m_yn_{uid}", [128, NHEAD * QC], BF16,
                                      offset=s_acc_addr)
        rcp = T("l1m_rcp", [128, QC], F32)
        ysb = T("l1m_ysb", [128, QC], F32)
        sub_t = T("l1m_sub", [128, 2 * QC], F32)
        sub_b = T("l1m_subb", [128, QC], BF16)
        sub_2 = T("l1m_sub2", [128, QC], BF16)
        psS = [P(f"l1m_psS{i}", [128, QC]) for i in range(2)]
        psY = P("l1m_psY", [128, QC])
        psSum = P("l1m_psSum", [128, QC])
        psA = P("l1m_psA", [128, QC])
        psB = P("l1m_psB", [128, QC])
        psO1 = P("l1m_psO1", [128, QC])
        psO2 = P("l1m_psO2", [128, QC])

        tile_free = {0: 0, 1: 0}
        kv_free = {0: 0, 1: 0}
        sp_free = {}
        prev_qr = None
        prev_stt = None
        for qc in range(NQC):
            KT = 8 * (qc // 2) + 8
            NSB = KT // 2
            xt = plan_xnr(ctx, pl, xnr, xtmp, qc,
                          sb["ln_r"][:, qc * QC:(qc + 1) * QC],
                          sb["ln_a"][:, qc * QC:(qc + 1) * QC], width=QC)
            if prev_qr is not None:
                wait(pl, "y", vv, prev_qr)
            dma(pl, "y", cs_o[:, :], dt["cos1o"][:, qc * QC:(qc + 1) * QC], S["d_cs"])
            cs_t = dma(pl, "y", sn_o[:, :], dt["sin1o"][:, qc * QC:(qc + 1) * QC], S["d_cs"])
            wait(pl, "g", mm, tile_free[0])
            wt = load_l1w(ctx, pl, "L1_attn_WqT", 0, wtile, 0)
            for h in range(NHEAD):
                slot = h % 2
                if h + 1 < NHEAD:
                    ns = (h + 1) % 2
                    wait(pl, "g", mm, tile_free[ns])
                    nwt = load_l1w(ctx, pl, "L1_attn_WqT", h + 1, wtile, ns)
                wait(pl, "t", S[f"d_wt{slot}"], wt)
                if h == 0:
                    wait(pl, "t", vv, xt)
                for c in range(NCH - 1):
                    pl.t(lambda e, o=psA[:, :],
                         l=wtile[:, slot * 2048 + c * 128:slot * 2048 + (c + 1) * 128],
                         r=xnr[:, c * QC:(c + 1) * QC], a=(c == 0):
                         e.matmul(o, l, r, start=a, stop=False))
                pj = minc(pl, "t", mm, lambda e, o=psA[:, :],
                          l=wtile[:, slot * 2048 + 15 * 128:slot * 2048 + 16 * 128],
                          r=xnr[:, 15 * QC:16 * QC]: e.matmul(o, l, r, start=False, stop=True))
                tile_free[slot] = pj
                wait(pl, "s", mm, pj)
                if prev_qr is not None:
                    wait(pl, "s", vv, prev_qr)
                qe = minc(pl, "s", ss, lambda e, o=qtmp[:, :], i=psA[:, :]:
                          e.activation(o, i, AF.Identity))
                wait(pl, "t", ss, qe)
                qpm = minc(pl, "t", mm, lambda e, o=psB[:, :], l=sb["perm1_b"][:, :],
                           r=qtmp[:, :]: e.matmul(o, l, r, start=True, stop=True))
                wait(pl, "v", mm, qpm)
                if h == 0:
                    wait(pl, "v", S["d_cs"], cs_t)
                pl.v(lambda e, o=rt_q[:, :], i=psB[:, :], j=sn_o[:, :]: e.tensor_mul(o, i, j))
                pl.v(lambda e, o=qrot_all[:, h * QC:(h + 1) * QC], i=qtmp[:, :], j=cs_o[:, :]:
                     e.tensor_mul(o, i, j))
                qr = minc(pl, "v", vv, lambda e, o=qrot_all[:, h * QC:(h + 1) * QC],
                          i=qrot_all[:, h * QC:(h + 1) * QC], j=rt_q[:, :]:
                          e.tensor_add(o, i, j))
                wait(pl, "t", vv, qr)
                prev_qr = qr
                if h + 1 < NHEAD:
                    wt = nwt
            qproj_done = tile_free[1]
            for sbk in range(NSB):
                k0 = qc // 2 * 0 + sbk * 256
                slot = sbk % 2
                wait(pl, "y", vv, kv_free[slot])
                if sbk < 2:
                    wait(pl, "y", mm, qproj_done)
                k_t = dma(pl, "y", AP(kblk, slot * (16 * 256),
                                      [[2 * 16 * 256, 128], [256, 16], [1, 256]]),
                          AP(dt["k_sh"], k0, [[TG, 128], [128 * TG, 16], [1, 256]]),
                          S[f"d_kv{slot}"])
                v_t = dma(pl, "y", AP(vblk, slot * (2 * 2048),
                                      [[2 * 2 * 2048, 128], [2048, 2], [1, 2048]]),
                          AP(dt["v_sh"], k0 * D, [[D, 128], [128 * D, 2], [1, 2048]]),
                          S[f"d_v{slot}"])
                wait(pl, "t", S[f"d_kv{slot}"], k_t)
                wait(pl, "t", S[f"d_v{slot}"], v_t)
                pa = None
                for h in range(NHEAD):
                    qk = None
                    for u in range(2):
                        fn = (lambda e, o=psS[u][:, :],
                              l=AP(kblk, slot * (16 * 256) + h * 256 + u * 128,
                                   [[2 * 16 * 256, 128], [1, 128]]),
                              r=qrot_all[:, h * QC:(h + 1) * QC]:
                              e.matmul(o, l, r, start=True, stop=True))
                        if u == 1:
                            qk = minc(pl, "t", mm, fn)
                        else:
                            pl.t(fn)
                    wait(pl, "s", mm, qk)
                    sqt = None
                    for u in range(2):
                        fn = (lambda e, o=sq1[:, u * QC:(u + 1) * QC], i=psS[u][:, :]:
                              e.activation(o, i, AF.Square))
                        if u == 1:
                            sqt = minc(pl, "s", ss, fn)
                        else:
                            pl.s(fn)
                    wait(pl, "v", ss, sqt)
                    pt = None
                    for u in range(2):
                        kt = sbk * 2 + u
                        w = kt - 8 * (qc // 2)
                        h2 = qc % 2
                        if w >= 0:
                            pl.v(lambda e, o=sqm[:, u * QC:(u + 1) * QC],
                                 i=sq1[:, u * QC:(u + 1) * QC],
                                 j=sb["masks_b"][:, w * 512 + h2 * 256:w * 512 + h2 * 256 + 256]:
                                 e.tensor_mul(o, i, j))
                            src2 = sqm
                        else:
                            src2 = sq1
                        pt = minc(pl, "v", vv, lambda e, o=pb[:, u * QC:(u + 1) * QC],
                                  i=sq1[:, u * QC:(u + 1) * QC],
                                  j=src2[:, u * QC:(u + 1) * QC]: e.tensor_mul(o, i, j))
                    wait(pl, "t", vv, pt)
                    av = None
                    for u in range(2):
                        pl.t(lambda e, o=psY[:, :],
                             l=AP(vblk, slot * (2 * 2048) + u * 2048 + h * 128,
                                  [[2 * 2 * 2048, 128], [1, 128]]),
                             r=pb[:, u * QC:(u + 1) * QC], a=(u == 0), b=(u == 1):
                             e.matmul(o, l, r, start=a, stop=b))
                        fn = (lambda e, o=psSum[:, :], l=sb["ones_b"][:, :],
                              r=pb[:, u * QC:(u + 1) * QC], a=(u == 0), b=(u == 1):
                              e.matmul(o, l, r, start=a, stop=b))
                        if u == 1:
                            av = minc(pl, "t", mm, fn)
                        else:
                            pl.t(fn)
                    wait(pl, "v", mm, av)
                    if sbk == 0:
                        pl.v(lambda e, o=y_acc[:, h * QC:(h + 1) * QC], i=psY[:, :]:
                             e.tensor_copy(o, i))
                        pa = minc(pl, "v", vv,
                                  lambda e, o=s_acc[:, h * QC:(h + 1) * QC], i=psSum[:, :]:
                                  e.tensor_copy(o, i))
                    else:
                        pl.v(lambda e, o=y_acc[:, h * QC:(h + 1) * QC],
                             i=y_acc[:, h * QC:(h + 1) * QC], j=psY[:, :]: e.tensor_add(o, i, j))
                        pa = minc(pl, "v", vv,
                                  lambda e, o=s_acc[:, h * QC:(h + 1) * QC],
                                  i=s_acc[:, h * QC:(h + 1) * QC], j=psSum[:, :]:
                                  e.tensor_add(o, i, j))
                    wait(pl, "t", vv, pa)
                    wait(pl, "s", vv, pa)
                kv_free[slot] = pa
            yv = None
            for h in range(NHEAD):
                pl.v(lambda e, o=rcp[:, :], i=s_acc[:, h * QC:(h + 1) * QC]:
                     e.tensor_scalar(o, i, 1e-6, None, ALU.max))
                pl.v(lambda e, o=ysb[:, :], i=rcp[:, :]: e.reciprocal(o, i))
                yv = minc(pl, "v", vv, lambda e, o=ynb[:, h * QC:(h + 1) * QC],
                          i=y_acc[:, h * QC:(h + 1) * QC], j=ysb[:, :]: e.tensor_mul(o, i, j))
            wait(pl, "t", vv, yv)
            wait(pl, "g", mm, tile_free[0])
            wp = load_l1w(ctx, pl, "L1_attn_WoT", 0, wtile, 0)
            for mc in range(NCH):
                slot = mc % 2
                if mc + 1 < NCH:
                    ns = (mc + 1) % 2
                    wait(pl, "g", mm, tile_free[ns])
                    nwp = load_l1w(ctx, pl, "L1_attn_WoT", mc + 1, wtile, ns)
                wait(pl, "t", S[f"d_wt{slot}"], wp)
                for hc in range(NHEAD - 1):
                    pl.t(lambda e, o=psA[:, :],
                         l=wtile[:, slot * 2048 + hc * 128:slot * 2048 + (hc + 1) * 128],
                         r=ynb[:, hc * QC:(hc + 1) * QC], a=(hc == 0):
                         e.matmul(o, l, r, start=a, stop=False))
                oj = minc(pl, "t", mm, lambda e, o=psA[:, :],
                          l=wtile[:, slot * 2048 + 15 * 128:slot * 2048 + 16 * 128],
                          r=ynb[:, 15 * QC:16 * QC]: e.matmul(o, l, r, start=False, stop=True))
                tile_free[slot] = oj
                wait(pl, "s", mm, oj)
                if slot in sp_free:
                    wait(pl, "s", S[f"d_sp{slot}"], sp_free[slot])
                se = minc(pl, "s", ss, lambda e, o=sub_t[:, slot * QC:(slot + 1) * QC],
                          i=psA[:, :]: e.activation(o, i, AF.Identity))
                wait(pl, "v", ss, se)
                if prev_stt is not None:
                    wait(pl, "v", mm, prev_stt)
                sbt = minc(pl, "v", vv, lambda e, o=sub_b[:, :],
                           i=sub_t[:, slot * QC:(slot + 1) * QC]: e.tensor_copy(o, i))
                if prev_stt is not None:
                    wait(pl, "s", mm, prev_stt)
                s2t = minc(pl, "s", ss, lambda e, o=sub_2[:, :],
                           i=sub_t[:, slot * QC:(slot + 1) * QC]: e.activation(o, i, AF.Square))
                wait(pl, "t", vv, sbt)
                wait(pl, "t", ss, s2t)
                pl.t(lambda e, o=psO1[:, :], l=sb["ones_b"][:, :], r=sub_b[:, :],
                     a=(mc == 0), b=(mc == NCH - 1): e.matmul(o, l, r, start=a, stop=b))
                stt = minc(pl, "t", mm, lambda e, o=psO2[:, :], l=sb["ones_b"][:, :],
                           r=sub_2[:, :], a=(mc == 0), b=(mc == NCH - 1):
                           e.matmul(o, l, r, start=a, stop=b))
                prev_stt = stt
                wait(pl, "y", ss, se)
                spw = dma(pl, "y", AP(dt["sub_spill"], mc * 128 * TL + qc * QC,
                                      [[TL, 128], [1, QC]]),
                          sub_t[:, slot * QC:(slot + 1) * QC], S[f"d_sp{slot}"])
                sp_free[slot] = spw
                if mc + 1 < NCH:
                    wp = nwp
            wait(pl, "v", mm, prev_stt)
            pl.v(lambda e, o=sb["ln_o1"][:, qc * QC:(qc + 1) * QC], i=psO1[:, :]:
                 e.tensor_copy(o, i))
            ot = minc(pl, "v", vv, lambda e, o=sb["ln_o2"][:, qc * QC:(qc + 1) * QC],
                      i=psO2[:, :]: e.tensor_copy(o, i))
            wait(pl, "t", vv, ot)
        pl.run(nc)



DFF = 8192
N_W = float(DFF * D)


def phase_l1_ffn(ctx, is_last):
    nc, sb = ctx.nc, ctx.sb
    uidF = ctx.uid = getattr(ctx, "uid", 0) + 1
    with contextlib.ExitStack() as fstack:
        sb["ln_o1"] = fstack.enter_context(nc.sbuf_tensor(f"ln_o1_{uidF}", [128, TL], F32))
        sb["ln_o2"] = fstack.enter_context(nc.sbuf_tensor(f"ln_o2_{uidF}", [128, TL], F32))
        plan_l1f_prep(ctx)
        plan_l1f_pass1(ctx)
        plan_l1f_pass2(ctx)
        phase_apply(ctx, first=False, last=is_last, need_stats=not is_last)
    del sb["ln_o1"], sb["ln_o2"]


def _l1f_pieces(wi):
    out = []
    if wi == 0:
        for pi in range(64):
            out.append((pi * 128 * D, "w1T_scratch", pi * 128, D))
    else:
        for pi in range(64):
            ri, pc = divmod(pi, 4)
            out.append((ri * 128 * DFF + pc * 2048, "w2T_scratch",
                        pc * 2048 * D + ri * 128, DFF))
    return out


def plan_l1f_prep(ctx):
    nc, sb, dt, S = ctx.nc, ctx.sb, ctx.dt, ctx.sem
    mm, vv, ss = S["mm"], S["vv"], S["ss"]
    uid = ctx.uid = getattr(ctx, "uid", 0) + 1
    with contextlib.ExitStack() as stack:
        def T(name, shape, dtype):
            return stack.enter_context(nc.sbuf_tensor(f"{name}_{uid}", shape, dtype))

        def P(name, shape, dtype=F32):
            return stack.enter_context(nc.psum_tensor(f"{name}_{uid}", shape, dtype))

        pl = Plan()
        wbuf = T("fp_wb", [128, 2 * 2048], BF16)
        absb = T("fp_ab", [128, 2 * 2048], BF16)
        wf = T("fp_wf", [128, 2 * 2048], F32)
        geb = T("fp_ge", [128, 2048], BF16)
        leb = T("fp_le", [128, 2048], BF16)
        gei = T("fp_gei", [128, 2048], BF16)
        lei = T("fp_lei", [128, 2048], BF16)
        ternb = T("fp_tn", [128, 2 * 2048], BF16)
        tb = T("fp_tb", [128, 2 * 2048], BF16)
        srow = T("fp_sr", [128, 4], F32)
        thr = T("fp_th", [128, 4], F32)
        eps_col = T("fp_eps", [128, 4], F32)
        psS = P("fp_psS", [128, 512])
        psT = P("fp_psT", [128, 512], BF16)

        gg = S["gg"]
        eps_m = minc(pl, "g", gg, lambda e, o=eps_col[:, :]: e.memset(o, 1e-8))
        abs_free = {}
        sum_free = {}
        red_prev = None
        for wi, nm in [(0, "L1_ffn_w1"), (1, "L1_ffn_w2")]:
            pieces = _l1f_pieces(wi)
            for pi, (soff, _, _, srcw) in enumerate(pieces):
                slot = pi % 2
                if slot in abs_free:
                    wait(pl, "g", ss, abs_free[slot])
                ld = dma(pl, "g", wbuf[:, slot * 2048:(slot + 1) * 2048],
                         AP(dt[nm], soff, [[srcw, 128], [1, 2048]]), S[f"d_we{slot}"])
                wait(pl, "s", S[f"d_we{slot}"], ld)
                if slot in sum_free:
                    wait(pl, "s", mm, sum_free[slot])
                ab = minc(pl, "s", ss, lambda e, o=absb[:, slot * 2048:(slot + 1) * 2048],
                          i=wbuf[:, slot * 2048:(slot + 1) * 2048]: e.activation(o, i, AF.Abs))
                abs_free[slot] = ab
                wait(pl, "t", ss, ab)
                if pi == 0 and red_prev is not None:
                    wait(pl, "t", vv, red_prev)
                for q in range(3):
                    pl.t(lambda e, o=psS[:, :], l=sb["ones_b"][:, :],
                         r=absb[:, slot * 2048 + q * 512:slot * 2048 + (q + 1) * 512],
                         a=(pi == 0 and q == 0):
                         e.matmul(o, l, r, start=a, stop=False))
                sm = minc(pl, "t", mm, lambda e, o=psS[:, :], l=sb["ones_b"][:, :],
                          r=absb[:, slot * 2048 + 3 * 512:slot * 2048 + 4 * 512],
                          b=(pi == len(pieces) - 1):
                          e.matmul(o, l, r, start=False, stop=b))
                sum_free[slot] = sm
            wait(pl, "v", mm, sm)
            red_prev = minc(pl, "v", vv, lambda e, o=srow[:, 0:1], i=psS[:, :]:
                            e.reduce_sum(o, i, axis=AX.X))
            wait(pl, "v", vv, red_prev)
            f1 = minc(pl, "v", vv, lambda e, o=srow[:, 1:2], i=srow[:, 0:1]:
                      e.tensor_scalar(o, i, 1.0 / N_W, None, ALU.mult))
            wait(pl, "v", vv, f1)
            if wi == 0:
                wait(pl, "v", gg, eps_m)
            f2 = minc(pl, "v", vv, lambda e, o=srow[:, 2:3], i=srow[:, 1:2],
                      j=eps_col[:, 0:1]: e.tensor_max(o, i, j))
            wait(pl, "v", vv, f2)
            pl.v(lambda e, o=thr[:, 2 * wi:2 * wi + 1], i=srow[:, 2:3]:
                 e.tensor_scalar(o, i, 0.5, None, ALU.mult))
            thr_m = minc(pl, "v", vv, lambda e, o=thr[:, 2 * wi + 1:2 * wi + 2],
                         i=srow[:, 2:3]: e.tensor_scalar(o, i, -0.5, None, ALU.mult))
            wait(pl, "v", vv, thr_m)

        wait(pl, "s", vv, thr_m)
        wf_free = {}
        tern_free = {}
        out_free = {}
        ev_free = {}
        tn_prev = None
        for wi, nm in [(0, "L1_ffn_w1"), (1, "L1_ffn_w2")]:
            for pi, (soff, dstn, doff, srcw) in enumerate(_l1f_pieces(wi)):
                slot = pi % 2
                if slot in wf_free:
                    wait(pl, "g", ss, wf_free[slot])
                ld = dma(pl, "g", wf[:, slot * 2048:(slot + 1) * 2048],
                         AP(dt[nm], soff, [[srcw, 128], [1, 2048]]), S[f"d_wt{slot}"])
                wait(pl, "s", S[f"d_wt{slot}"], ld)
                if tn_prev is not None:
                    wait(pl, "s", vv, tn_prev)
                pl.s(lambda e, o=gei[:, :], i=wf[:, slot * 2048:(slot + 1) * 2048],
                     b=thr[:, 2 * wi + 1:2 * wi + 2]:
                     e.activation(o, i, AF.Identity, bias=b))
                ac = minc(pl, "s", ss, lambda e, o=lei[:, :],
                          i=wf[:, slot * 2048:(slot + 1) * 2048],
                          b=thr[:, 2 * wi:2 * wi + 1]:
                          e.activation(o, i, AF.Identity, bias=b))
                wf_free[slot] = ac
                wait(pl, "v", ss, ac)
                if slot in tern_free:
                    wait(pl, "v", mm, tern_free[slot])
                pl.v(lambda e, o=geb[:, :], i=gei[:, :]:
                     e.tensor_scalar(o, i, 0.0, None, ALU.is_ge))
                pl.v(lambda e, o=leb[:, :], i=lei[:, :]:
                     e.tensor_scalar(o, i, 0.0, None, ALU.is_le))
                tn = minc(pl, "v", vv, lambda e, o=ternb[:, slot * 2048:(slot + 1) * 2048],
                          i=geb[:, :], j=leb[:, :]: e.tensor_sub(o, i, j))
                tn_prev = tn
                wait(pl, "t", vv, tn)
                if slot in out_free:
                    wait(pl, "s", S[f"d_wb{slot}"], out_free[slot])
                for grp in range(4):
                    if ev_free.get("last") is not None:
                        wait(pl, "t", ss, ev_free["last"])
                    for q in range(3):
                        c = grp * 4 + q
                        pl.t(lambda e, o=psT[:, q * 128:(q + 1) * 128],
                             i=ternb[:, slot * 2048 + c * 128:slot * 2048 + (c + 1) * 128],
                             d=sb["ident_b"][:, :]: e.transpose(o, i, d))
                    tr = minc(pl, "t", mm, lambda e, o=psT[:, 3 * 128:4 * 128],
                              i=ternb[:, slot * 2048 + (grp * 4 + 3) * 128:
                                      slot * 2048 + (grp * 4 + 4) * 128],
                              d=sb["ident_b"][:, :]: e.transpose(o, i, d))
                    wait(pl, "s", mm, tr)
                    ev = minc(pl, "s", ss, lambda e,
                              o=tb[:, slot * 2048 + grp * 512:slot * 2048 + (grp + 1) * 512],
                              i=psT[:, :]: e.activation(o, i, AF.Identity))
                    ev_free["last"] = ev
                tern_free[slot] = tr
                wait(pl, "y", ss, ev)
                ow = dma(pl, "y", AP(dt[dstn], doff, [[DFF if wi == 0 else D, 128],
                                                      [128 * (DFF if wi == 0 else D), 16],
                                                      [1, 128]]),
                         AP(tb, slot * 2048, [[2 * 2048, 128], [128, 16], [1, 128]]),
                         S[f"d_wb{slot}"])
                out_free[slot] = ow
        if "l1f_thr" in DBG_DRAM_OUT:
            dbg = nc.dram_tensor("dbg_thr", [128, 8], F32, kind="ExternalOutput")
            wait(pl, "g", vv, thr_m)
            dg1 = dma(pl, "g", AP(dbg, 0, [[8, 128], [1, 4]]), thr[:, :], S["d_g"])
            dg2 = dma(pl, "g", AP(dbg, 4, [[8, 128], [1, 4]]), srow[:, :], S["d_g"])
            wait(pl, "g", S["d_g"], dg2)
        wait(pl, "y", S["d_wb0"], S["d_wb0"].n)
        wait(pl, "y", S["d_wb1"], S["d_wb1"].n)
        pl.run(nc)


def plan_l1f_pass1(ctx):
    nc, sb, dt, S = ctx.nc, ctx.sb, ctx.dt, ctx.sem
    mm, vv, ss = S["mm"], S["vv"], S["ss"]
    uid = ctx.uid = getattr(ctx, "uid", 0) + 1
    with contextlib.ExitStack() as stack:
        def T(name, shape, dtype):
            return stack.enter_context(nc.sbuf_tensor(f"{name}_{uid}", shape, dtype))

        def P(name, shape, dtype=F32):
            return stack.enter_context(nc.psum_tensor(f"{name}_{uid}", shape, dtype))

        pl = Plan()
        xnr = T("f1_xnr", [128, NCH * 512], BF16)
        xtmp = T("f1_xt", [128, 512], F32)
        wtile = T("f1_w", [128, 2 * 2048], BF16)
        hg = T("f1_hg", [128, 2 * 512], BF16)
        b1c = T("f1_b1", [128, 64], F32)
        psA = [P(f"f1_psA{i}", [128, 512]) for i in range(2)]

        b_t = dma(pl, "g", b1c[:, :], dt["b1_c"][:, :], S["d_bias"])
        tile_free = {}
        ge_free = {}
        hw_free = {}
        last_pj = None
        for tc in range(NTC):
            if last_pj is not None:
                wait(pl, "v", mm, last_pj)
            xt = plan_xnr(ctx, pl, xnr, xtmp, tc,
                          sb["ln_r"][:, tc * 512:(tc + 1) * 512],
                          sb["ln_a"][:, tc * 512:(tc + 1) * 512])
            for fi in range(DFF // 128):
                slot = fi % 2
                if slot in tile_free:
                    wait(pl, "g", mm, tile_free[slot])
                wt = dma(pl, "g", AP(wtile, slot * 2048, [[2 * 2048, 128], [128, 16], [1, 128]]),
                         AP(dt["w1T_scratch"], fi * 128, [[DFF, 128], [128 * DFF, 16], [1, 128]]),
                         S[f"d_we{slot}"])
                wait(pl, "t", S[f"d_we{slot}"], wt)
                if fi == 0:
                    wait(pl, "t", vv, xt)
                if slot in ge_free:
                    wait(pl, "t", ss, ge_free[slot])
                for c in range(NCH - 1):
                    pl.t(lambda e, o=psA[slot][:, :],
                         l=wtile[:, slot * 2048 + c * 128:slot * 2048 + (c + 1) * 128],
                         r=xnr[:, c * 512:(c + 1) * 512], a=(c == 0):
                         e.matmul(o, l, r, start=a, stop=False))
                pj = minc(pl, "t", mm, lambda e, o=psA[slot][:, :],
                          l=wtile[:, slot * 2048 + 15 * 128:slot * 2048 + 16 * 128],
                          r=xnr[:, 15 * 512:16 * 512]: e.matmul(o, l, r, start=False, stop=True))
                tile_free[slot] = pj
                last_pj = pj
                wait(pl, "s", mm, pj)
                if tc == 0 and fi == 0:
                    wait(pl, "s", S["d_bias"], b_t)
                if slot in hw_free:
                    wait(pl, "s", S[f"d_sp{slot}"], hw_free[slot])
                ge = minc(pl, "s", ss, lambda e, o=hg[:, slot * 512:(slot + 1) * 512],
                          i=psA[slot][:, :], b=b1c[:, fi:fi + 1]:
                          e.activation(o, i, AF.Gelu, bias=b))
                ge_free[slot] = ge
                wait(pl, "y", ss, ge)
                hw = dma(pl, "y", AP(dt["h_dram"], fi * 128 * TL + tc * 512,
                                     [[TL, 128], [1, 512]]),
                         hg[:, slot * 512:(slot + 1) * 512], S[f"d_sp{slot}"])
                hw_free[slot] = hw
        wait(pl, "y", S["d_sp0"], S["d_sp0"].n)
        wait(pl, "y", S["d_sp1"], S["d_sp1"].n)
        pl.run(nc)


def plan_l1f_pass2(ctx):
    nc, sb, dt, S = ctx.nc, ctx.sb, ctx.dt, ctx.sem
    mm, vv, ss = S["mm"], S["vv"], S["ss"]
    uid = ctx.uid = getattr(ctx, "uid", 0) + 1
    with contextlib.ExitStack() as stack:
        def T(name, shape, dtype):
            return stack.enter_context(nc.sbuf_tensor(f"{name}_{uid}", shape, dtype))

        def P(name, shape, dtype=F32):
            return stack.enter_context(nc.psum_tensor(f"{name}_{uid}", shape, dtype))

        pl = Plan()
        xsb_addr = nc.lookup_mloc(sb["xsb"]).addr
        hblk = nc.alloc_sbuf_tensor_at(f"f2_h_{uid}", [128, 64 * 512], BF16,
                                       offset=xsb_addr)
        wt2 = T("f2_w", [128, 2 * 64 * 128], BF16)
        sub_t = T("f2_sub", [128, 2 * 512], F32)
        sub_b = T("f2_subb", [128, 512], BF16)
        sub_2 = T("f2_sub2", [128, 512], BF16)
        b2c = T("f2_b2", [128, 16], F32)
        psA = [P(f"f2_psA{i}", [128, 512]) for i in range(2)]
        psO1 = P("f2_psO1", [128, 512])
        psO2 = P("f2_psO2", [128, 512])

        b_t = dma(pl, "g", b2c[:, :], dt["b2_c"][:, :], S["d_bias"])
        tile_free = {}
        se_free = {}
        sp_free = {}
        prev_stt = None
        oj_last = None
        for tc in range(NTC):
            if oj_last is not None:
                wait(pl, "y", mm, oj_last)
            hld = dma(pl, "y", hblk[:, :],
                      AP(dt["h_dram"], tc * 512, [[TL, 128], [128 * TL, 64], [1, 512]]),
                      S["d_kf"])
            for mc in range(NCH):
                slot = mc % 2
                if slot in tile_free:
                    wait(pl, "g", mm, tile_free[slot])
                wt = dma(pl, "g", AP(wt2, slot * (64 * 128),
                                     [[2 * 64 * 128, 128], [128, 64], [1, 128]]),
                         AP(dt["w2T_scratch"], mc * 128, [[D, 128], [128 * D, 64], [1, 128]]),
                         S[f"d_we{slot}"])
                wait(pl, "t", S[f"d_we{slot}"], wt)
                if mc == 0:
                    wait(pl, "t", S["d_kf"], hld)
                if slot in se_free:
                    wait(pl, "t", ss, se_free[slot])
                for fc in range(63):
                    pl.t(lambda e, o=psA[slot][:, :],
                         l=wt2[:, slot * (64 * 128) + fc * 128:slot * (64 * 128) + (fc + 1) * 128],
                         r=hblk[:, fc * 512:(fc + 1) * 512], a=(fc == 0):
                         e.matmul(o, l, r, start=a, stop=False))
                oj = minc(pl, "t", mm, lambda e, o=psA[slot][:, :],
                          l=wt2[:, slot * (64 * 128) + 63 * 128:slot * (64 * 128) + 64 * 128],
                          r=hblk[:, 63 * 512:64 * 512]: e.matmul(o, l, r, start=False, stop=True))
                tile_free[slot] = oj
                oj_last = oj
                wait(pl, "s", mm, oj)
                if tc == 0 and mc == 0:
                    wait(pl, "s", S["d_bias"], b_t)
                if slot in sp_free:
                    wait(pl, "s", S[f"d_sp{slot}"], sp_free[slot])
                se = minc(pl, "s", ss, lambda e, o=sub_t[:, slot * 512:(slot + 1) * 512],
                          i=psA[slot][:, :], b=b2c[:, mc:mc + 1]:
                          e.activation(o, i, AF.Identity, bias=b))
                se_free[slot] = se
                wait(pl, "v", ss, se)
                if prev_stt is not None:
                    wait(pl, "v", mm, prev_stt)
                sbt = minc(pl, "v", vv, lambda e, o=sub_b[:, :],
                           i=sub_t[:, slot * 512:(slot + 1) * 512]: e.tensor_copy(o, i))
                if prev_stt is not None:
                    wait(pl, "s", mm, prev_stt)
                s2t = minc(pl, "s", ss, lambda e, o=sub_2[:, :],
                           i=sub_t[:, slot * 512:(slot + 1) * 512]: e.activation(o, i, AF.Square))
                wait(pl, "t", vv, sbt)
                wait(pl, "t", ss, s2t)
                pl.t(lambda e, o=psO1[:, :], l=sb["ones_b"][:, :], r=sub_b[:, :],
                     a=(mc == 0), b=(mc == NCH - 1): e.matmul(o, l, r, start=a, stop=b))
                stt = minc(pl, "t", mm, lambda e, o=psO2[:, :], l=sb["ones_b"][:, :],
                           r=sub_2[:, :], a=(mc == 0), b=(mc == NCH - 1):
                           e.matmul(o, l, r, start=a, stop=b))
                prev_stt = stt
                wait(pl, "y", ss, se)
                spw = dma(pl, "y", AP(dt["sub_spill"], mc * 128 * TL + tc * 512,
                                      [[TL, 128], [1, 512]]),
                          sub_t[:, slot * 512:(slot + 1) * 512], S[f"d_sp{slot}"])
                sp_free[slot] = spw
            wait(pl, "v", mm, prev_stt)
            pl.v(lambda e, o=sb["ln_o1"][:, tc * 512:(tc + 1) * 512], i=psO1[:, :]:
                 e.tensor_copy(o, i))
            ot = minc(pl, "v", vv, lambda e, o=sb["ln_o2"][:, tc * 512:(tc + 1) * 512],
                      i=psO2[:, :]: e.tensor_copy(o, i))
            wait(pl, "t", vv, ot)
        pl.run(nc)



PHASES = {
    "l0a": phase_l0_attn,
    "l0f": phase_l0_ffn,
    "l1a": phase_l1_attn,
    "l1f": phase_l1_ffn,
}


def phase_debug_dump(ctx, items):
    pl = Plan()
    S = ctx.sem
    dx = S["d_x"]
    for name, hnd, shp in items:
        t = ctx.nc.dram_tensor("dbg_" + name, shp, F32, kind="ExternalOutput")
        tb = ctx.nc.dram_tensor("dbgi_" + name, shp,
                                hnd.dtype if hasattr(hnd, "dtype") else F32)
        dma(pl, "g", t.ap(), hnd[:shp[0], :shp[1]], dx)
    wait(pl, "g", dx, dx.n)
    pl.run(ctx.nc)


def build_program(sub_blocks):
    ctx = Ctx()
    nc = bass.Bass("TRN2", target_bir_lowering=False, num_devices=8)
    ctx.nc = nc
    ctx.sub_blocks = sub_blocks
    declare_dram(ctx)
    with contextlib.ExitStack() as stack:
        alloc_persistent(ctx, stack)
        phase_init(ctx)
        phase_apply(ctx, first=True, last=False, need_stats=True)
        for i, sbk in enumerate(sub_blocks):
            PHASES[sbk](ctx, is_last=(i == len(sub_blocks) - 1))
    return ctx


def kernel(**inputs):
    in_maps = prepare_inputs(inputs)
    launches = LAUNCHES if LAUNCHES is not None else [list(SUBBLOCKS)]
    cur = in_maps
    res = None
    for li, grp in enumerate(launches):
        ctx = build_program(grp)
        declared = set(ctx.dt.keys())
        maps = [{k: v for k, v in m.items() if k in declared} for m in cur]
        res = run_bass_kernel_spmd(ctx.nc, maps, core_ids=list(range(8)))
        if li + 1 < len(launches):
            cur = [dict(m, xT=np.ascontiguousarray(res.results[c]["outT"]))
                   for c, m in enumerate(in_maps)]
    return assemble_output(res.results)
